# version 14 (final)
# baseline (speedup 1.0000x reference)
import os
import numpy as np
import ml_dtypes
import concourse.bass as bass
import concourse.mybir as mybir
from concourse.bass_utils import run_bass_kernel_spmd
from concourse.expressions_rust import RuntimeValue

F32 = mybir.dt.float32
BF16 = mybir.dt.bfloat16
F16 = mybir.dt.float16
U8 = mybir.dt.uint8
I32 = mybir.dt.int32
AF = mybir.ActivationFunctionType
ALU = mybir.AluOpType

C = 2048
CH = 16
TKV = 4096
TOUT = 2048
NSL = 8
NTK = 4
NRA = 8
II = 5632
ICn = 44
NJ = 44
NQ = 4
NG = 16
GU_BT = 8
GU_OB = 11
DN_BT = 22
DN_OB = 2
NTILES = DN_OB * DN_BT + GU_OB * GU_BT
MAGIC = 6053

DBG = []


def _perm(n):
    q = np.arange(n).reshape(-1, 256)
    return np.concatenate([q[:, 0::2], q[:, 1::2]], axis=1).reshape(-1)


class Prog:
    def __init__(self, nc):
        self.nc = nc
        self.q = {e: [] for e in ("sync", "tensor", "vector", "scalar", "gpsimd")}
        self.cnt = {}

    def on(self, eng, fn):
        self.q[eng].append(fn)

    def bump(self, sem, n=1):
        self.cnt[sem] = self.cnt.get(sem, 0) + n
        return self.cnt[sem]

    def val(self, sem):
        return self.cnt.get(sem, 0)

    def finalize(self, sems=()):
        if sems:
            nums = sorted(s.num for s in sems)
            rng = range(nums[0], nums[-1] + 1)
            assert len(nums) == len(rng), nums
            self.nc.gpsimd.dma_reset(rng)
            self.nc.gpsimd.sem_clear(rng)
            self.nc.all_engine_barrier()
        with self.nc.Block() as blk:
            @blk.sync
            def _(e):
                for f in self.q["sync"]:
                    f(e)

            @blk.tensor
            def _(e):
                for f in self.q["tensor"]:
                    f(e)

            @blk.vector
            def _(e):
                for f in self.q["vector"]:
                    f(e)

            @blk.scalar
            def _(e):
                for f in self.q["scalar"]:
                    f(e)

            @blk.gpsimd
            def _(e):
                for f in self.q["gpsimd"]:
                    f(e)


def build(stages=("0", "A", "X", "Q", "C", "D")):
    nc = bass.Bass("TRN2", target_bir_lowering=False, debug=False)

    x_kv = nc.dram_tensor("x_kv", [C, TKV], F32, kind="ExternalInput").ap()
    wkv_r = nc.dram_tensor("wkv_r", [16 * 128, 4096], F32, kind="ExternalInput").ap()
    wq_r = nc.dram_tensor("wq_r", [16 * 128, 2048], F32, kind="ExternalInput").ap()
    wp_r = nc.dram_tensor("wp_r", [16 * 128, 2048], F32, kind="ExternalInput").ap()
    bproj = nc.dram_tensor("bproj", [128, 16], F32, kind="ExternalInput").ap()
    gu_by = nc.dram_tensor("gu_by", [1024, 5632], U8, kind="ExternalInput").ap()
    gu_sct = nc.dram_tensor("gu_sct", [88 * 128, 512], F16, kind="ExternalInput").ap()
    dn_by = nc.dram_tensor("dn_by", [2816, 1024], U8, kind="ExternalInput").ap()
    dn_sct = nc.dram_tensor("dn_sct", [44 * 128, 512], F16, kind="ExternalInput").ap()
    consts = nc.dram_tensor("consts", [128, 8], F32, kind="ExternalInput").ap()
    maskbd = nc.dram_tensor("maskbd", [128, 128], BF16, kind="ExternalInput").ap()
    hm4 = nc.dram_tensor("hm4", [128, 4], BF16, kind="ExternalInput").ap()
    bmap = nc.dram_tensor("bmap", [64, NG * 128], BF16, kind="ExternalInput").ap()
    onesc = nc.dram_tensor("onesc", [128, 2], BF16, kind="ExternalInput").ap()
    ones1f = nc.dram_tensor("ones1f", [1, 128], F32, kind="ExternalInput").ap()
    slots = nc.dram_tensor("slots", [1, 5], I32, kind="ExternalInput").ap()

    out = nc.dram_tensor("out", [C, TOUT], F32, kind="ExternalOutput").ap()
    dbg_t = {}
    if "stats" in DBG:
        dbg_t["d_ab"] = nc.dram_tensor("d_ab", [2 * 128, TKV], F32, kind="ExternalOutput").ap()
    if "kv" in DBG:
        dbg_t["d_kv"] = nc.dram_tensor("d_kv", [2 * 128, 2064], F32, kind="ExternalOutput").ap()
    if "kvpre" in DBG:
        dbg_t["d_kvp"] = nc.dram_tensor("d_kvp", [128, 2064], F32, kind="ExternalOutput").ap()
    if "wt" in DBG:
        dbg_t["d_wtg"] = nc.dram_tensor("d_wtg", [88 * 16 * 128, 128], F32, kind="ExternalOutput").ap()
        dbg_t["d_wtd"] = nc.dram_tensor("d_wtd", [16 * ICn * 128, 128], F32, kind="ExternalOutput").ap()
    if "ffn" in DBG:
        dbg_t["d_ffn"] = nc.dram_tensor("d_ffn", [C, TOUT], F32, kind="ExternalOutput").ap()

    wt_gu = nc.dram_tensor("wt_gu", [88 * 16 * 128, 128], BF16, kind="Internal", addr_space="Shared")
    wt_dn = nc.dram_tensor("wt_dn", [16 * ICn * 128, 128], BF16, kind="Internal", addr_space="Shared")
    wl_gu = nc.dram_tensor("wl_gu", [44 * 16 * 128, 128], BF16, kind="Internal")
    wl_dn = nc.dram_tensor("wl_dn", [8 * ICn * 128, 128], BF16, kind="Internal")
    ffn_d = nc.dram_tensor("ffn_d", [C, TOUT], BF16, kind="Internal")
    kvx = nc.dram_tensor("kvx", [2 * 128, 2064], F32, kind="Internal", addr_space="Shared")
    flg = nc.dram_tensor("flg", [6, 128], I32, kind="Internal", addr_space="Shared")

    p = Prog(nc)
    sem_names = ["s_c", "s_x0", "s_x1", "s_e0", "s_e1", "s_sq", "s_acc", "s_pe",
                 "s_ev", "s_evk", "s_evv",
                 "s_pe2", "s_ab", "s_xn", "s_wkv0", "s_wkv1", "s_kvw", "s_poll", "s_xch",
                 "s_by0", "s_by1", "s_dqp", "s_dqp2", "s_dqa", "s_dqv",
                 "s_dqw0", "s_dqw1", "s_dqw2", "s_dqw3",
                 "s_dqc", "s_sc0", "s_sc1", "s_poll2", "s_poll3",
                 "s_gu0", "s_gu1", "s_dn0", "s_dn1", "s_ffw0", "s_ffw1",
                 "s_q0", "s_q1", "s_den", "s_at", "s_eo", "s_out0", "s_out1", "s_dbg"]
    import contextlib
    estack = contextlib.ExitStack()
    S = {n: estack.enter_context(nc.semaphore(n)) for n in sem_names}

    A = nc.alloc_sbuf_tensor

    cst = A("cst", [128, 8], F32)
    mk_s = A("mk_s", [128, 128], BF16)
    hm4_s = A("hm4_s", [128, 4], BF16)
    bmap_s = A("bmap_s", [64, NG * 128], BF16)
    ones_s = A("ones_s", [128, 2], BF16)
    ones1_s = A("ones1_s", [1, 128], F32)
    bp_s = A("bp_s", [128, 16], F32)
    slot_s = A("slot_s", [1, 5], I32)
    flag_s = A("flag_s", [1, 128], I32)

    xbuf = [A(f"xb{i}", [128, 512], F32) for i in range(2)]
    rowb = A("rowb", [128, 1536], F32)
    A_s = A("A_s", [128, TKV], BF16)
    B_s = A("B_s", [128, TKV], BF16)
    xn = [A(f"xn{i}", [128, 512], BF16) for i in range(CH)]
    xjit = A("xjit", [128, 512], F32)

    reg1 = A("reg1", [128, 16384], U8)

    def carve(reg, off, cols, dt):
        bl = {F32: 4, BF16: 2, F16: 2, U8: 1, I32: 4}[dt]
        return reg[:, off:off + cols * bl].bitcast(dt)

    xacc = carve(reg1, 0, TKV, BF16)
    xqacc = carve(reg1, 8192, TKV, BF16)
    wkv = [carve(reg1, 8192 * i, CH * 256, BF16) for i in range(2)]
    gurd = [carve(reg1, 4096 * i, 2048, BF16) for i in range(4)]
    exb = [carve(reg1, 2048 * i, 512, F32) for i in range(2)]
    fchunk = [carve(reg1, 4096 + 1024 * i, 512, BF16) for i in range(2)]
    osb = [carve(reg1, 6144 + 2048 * i, 512, F32) for i in range(2)]
    tsb = carve(reg1, 10240, 512, F32)
    t2sb = carve(reg1, 12288, 512, F32)
    t3sb = carve(reg1, 14336, 512, F32)

    reg2 = A("reg2", [128, 16384], U8)
    xsqt = [carve(reg2, 1024 * i, 512, BF16) for i in range(2)]
    kvK = [carve(reg2, 512 * i, 256, BF16) for i in range(NTK)]
    kvV = [carve(reg2, 2048 + 512 * i, 256, BF16) for i in range(4)]
    KVp = carve(reg2, 4096, 2048, F32)
    ksp = carve(reg2, 12288, 16, F32)
    att = [carve(reg2, 1024 * g, 512, BF16) for g in range(NG)]

    KVm = A("KVm", [128, 2048], F32)
    ksm = A("ksm", [128, 16], F32)
    KVb = A("KVb", [128, 2048], BF16)
    denl = A("denl", [128, NG * 64], BF16)

    reg3 = A("reg3", [128, 26624], U8)
    reg4 = A("reg4", [128, 18432], U8)
    hsb = [carve(reg3, 1024 * i, 512, BF16) if i < 26 else
           carve(reg4, 1024 * (i - 26), 512, BF16) for i in range(ICn)]
    qsb = [carve(reg4, 1024 * g, 512, BF16) for g in range(NG)]

    byb = [A(f"byb{i}", [128, 512], U8) for i in range(2)]
    u8a = [A(f"u8a{i}", [128, 512], U8) for i in range(2)]
    u8b = [A(f"u8b{i}", [128, 512], U8) for i in range(2)]
    u8c = [A(f"u8c{i}", [128, 512], U8) for i in range(2)]
    u8d = [A(f"u8d{i}", [128, 512], U8) for i in range(2)]
    thl = [A(f"thl{i}", [128, 512], BF16) for i in range(2)]
    thh = [A(f"thh{i}", [128, 512], BF16) for i in range(2)]
    sml = [A(f"sml{i}", [128, 512], BF16) for i in range(2)]
    smh = [A(f"smh{i}", [128, 512], BF16) for i in range(2)]
    r1b = [A(f"r1t{i}", [128, 512], BF16) for i in range(2)]
    r2b = [A(f"r2t{i}", [128, 512], BF16) for i in range(2)]
    mt = A("mt", [128, 512], BF16)
    sst = A("sst", [128, 512], BF16)
    sct = [A(f"sct{i}", [128, 512], BF16) for i in range(2)]
    dqw = [A(f"dqw{i}", [128, 512], BF16) for i in range(4)]

    dnreg = A("dnreg", [128, 11264], U8)
    dnrd = [carve(dnreg, 5632 * i, 22 * 128, BF16) for i in range(2)]
    wqb = [carve(dnreg, 4096 * i, 2048, BF16) for i in range(2)]
    dsb = carve(dnreg, 8192, 512, F32)[0:64, :]
    rcp = A("rcp", [64, 512], F32)
    rcpb = A("rcpb", [64, 512], BF16)
    nsb = A("nsb", [128, 512], BF16)

    silb = A("silb", [128, 512], BF16)
    yraw = [A(f"yraw{i}", [128, 512], BF16) for i in range(2)]
    ysq = [A(f"ysq{i}", [128, 512], BF16) for i in range(2)]
    rffb4 = [A(f"rffb{i}", [128, 512], BF16) for i in range(4)]

    pb = [nc.alloc_psum_tensor(f"pb{i}", [128, 512], F32) for i in range(8)]

    try:
        print(f"[kernel build] sbuf bytes remaining: {nc.sbuf_bytes_remaining}")
    except Exception:
        pass

    rowm = rowb[0:1, 0:512]
    rowv = rowb[0:1, 512:1024]
    rowt = rowb[0:1, 1024:1536]

    const_pairs = ((cst[:], consts), (mk_s[:], maskbd), (hm4_s[:], hm4),
                   (bmap_s[:], bmap), (ones_s[:], onesc), (ones1_s[:], ones1f),
                   (bp_s[:], bproj), (slot_s[:], slots))

    def init_consts(e):
        for dst, src in const_pairs:
            e.dma_start(out=dst, in_=src).then_inc(S["s_c"], 16)
    p.on("sync", init_consts)
    C_RDY = p.bump("s_c", len(const_pairs) * 16)

    def g_init(e):
        e.wait_ge(S["s_c"], C_RDY)
    p.on("gpsimd", g_init)

    wregs = {}

    def s_initregs(e):
        e.wait_ge(S["s_c"], C_RDY)
        wregs["gu"] = e.alloc_register("rs_gu")
        wregs["dn"] = e.alloc_register("rs_dn")
        wregs["tmp"] = e.alloc_register("rs_tmp")
        e.reg_load(wregs["gu"], slot_s[0:1, 2:3])
        e.reg_load(wregs["dn"], slot_s[0:1, 3:4])
    p.on("sync", s_initregs)

    if "0" in stages:
        for u in range(128):
            ch, hf = u // 8, u % 8
            buf = xbuf[u % 2]
            aw = p.val("s_acc")

            sx = f"s_x{u % 2}"

            def s_load(e, u=u, ch=ch, hf=hf, buf=buf, aw=aw, sx=sx):
                if u >= 2:
                    e.wait_ge(S["s_acc"], aw - 1)
                e.dma_start(out=buf[:],
                            in_=x_kv.rearrange("(c p) t -> p c t", p=128)[:, ch, 512 * hf:512 * hf + 512]
                            ).then_inc(S[sx], 16)
            p.on("sync", s_load)
            XW = p.bump(sx, 16)

            def a_sq(e, buf=buf, u=u, XW=XW, sx=sx):
                e.wait_ge(S[sx], XW)
                e.activation(out=xsqt[u % 2][:], in_=buf[:], func=AF.Square).then_inc(S["s_sq"], 1)
            p.on("scalar", a_sq)
            SQ = p.bump("s_sq", 1)

            def v_acc(e, buf=buf, ch=ch, hf=hf, u=u, XW=XW, SQ=SQ, sx=sx):
                e.wait_ge(S[sx], XW)
                sl = slice(512 * hf, 512 * hf + 512)
                if ch == 0:
                    e.tensor_copy(out=xacc[:, sl], in_=buf[:])
                else:
                    e.tensor_tensor(out=xacc[:, sl], in0=xacc[:, sl], in1=buf[:], op=ALU.add)
                e.wait_ge(S["s_sq"], SQ)
                if ch == 0:
                    e.tensor_copy(out=xqacc[:, sl], in_=xsqt[u % 2][:]).then_inc(S["s_acc"], 1)
                else:
                    e.tensor_tensor(out=xqacc[:, sl], in0=xqacc[:, sl], in1=xsqt[u % 2][:],
                                    op=ALU.add).then_inc(S["s_acc"], 1)
            p.on("vector", v_acc)
            p.bump("s_acc", 1)

        ACC_ALL = p.val("s_acc")

        for tb in range(8):
            evw = p.val("s_ev")
            abw = p.val("s_ab")

            def t_st(e, tb=tb, evw=evw):
                e.wait_ge(S["s_acc"], ACC_ALL)
                e.wait_ge(S["s_c"], C_RDY)
                if tb >= 1:
                    e.wait_ge(S["s_ev"], evw)
                e.matmul(pb[0][0:1, :], ones_s[:, 0:1], xacc[:, 512 * tb:512 * tb + 512],
                         start=True, stop=True)
                e.matmul(pb[1][0:1, :], ones_s[:, 0:1], xqacc[:, 512 * tb:512 * tb + 512],
                         start=True, stop=True).then_inc(S["s_pe"], 1)
            p.on("tensor", t_st)
            PEW = p.bump("s_pe", 1)

            def a_ev(e, tb=tb, PEW=PEW, abw=abw):
                e.wait_ge(S["s_pe"], PEW)
                if tb >= 1:
                    e.wait_ge(S["s_ab"], abw)
                e.activation(out=rowm, in_=pb[0][0:1, :], func=AF.Copy)
                e.activation(out=rowv, in_=pb[1][0:1, :], func=AF.Copy).then_inc(S["s_ev"], 1)
            p.on("scalar", a_ev)
            EV1 = p.bump("s_ev", 1)

            def v_row(e, EV1=EV1):
                e.wait_ge(S["s_ev"], EV1)
                e.tensor_scalar(out=rowm, in0=rowm, scalar1=1.0 / 2048.0, scalar2=None, op0=ALU.mult)
                e.tensor_scalar(out=rowv, in0=rowv, scalar1=1.0 / 2048.0, scalar2=None, op0=ALU.mult)
                e.tensor_tensor(out=rowt, in0=rowm, in1=rowm, op=ALU.mult)
                e.tensor_tensor(out=rowv, in0=rowv, in1=rowt, op=ALU.subtract).then_inc(S["s_ev"], 1)
            p.on("vector", v_row)
            EV2 = p.bump("s_ev", 1)

            def a_row(e, EV2=EV2):
                e.wait_ge(S["s_ev"], EV2)
                e.activation(out=rowv, in_=rowv, func=AF.Ln, bias=cst[0:1, 0:1])
                e.activation(out=rowv, in_=rowv, func=AF.Exp, scale=cst[0:1, 1:2]).then_inc(S["s_ev"], 1)
            p.on("scalar", a_row)
            EV3 = p.bump("s_ev", 1)

            def v_row2(e, EV3=EV3):
                e.wait_ge(S["s_ev"], EV3)
                e.tensor_tensor(out=rowm, in0=rowm, in1=rowv, op=ALU.mult)
                e.tensor_scalar(out=rowm, in0=rowm, scalar1=-1.0, scalar2=None,
                                op0=ALU.mult).then_inc(S["s_ev"], 1)
            p.on("vector", v_row2)
            EV4 = p.bump("s_ev", 1)

            def t_bc(e, EV4=EV4):
                e.wait_ge(S["s_ev"], EV4)
                e.matmul(pb[2][:], ones1_s[:], rowv, start=True, stop=True)
                e.matmul(pb[3][:], ones1_s[:], rowm, start=True, stop=True).then_inc(S["s_pe"], 1)
            p.on("tensor", t_bc)
            PEB = p.bump("s_pe", 1)

            def a_bc(e, tb=tb, PEB=PEB):
                e.wait_ge(S["s_pe"], PEB)
                e.activation(out=A_s[:, 512 * tb:512 * tb + 512], in_=pb[2][:], func=AF.Copy)
                e.activation(out=B_s[:, 512 * tb:512 * tb + 512], in_=pb[3][:],
                             func=AF.Copy).then_inc(S["s_ab"], 1)
            p.on("scalar", a_bc)
            p.bump("s_ab", 1)

        AB_RDY = p.val("s_ab")
        ST0_PE = p.val("s_pe")

        if "stats" in DBG:
            def dbg_ab(e):
                e.wait_ge(S["s_ab"], AB_RDY)
                e.dma_start(out=dbg_t["d_ab"][0:128, :], in_=A_s[:]).then_inc(S["s_dbg"], 16)
                e.dma_start(out=dbg_t["d_ab"][128:256, :], in_=B_s[:]).then_inc(S["s_dbg"], 16)
            p.on("gpsimd", dbg_ab)
            p.bump("s_dbg", 32)
    else:
        AB_RDY = p.val("s_ab")
        ACC_ALL = p.val("s_acc")
        ST0_PE = p.val("s_pe")

    xn_ctr = [0]

    def emit_xn_jit(c0, pe_guard):
        PARA = globals().get('PARA', 0)
        for ch in range(CH):
            buf = xbuf[ch % 2]
            gidx = xn_ctr[0]
            aw = p.val("s_xn")
            sx = f"s_x{gidx % 2}"

            def s_ld(e, ch=ch, buf=buf, c0=c0, aw=aw, gidx=gidx, PARA=PARA, sx=sx):
                if gidx >= 2:
                    e.wait_ge(S["s_xn"], aw if (PARA & 1) else aw - 1)
                else:
                    e.wait_ge(S["s_acc"], ACC_ALL)
                e.dma_start(out=buf[:],
                            in_=x_kv.rearrange("(c p) t -> p c t", p=128)[:, ch, c0:c0 + 512]
                            ).then_inc(S[sx], 16)
            p.on("sync", s_ld)
            XW = p.bump(sx, 16)

            def v_xn(e, ch=ch, buf=buf, c0=c0, XW=XW, pe_guard=pe_guard, sx=sx):
                e.wait_ge(S[sx], XW)
                e.wait_ge(S["s_ab"], AB_RDY)
                if pe_guard:
                    e.wait_ge(S["s_pe"], pe_guard)
                e.tensor_tensor(out=xjit[:], in0=buf[:], in1=A_s[:, c0:c0 + 512], op=ALU.mult)
                e.tensor_tensor(out=xn[ch][:], in0=xjit[:], in1=B_s[:, c0:c0 + 512],
                                op=ALU.add).then_inc(S["s_xn"], 1)
            p.on("vector", v_xn)
            p.bump("s_xn", 1)
            xn_ctr[0] += 1
        return p.val("s_xn")

    seq = []
    for ob in range(GU_OB):
        for t in range(GU_BT):
            seq.append(("gu", t, ob))
    for obd in range(DN_OB):
        for t in range(DN_BT):
            seq.append(("dn", t, obd))

    dqs = {"idx": 0, "int": [], "casta": [], "nib2": [], "nibm": [], "w": []}
    dqw_gu, dqw_dn = {}, {}

    def _dq_phase1(ti):
        kind, t, ob = seq[ti]
        i = ti % 2
        bybuf = byb[i]
        src_by = dn_by if kind == "dn" else gu_by
        src_sc = dn_sct if kind == "dn" else gu_sct
        k_sc = (22 * ob + t) if kind == "dn" else (8 * ob + t)

        int2 = dqs["int"][ti - 2] if ti >= 2 else 0
        casta2 = dqs["casta"][ti - 2] if ti >= 2 else 0
        nib2_2 = dqs["nib2"][ti - 2] if ti >= 2 else 0
        dqa_prev = p.val("s_dqa")

        sby = f"s_by{ti % 2}"

        def s_by_ld(e, t=t, ob=ob, bybuf=bybuf, src_by=src_by, int2=int2, ti=ti, sby=sby):
            if ti >= 2:
                e.wait_ge(S["s_dqv"], int2)
            e.dma_start(out=bybuf[:], in_=src_by[128 * t:128 * t + 128, 512 * ob:512 * ob + 512]
                        ).then_inc(S[sby], 16)
        p.on("sync", s_by_ld)
        BY = p.bump(sby, 16)

        ssc = f"s_sc{i}"

        def g_scl(e, i=i, src_sc=src_sc, k_sc=k_sc, nib2_2=nib2_2, ti=ti, ssc=ssc):
            if ti >= 2:
                e.wait_ge(S["s_dqv"], nib2_2)
            e.dma_start(out=sct[i][:], in_=src_sc[128 * k_sc:128 * k_sc + 128, :]
                        ).then_inc(S[ssc], 16)
        p.on("gpsimd", g_scl)
        SCL = p.bump(ssc, 16)

        def v_int(e, i=i, bybuf=bybuf, BY=BY, casta2=casta2, ti=ti, sby=sby):
            e.wait_ge(S[sby], BY)
            if ti >= 2:
                e.wait_ge(S["s_dqp2"], casta2)
            e.tensor_scalar(out=u8a[i][:], in0=bybuf[:], scalar1=7, scalar2=None, op0=ALU.bitwise_and)
            e.tensor_scalar(out=u8b[i][:], in0=bybuf[:], scalar1=4, scalar2=7,
                            op0=ALU.logical_shift_right, op1=ALU.bitwise_and)
            e.tensor_scalar(out=u8c[i][:], in0=bybuf[:], scalar1=8, scalar2=None, op0=ALU.bitwise_and)
            e.tensor_scalar(out=u8d[i][:], in0=bybuf[:], scalar1=7, scalar2=None,
                            op0=ALU.logical_shift_right).then_inc(S["s_dqv"], 1)
        p.on("vector", v_int)
        INTM = p.bump("s_dqv", 1)
        dqs["int"].append(INTM)

        def v_cast(e, i=i, dqa_prev=dqa_prev, ti=ti):
            if dqa_prev:
                e.wait_ge(S["s_dqa"], dqa_prev)
            e.tensor_scalar(out=thl[i][:], in0=u8a[i][:], scalar1=0.5, scalar2=None, op0=ALU.mult)
            e.tensor_scalar(out=thh[i][:], in0=u8b[i][:], scalar1=0.5, scalar2=None,
                            op0=ALU.mult).then_inc(S["s_dqp"], 1)
        p.on("vector", v_cast)
        CASTV = p.bump("s_dqp", 1)

        def a_cast(e, i=i, INTM=INTM, nib2_2=nib2_2, ti=ti):
            e.wait_ge(S["s_dqv"], INTM)
            if ti >= 2:
                e.wait_ge(S["s_dqv"], nib2_2)
            e.activation(out=sml[i][:], in_=u8c[i][:], func=AF.Copy, scale=-0.25)
            e.activation(out=smh[i][:], in_=u8d[i][:], func=AF.Copy,
                         scale=-2.0).then_inc(S["s_dqp2"], 1)
        p.on("scalar", a_cast)
        CASTA = p.bump("s_dqp2", 1)
        dqs["casta"].append(CASTA)
        return dict(ti=ti, kind=kind, t=t, ob=ob, i=i, CASTV=CASTV, SCL=SCL, ssc=ssc)

    def _dq_phase2(ctx):
        ti, kind, t, ob, i = ctx["ti"], ctx["kind"], ctx["t"], ctx["ob"], ctx["i"]
        CASTV, SCL, ssc = ctx["CASTV"], ctx["SCL"], ctx["ssc"]
        for nib in range(2):
            th_t = [thl, thh][nib][i]
            sm_t = [sml, smh][nib][i]
            r1 = r1b[nib]
            r2 = r2b[nib]
            wt = dqw[(2 * ti + nib) % 4]
            prevn = dqs["nibm"][2 * (ti - 1) + nib] if ti >= 1 else 0

            def a_relu(e, th_t=th_t, r1=r1, r2=r2, CASTV=CASTV, prevn=prevn):
                e.wait_ge(S["s_dqp"], CASTV)
                if prevn:
                    e.wait_ge(S["s_dqv"], prevn)
                e.activation(out=r1[:], in_=th_t[:], func=AF.Relu, bias=cst[:, 2:3])
                e.activation(out=r2[:], in_=th_t[:], func=AF.Relu, bias=cst[:, 3:4],
                             scale=cst[:, 4:5]).then_inc(S["s_dqa"], 1)
            p.on("scalar", a_relu)
            DQA = p.bump("s_dqa", 1)

            wm_idx = 2 * ti + nib
            sdqw = f"s_dqw{wm_idx % 4}"
            w_prev = p.val(sdqw) if wm_idx >= 4 else 0

            def v_nib(e, th_t=th_t, sm_t=sm_t, wt=wt, i=i, DQA=DQA, SCL=SCL, ssc=ssc,
                      w_prev=w_prev, sdqw=sdqw, r1=r1, r2=r2):
                e.wait_ge(S["s_dqa"], DQA)
                e.wait_ge(S[ssc], SCL)
                if w_prev:
                    e.wait_ge(S[sdqw], w_prev)
                e.scalar_tensor_tensor(out=mt[:], in0=th_t[:], scalar=1.0, in1=r1[:],
                                       op0=ALU.mult, op1=ALU.add)
                e.tensor_tensor(out=mt[:], in0=mt[:], in1=r2[:], op=ALU.add)
                e.scalar_tensor_tensor(out=sst[:], in0=sm_t[:], scalar=1.0, in1=sct[i][:],
                                       op0=ALU.add, op1=ALU.mult)
                e.tensor_tensor(out=wt[:], in0=mt[:], in1=sst[:], op=ALU.mult).then_inc(S["s_dqv"], 1)
            p.on("vector", v_nib)
            NIB = p.bump("s_dqv", 1)
            dqs["nibm"].append(NIB)
            if nib == 1:
                dqs["nib2"].append(NIB)

            if kind == "gu":
                jj0 = 4 * ob
                hc = 2 * t + nib
                dst = wl_gu.ap().rearrange("(jj p hc) o -> jj p hc o", p=128, hc=16)[jj0:jj0 + 4, :, hc, :]
            else:
                hc0 = 4 * ob
                ic = 2 * t + nib
                dst = wl_dn.ap().rearrange("(Hc p ic) o -> Hc p ic o", p=128, ic=ICn)[hc0:hc0 + 4, :, ic, :]

            def s_wt_w(e, wt=wt, dst=dst, NIB=NIB, sdqw=sdqw):
                e.wait_ge(S["s_dqv"], NIB)
                e.dma_start(out=dst.transpose([1, 0, 2]),
                            in_=wt[:].rearrange("p (q o) -> p q o", q=4)
                            ).then_inc(S[sdqw], 16)
            p.on("sync", s_wt_w)
            p.bump(sdqw, 16)
            dqs["w"].append(tuple(p.val(f"s_dqw{k}") for k in range(4)))

    dq_pend = []

    def emit_dq_tiles(n):
        if "Q" not in stages:
            return
        for _ in range(n):
            ti = dqs["idx"]
            if ti >= len(seq):
                return
            dqs["idx"] += 1
            dq_pend.append(_dq_phase1(ti))
            if len(dq_pend) >= 2:
                _dq_phase2(dq_pend.pop(0))

    def flush_dq():
        while dq_pend:
            _dq_phase2(dq_pend.pop(0))

    gu_chunks = [(0, 3), (3, 6), (6, 9), (9, 11)]
    gu_copied = set()

    def emit_gu_copy(ci):
        if "Q" not in stages or ci in gu_copied:
            return
        gu_copied.add(ci)
        o0, o1 = gu_chunks[ci]
        need4 = dqs["w"][2 * (8 * o1) - 1]
        e0 = (4 * o0) * 16 * 16384
        e1 = (4 * o1) * 16 * 16384

        def s_cp(e, need4=need4, e0=e0, e1=e1):
            for k in range(4):
                if need4[k]:
                    e.wait_ge(S[f"s_dqw{k}"], need4[k])
            e.reg_add(wregs["tmp"], wregs["gu"], e0)
            dst = bass.AP(wt_gu, wregs["tmp"], [[1, e1 - e0]])
            e.dma_start(out=dst, in_=bass.AP(wl_gu, e0, [[1, e1 - e0]])
                        ).then_inc(S["s_dqc"], 16)
        p.on("sync", s_cp)
        p.bump("s_dqc", 16)

    if "A" in stages:
        kv_mm_marks = []
        tkv_pend = []
        for sl in range(NSL):
            pe_guard = p.val("s_pe") if sl > 0 else ST0_PE
            XN_RDY = emit_xn_jit(512 * sl, pe_guard)

            for r in range(NRA):
                for kv in range(2):
                    ridx = (sl * NRA + r) * 2 + kv
                    wbuf = wkv[ridx % 2]
                    wfree = kv_mm_marks[ridx - 2] if ridx >= 2 else ST0_PE
                    col0 = 2048 + 2048 * kv + 256 * r

                    wfull = kv_mm_marks[ridx - 1] if ridx >= 1 else ST0_PE
                    swkv = f"s_wkv{ridx % 2}"
                    blk = 8 * kv + r

                    def g_w(e, wbuf=wbuf, wfree=wfree, blk=blk, wfull=wfull, swkv=swkv):
                        e.wait_ge(S["s_c"], C_RDY)
                        wf = wfull if (globals().get('PARA', 0) & 8) else wfree
                        if wf:
                            e.wait_ge(S["s_pe"], wf)
                        e.dma_start(out=wbuf[:],
                                    in_=wkv_r.rearrange("(b p) o -> b p o", p=128)[blk]
                                    ).then_inc(S[swkv], 16)
                    p.on("gpsimd", g_w)
                    WKV = p.bump(swkv, 16)
                    pe2_prev = p.val("s_pe2")

                    for tk in range(NTK):
                        evsem = "s_evk" if kv == 0 else "s_evv"
                        evm = p.val(evsem)

                        PARA = globals().get('PARA', 0)

                        def t_mm(e, tk=tk, kv=kv, wbuf=wbuf, WKV=WKV, XN_RDY=XN_RDY,
                                 evm=evm, ridx=ridx, evsem=evsem, PARA=PARA, swkv=swkv):
                            e.wait_ge(S["s_xn"], XN_RDY)
                            e.wait_ge(S[swkv], WKV)
                            if ridx + tk > 0:
                                e.wait_ge(S[evsem], evm if (PARA & 2) else max(0, evm - 1))
                            pq = pb[2 * kv + tk % 2]
                            wv = wbuf[:].rearrange("p (c o) -> p c o", o=256)
                            for ch in range(CH):
                                mm = e.matmul(pq[:, 0:256], xn[ch][:, 128 * tk:128 * tk + 128],
                                              wv[:, ch, :], start=(ch == 0), stop=(ch == CH - 1))
                            mm.then_inc(S["s_pe"], 1)
                        p.on("tensor", t_mm)
                        PEM = p.bump("s_pe", 1)

                        if kv == 0:
                            def a_evk(e, tk=tk, PEM=PEM, pe2_prev=pe2_prev):
                                e.wait_ge(S["s_pe"], PEM)
                                if pe2_prev:
                                    e.wait_ge(S["s_pe2"], pe2_prev)
                                e.activation(out=kvK[tk][:], in_=pb[tk % 2][:, 0:256],
                                             func=AF.Relu).then_inc(S["s_evk"], 1)
                            p.on("scalar", a_evk)
                            p.bump("s_evk", 1)
                        else:
                            def v_evv(e, tk=tk, PEM=PEM, pe2_rs=pe2_prev):
                                e.wait_ge(S["s_pe"], PEM)
                                if pe2_rs:
                                    e.wait_ge(S["s_pe2"], pe2_rs)
                                e.activation(out=kvV[tk][:], in_=pb[2 + tk % 2][:, 0:256],
                                             func=AF.Copy).then_inc(S["s_evv"], 1)
                            p.on("scalar", v_evv)
                            EVV = p.bump("s_evv", 1)
                            evk_need = p.val("s_evk")

                            def t_KV(e, tk=tk, EVV=EVV, evk_need=evk_need):
                                e.wait_ge(S["s_evv"], EVV)
                                e.wait_ge(S["s_evk"], evk_need)
                                kb = kvK[tk]
                                vb = kvV[tk]
                                for gl in range(2):
                                    e.matmul(pb[4][:, 128 * gl:128 * gl + 128],
                                             kb[:, 128 * gl:128 * gl + 128],
                                             vb[:, 128 * gl:128 * gl + 128],
                                             start=(tk == 0 and gl == 0), stop=(tk == NTK - 1))
                                e.matmul(pb[5][:, 0:1], kb[:, 0:128], ones_s[:, 0:1],
                                         start=(tk == 0), stop=(tk == NTK - 1))
                                e.matmul(pb[5][:, 1:2], kb[:, 128:256], ones_s[:, 0:1],
                                         start=False, stop=(tk == NTK - 1)).then_inc(S["s_pe2"], 1)
                            if tkv_pend:
                                p.on("tensor", tkv_pend.pop())
                                p.bump("s_pe2", 1)
                            tkv_pend.append(t_KV)

                    if kv == 1 and tkv_pend:
                        p.on("tensor", tkv_pend.pop())
                        p.bump("s_pe2", 1)
                    kv_mm_marks.append(p.val("s_pe"))

                PE2 = p.val("s_pe2")

                def v_kvadd(e, sl=sl, r=r, PE2=PE2):
                    e.wait_ge(S["s_pe2"], PE2)
                    slc = slice(256 * r, 256 * r + 256)
                    if sl == 0:
                        e.tensor_copy(out=KVm[:, slc], in_=pb[4][:, 0:256])
                        e.tensor_copy(out=ksm[:, 2 * r:2 * r + 2], in_=pb[5][:, 0:2]).then_inc(S["s_ev"], 1)
                    else:
                        e.tensor_tensor(out=KVm[:, slc], in0=KVm[:, slc], in1=pb[4][:, 0:256], op=ALU.add)
                        e.tensor_tensor(out=ksm[:, 2 * r:2 * r + 2], in0=ksm[:, 2 * r:2 * r + 2],
                                        in1=pb[5][:, 0:2], op=ALU.add).then_inc(S["s_ev"], 1)
                p.on("vector", v_kvadd)
                KVEV = p.bump("s_ev", 1)

                def t_guard(e, KVEV=KVEV):
                    e.wait_ge(S["s_ev"], KVEV)
                p.on("tensor", t_guard)

                if not globals().get('NO_ILV'):
                    emit_dq_tiles((2, 1, 2, 1, 2, 1, 1, 1)[r])

            if not globals().get('NO_ILV'):
                if sl == 3:
                    emit_gu_copy(0)
                elif sl == 5:
                    emit_gu_copy(1)
                elif sl == 6:
                    emit_gu_copy(2)

        A_DONE_EV = p.val("s_ev")
        A_PE = p.val("s_pe")
    else:
        A_DONE_EV = p.val("s_ev")
        A_PE = p.val("s_pe")

    if "X" in stages:
        if "kvpre" in DBG:
            def dbg_kvpre(e):
                e.wait_ge(S["s_ev"], A_DONE_EV)
                e.dma_start(out=dbg_t["d_kvp"][:, 0:2048], in_=KVm[:]).then_inc(S["s_dbg"], 16)
                e.dma_start(out=dbg_t["d_kvp"][:, 2048:2064], in_=ksm[:]).then_inc(S["s_dbg"], 16)
            p.on("sync", dbg_kvpre)
            KVPRE = p.bump("s_dbg", 32)

        def s_xw(e):
            e.wait_ge(S["s_ev"], A_DONE_EV)
            with e.register("mo") as mo, e.register("mf") as mf:
                e.reg_load(mo, slot_s[0:1, 0:1])
                e.reg_mul(mf, mo, 128 * 2064)
                e.dma_start(out=bass.AP(kvx, mf, [[2064, 128], [1, 2048]]),
                            in_=KVm[:]).then_inc(S["s_kvw"], 16)
                with e.register("mf2") as mf2:
                    e.reg_add(mf2, mf, 2048)
                    e.dma_start(out=bass.AP(kvx, mf2, [[2064, 128], [1, 16]]),
                                in_=ksm[:]).then_inc(S["s_kvw"], 16)
        p.on("sync", s_xw)
        KVW = p.bump("s_kvw", 32)

        def g_flag(e, KVW=KVW):
            e.wait_ge(S["s_kvw"], KVW)
            with e.register("f_mo") as mo, e.register("f_mf") as mf:
                e.reg_load(mo, slot_s[0:1, 0:1])
                e.reg_mul(mf, mo, 128)
                e.dma_start(out=bass.AP(flg, mf, [[128, 1], [1, 1]]),
                            in_=slot_s[0:1, 4:5]).then_inc(S["s_kvw"], 16)
        p.on("gpsimd", g_flag)
        p.bump("s_kvw", 16)

        def g_spin(e):
            with e.register("po") as po, e.register("pf") as pf, \
                 e.register("t") as t, e.register("f") as f, \
                 e.register("nn") as nn:
                e.reg_load(nn, slot_s[0:1, 4:5])
                e.reg_load(po, slot_s[0:1, 1:2])
                e.reg_mul(pf, po, 128)
                fsrc = bass.AP(flg, pf, [[128, 1], [1, 1]])
                e.reg_mov(t, 16)
                e.reg_mov(f, 0)

                def cond():
                    e.dma_start(out=flag_s[0:1, 0:1], in_=fsrc).then_inc(S["s_poll"], 16)
                    e.wait_ge(S["s_poll"], t)
                    e.reg_add(t, t, 16)
                    e.reg_load(f, flag_s[0:1, 0:1])
                    return RuntimeValue(f) != RuntimeValue(nn)

                with e.While(cond):
                    pass
                with e.register("pk") as pk:
                    e.reg_mul(pk, po, 128 * 2064)
                    e.dma_start(out=KVp[:], in_=bass.AP(kvx, pk, [[2064, 128], [1, 2048]])
                                ).then_inc(S["s_xch"], 16)
                    with e.register("pk2") as pk2:
                        e.reg_add(pk2, pk, 2048)
                        e.dma_start(out=ksp[:], in_=bass.AP(kvx, pk2, [[2064, 128], [1, 16]])
                                    ).then_inc(S["s_xch"], 16)
        p.on("gpsimd", g_spin)
        XCH = p.bump("s_xch", 32)

        def v_kvfull(e, XCH=XCH):
            e.wait_ge(S["s_xch"], XCH)
            if "kvpre" in DBG:
                e.wait_ge(S["s_dbg"], KVPRE)
            e.tensor_tensor(out=KVm[:], in0=KVm[:], in1=KVp[:], op=ALU.add)
            e.tensor_tensor(out=ksm[:], in0=ksm[:], in1=ksp[:], op=ALU.add)
            for g in range(NG):
                e.tensor_tensor(out=KVb[:, 128 * g:128 * g + 128],
                                in0=KVm[:, 128 * g:128 * g + 128], in1=mk_s[:], op=ALU.mult)
            e.memset(denl[:], 0.0)
            last = None
            for g in range(NG):
                for hh in range(4):
                    col = 64 * g + 4 * g + hh
                    last = e.tensor_scalar(out=denl[:, col:col + 1], in0=hm4_s[:, hh:hh + 1],
                                           scalar1=ksm[:, g:g + 1], scalar2=None, op0=ALU.mult)
            last.then_inc(S["s_xch"], 1)
        p.on("vector", v_kvfull)
        KV_RDY = p.bump("s_xch", 1)

        if "kv" in DBG:
            def dbg_kv(e):
                e.wait_ge(S["s_xch"], KV_RDY)
                e.dma_start(out=dbg_t["d_kv"][0:128, 0:2048], in_=KVm[:]).then_inc(S["s_dbg"], 16)
                e.dma_start(out=dbg_t["d_kv"][0:128, 2048:2064], in_=ksm[:]).then_inc(S["s_dbg"], 16)
                e.dma_start(out=dbg_t["d_kv"][128:256, 0:2048], in_=KVb[:]).then_inc(S["s_dbg"], 16)
            p.on("gpsimd", dbg_kv)
            p.bump("s_dbg", 48)
    else:
        KV_RDY = p.val("s_xch")

    if "Q" in stages:
        emit_dq_tiles(88 - dqs["idx"])
        flush_dq()
        for ci in range(4):
            emit_gu_copy(ci)
        DQC_GU = p.val("s_dqc")

        def g_flag_dq(e):
            e.wait_ge(S["s_dqc"], DQC_GU)
            with e.register("fq_mo") as mo, e.register("fq_mf") as mf:
                e.reg_load(mo, slot_s[0:1, 0:1])
                e.reg_mul(mf, mo, 128)
                with e.register("fq_mf2") as mf2:
                    e.reg_add(mf2, mf, 256)
                    e.dma_start(out=bass.AP(flg, mf2, [[128, 1], [1, 1]]),
                                in_=slot_s[0:1, 4:5]).then_inc(S["s_kvw"], 16)
        p.on("gpsimd", g_flag_dq)
        p.bump("s_kvw", 16)

        def g_spin_dq(e):
            with e.register("dq_po") as po, e.register("dq_pf") as pf, \
                 e.register("dq_t") as t, e.register("dq_f") as f, \
                 e.register("dq_nn") as nn:
                e.reg_load(nn, slot_s[0:1, 4:5])
                e.reg_load(po, slot_s[0:1, 1:2])
                e.reg_mul(pf, po, 128)
                with e.register("dq_pf2") as pf2:
                    e.reg_add(pf2, pf, 256)
                    fsrc = bass.AP(flg, pf2, [[128, 1], [1, 1]])
                    e.reg_mov(t, 16)
                    e.reg_mov(f, 0)

                    def cond():
                        e.dma_start(out=flag_s[0:1, 1:2], in_=fsrc).then_inc(S["s_poll2"], 16)
                        e.wait_ge(S["s_poll2"], t)
                        e.reg_add(t, t, 16)
                        e.reg_load(f, flag_s[0:1, 1:2])
                        return RuntimeValue(f) != RuntimeValue(nn)

                    with e.While(cond):
                        pass
                e.sem_inc(S["s_xch"], 1)
        p.on("gpsimd", g_spin_dq)
        DQ_RDY2 = p.bump("s_xch", 1)
    else:
        DQ_RDY2 = p.val("s_xch")

    if "C" in stages:
        for qt in range(NQ):
            pe_guard = p.val("s_pe") if (qt > 0 or "A" in stages) else ST0_PE
            XN_RDY = emit_xn_jit(512 * qt, pe_guard)

            gu_marks = []
            for j in range(NJ):
                sgu = f"s_gu{j % 2}"
                for half, jj in ((0, j), (1, 44 + j)):
                    rb = gurd[2 * (j % 2) + half]
                    need = DQ_RDY2 if qt == 0 and j < 2 else 0
                    if j >= 2:
                        gufree = gu_marks[j - 2]
                    elif qt == 0:
                        gufree = A_PE
                    else:
                        gufree = gu_prev_last

                    def s_rd(e, rb=rb, jj=jj, need=need, gufree=gufree, sgu=sgu):
                        if gufree:
                            e.wait_ge(S["s_pe"], gufree)
                        if need:
                            e.wait_ge(S["s_xch"], need)
                        e.dma_start(out=rb[:],
                                    in_=wt_gu.ap().rearrange("(jj p r) o -> jj p (r o)", jj=88, p=128)[jj]
                                    ).then_inc(S[sgu], 16)
                    p.on("sync", s_rd)
                p.bump(sgu, 32)
                WT = p.val(sgu)
                evg = p.val("s_ev")

                def t_gu(e, j=j, WT=WT, XN_RDY=XN_RDY, evg=evg, sgu=sgu):
                    e.wait_ge(S[sgu], WT)
                    e.wait_ge(S["s_xn"], XN_RDY)
                    e.wait_ge(S["s_ev"], max(0, evg - 2))
                    pg = pb[2 * (j % 2)]
                    pu = pb[2 * (j % 2) + 1]
                    for hc in range(CH):
                        e.matmul(pg[:], gurd[2 * (j % 2)][:, 128 * hc:128 * hc + 128], xn[hc][:],
                                 start=(hc == 0), stop=(hc == CH - 1))
                    for hc in range(CH):
                        mm = e.matmul(pu[:], gurd[2 * (j % 2) + 1][:, 128 * hc:128 * hc + 128],
                                      xn[hc][:], start=(hc == 0), stop=(hc == CH - 1))
                    mm.then_inc(S["s_pe"], 1)
                p.on("tensor", t_gu)
                GU = p.bump("s_pe", 1)
                gu_marks.append(GU)

                def a_sil(e, j=j, GU=GU):
                    e.wait_ge(S["s_pe"], GU)
                    e.activation(out=silb[:], in_=pb[2 * (j % 2)][:], func=AF.Silu).then_inc(S["s_ev"], 1)
                p.on("scalar", a_sil)
                SIL = p.bump("s_ev", 1)

                def v_h(e, j=j, SIL=SIL):
                    e.wait_ge(S["s_ev"], SIL)
                    e.tensor_tensor(out=hsb[j][:], in0=silb[:], in1=pb[2 * (j % 2) + 1][:],
                                    op=ALU.mult).then_inc(S["s_ev"], 1)
                p.on("vector", v_h)
                p.bump("s_ev", 1)

                if qt == 0 and j < 22:
                    emit_dq_tiles(2)

            gu_prev_last = gu_marks[-1]
            H_RDY = p.val("s_ev")

            if qt == 0 and "Q" in stages:
                assert dqs["idx"] == len(seq), dqs["idx"]
                flush_dq()
                for obd in range(2):
                    need4 = dqs["w"][2 * (88 + 22 * (obd + 1)) - 1]
                    e0 = (4 * obd) * ICn * 16384
                    e1 = (4 * (obd + 1)) * ICn * 16384

                    def g_cpd(e, need4=need4, e0=e0, e1=e1, obd=obd):
                        for k in range(4):
                            if need4[k]:
                                e.wait_ge(S[f"s_dqw{k}"], need4[k])
                        with e.register(f"cd_b{obd}") as rb_, e.register(f"cd_t{obd}") as rt_:
                            e.reg_load(rb_, slot_s[0:1, 3:4])
                            e.reg_add(rt_, rb_, e0)
                            dst = bass.AP(wt_dn, rt_, [[1, e1 - e0]])
                            e.dma_start(out=dst, in_=bass.AP(wl_dn, e0, [[1, e1 - e0]])
                                        ).then_inc(S["s_dqc"], 16)
                    p.on("gpsimd", g_cpd)
                    p.bump("s_dqc", 16)
                DQC_DN = p.val("s_dqc")

                def g_flag_dn(e, DQC_DN=DQC_DN):
                    e.wait_ge(S["s_dqc"], DQC_DN)
                    with e.register("fd_mo") as mo, e.register("fd_mf") as mf:
                        e.reg_load(mo, slot_s[0:1, 0:1])
                        e.reg_mul(mf, mo, 128)
                        with e.register("fd_mf2") as mf2:
                            e.reg_add(mf2, mf, 512)
                            e.dma_start(out=bass.AP(flg, mf2, [[128, 1], [1, 1]]),
                                        in_=slot_s[0:1, 4:5]).then_inc(S["s_kvw"], 16)
                p.on("gpsimd", g_flag_dn)
                p.bump("s_kvw", 16)

                def g_spin_dn(e):
                    with e.register("dn_po") as po, e.register("dn_pf") as pf, \
                         e.register("dn_t") as t, e.register("dn_f") as f, \
                         e.register("dn_nn") as nn:
                        e.reg_load(nn, slot_s[0:1, 4:5])
                        e.reg_load(po, slot_s[0:1, 1:2])
                        e.reg_mul(pf, po, 128)
                        with e.register("dn_pf2") as pf2:
                            e.reg_add(pf2, pf, 512)
                            fsrc = bass.AP(flg, pf2, [[128, 1], [1, 1]])
                            e.reg_mov(t, 16)
                            e.reg_mov(f, 0)

                            def cond():
                                e.dma_start(out=flag_s[0:1, 2:3], in_=fsrc).then_inc(S["s_poll3"], 16)
                                e.wait_ge(S["s_poll3"], t)
                                e.reg_add(t, t, 16)
                                e.reg_load(f, flag_s[0:1, 2:3])
                                return RuntimeValue(f) != RuntimeValue(nn)

                            with e.While(cond):
                                pass
                    e.sem_inc(S["s_xch"], 1)
                p.on("gpsimd", g_spin_dn)
                DN_RDY = p.bump("s_xch", 1)
            elif qt == 0:
                DN_RDY = p.val("s_xch")

            dn_h0 = []
            dn_marks = []
            sff_marks = []
            sff_pending = []
            for Hc in range(CH):
                need = DN_RDY if qt == 0 and Hc < 2 else 0
                dn_cnt = [0, 0]
                for hv in range(2):
                    rb = dnrd[hv]
                    if Hc >= 1:
                        dnfree = dn_h0[Hc - 1] if hv == 0 else dn_marks[Hc - 1]
                    else:
                        dnfree = (A_PE if qt == 0 else dn_prev_last)

                    def s_dnl(e, rb=rb, Hc=Hc, hv=hv, need=need, dnfree=dnfree):
                        if dnfree:
                            e.wait_ge(S["s_pe"], dnfree)
                        if need:
                            e.wait_ge(S["s_xch"], need)
                        e.dma_start(out=rb[:],
                                    in_=wt_dn.ap().rearrange("(Hc p r) o -> Hc p (r o)", Hc=16, p=128)
                                    [Hc][:, 22 * 128 * hv:22 * 128 * (hv + 1)]
                                    ).then_inc(S[f"s_dn{hv}"], 16)
                    p.on("sync", s_dnl)
                    dn_cnt[hv] = p.bump(f"s_dn{hv}", 16)
                evd = p.val("s_ev")

                def t_dn_a(e, Hc=Hc, WT=dn_cnt[0], H_RDY=H_RDY, evd=evd):
                    e.wait_ge(S["s_dn0"], WT)
                    e.wait_ge(S["s_ev"], H_RDY)
                    e.wait_ge(S["s_ev"], max(0, evd - 1))
                    pd = pb[4 + Hc % 2]
                    for ic in range(22):
                        mm = e.matmul(pd[:], dnrd[0][:, 128 * ic:128 * ic + 128], hsb[ic][:],
                                      start=(ic == 0), stop=False)
                    mm.then_inc(S["s_pe"], 1)
                p.on("tensor", t_dn_a)
                dn_h0.append(p.bump("s_pe", 1))

                def t_dn_b(e, Hc=Hc, WT=dn_cnt[1]):
                    e.wait_ge(S["s_dn1"], WT)
                    pd = pb[4 + Hc % 2]
                    for ic in range(22, ICn):
                        mm = e.matmul(pd[:], dnrd[1][:, 128 * (ic - 22):128 * (ic - 22) + 128], hsb[ic][:],
                                      start=False, stop=(ic == ICn - 1))
                    mm.then_inc(S["s_pe"], 1)
                p.on("tensor", t_dn_b)
                DN = p.bump("s_pe", 1)
                dn_marks.append(DN)

                if sff_pending:
                    p.on("tensor", sff_pending.pop())
                    sff_marks.append(p.bump("s_pe2", 1))

                sffw = f"s_ffw{Hc % 2}"
                ffw = p.val(sffw)
                sffp = sff_marks[Hc - 2] if Hc >= 2 else 0

                def a_y(e, Hc=Hc, DN=DN, ffw=ffw, sffp=sffp, sffw=sffw):
                    e.wait_ge(S["s_pe"], DN)
                    if ffw:
                        e.wait_ge(S[sffw], ffw)
                    if Hc >= 2:
                        e.wait_ge(S["s_pe2"], sffp)
                    e.activation(out=yraw[Hc % 2][:], in_=pb[4 + Hc % 2][:], func=AF.Copy)
                    e.activation(out=ysq[Hc % 2][:], in_=yraw[Hc % 2][:], func=AF.Square).then_inc(S["s_ev"], 1)
                p.on("scalar", a_y)
                YEV = p.bump("s_ev", 1)

                def t_sff(e, Hc=Hc, YEV=YEV):
                    e.wait_ge(S["s_ev"], YEV)
                    e.matmul(pb[6][0:1, :], ones_s[:, 0:1], ysq[Hc % 2][:],
                             start=(Hc == 0), stop=(Hc == CH - 1)).then_inc(S["s_pe2"], 1)
                sff_pending.append(t_sff)

                def s_ffw_f(e, Hc=Hc, qt=qt, YEV=YEV, sffw=sffw):
                    e.wait_ge(S["s_ev"], YEV)
                    e.dma_start(out=ffn_d.rearrange("(c p) t -> c p t", p=128)[Hc, :, 512 * qt:512 * qt + 512],
                                in_=yraw[Hc % 2][:]).then_inc(S[sffw], 16)
                p.on("sync", s_ffw_f)
                p.bump(sffw, 16)

            dn_prev_last = dn_marks[-1]
            p.on("tensor", sff_pending.pop())
            sff_marks.append(p.bump("s_pe2", 1))
            SFF = p.val("s_pe2")

            def v_sff(e, SFF=SFF):
                e.wait_ge(S["s_pe2"], SFF)
                e.tensor_scalar(out=rowt, in0=pb[6][0:1, :], scalar1=1.0 / 2048.0,
                                scalar2=None, op0=ALU.mult).then_inc(S["s_ev"], 1)
            p.on("vector", v_sff)
            VS = p.bump("s_ev", 1)

            def a_sff(e, VS=VS):
                e.wait_ge(S["s_ev"], VS)
                e.activation(out=rowt, in_=rowt, func=AF.Ln, bias=cst[0:1, 5:6])
                e.activation(out=rowt, in_=rowt, func=AF.Exp, scale=cst[0:1, 1:2]).then_inc(S["s_ev"], 1)
            p.on("scalar", a_sff)
            RFQ = p.bump("s_ev", 1)

            def t_rbq(e, RFQ=RFQ):
                e.wait_ge(S["s_ev"], RFQ)
                e.matmul(pb[6][:], ones1_s[:], rowt, start=True, stop=True).then_inc(S["s_pe2"], 1)
            p.on("tensor", t_rbq)
            RBQ = p.bump("s_pe2", 1)

            def a_rbq(e, qt=qt, RBQ=RBQ):
                e.wait_ge(S["s_pe2"], RBQ)
                e.activation(out=rffb4[qt][:], in_=pb[6][:], func=AF.Copy).then_inc(S["s_ev"], 1)
            p.on("scalar", a_rbq)
            RFF = p.bump("s_ev", 1)

            def t_g2(e, RFF=RFF):
                e.wait_ge(S["s_ev"], RFF)
            p.on("tensor", t_g2)

        FFW_ALL = (p.val("s_ffw0"), p.val("s_ffw1"))
        RFF_RDY = p.val("s_ev")
        C_PE = p.val("s_pe")
    else:
        FFW_ALL = (p.val("s_ffw0"), p.val("s_ffw1"))
        RFF_RDY = p.val("s_ev")
        C_PE = p.val("s_pe")

    if "wt" in DBG:
        DQW_ALL4 = tuple(p.val(f"s_dqw{k}") for k in range(4))

        def dbg_wt(e, DQW_ALL4=DQW_ALL4):
            for k in range(4):
                if DQW_ALL4[k]:
                    e.wait_ge(S[f"s_dqw{k}"], DQW_ALL4[k])
            e.dma_start(out=dbg_t["d_wtg"], in_=wt_gu.ap()).then_inc(S["s_dbg"], 16)
            e.dma_start(out=dbg_t["d_wtd"], in_=wt_dn.ap()).then_inc(S["s_dbg"], 16)
        p.on("gpsimd", dbg_wt)
        p.bump("s_dbg", 32)

    DQ_NIB_ALL = p.val("s_dqv")

    if "D" in stages:
        den_marks = [0] * 4
        pass2_marks = [0] * 4
        pj_all = [0] * 4
        for tt in range(4):
            pe_guard = p.val("s_pe")
            XN_RDY = emit_xn_jit(512 * tt, pe_guard)

            for g in range(NG):
                wqfree = p.val("s_pe")

                sq = f"s_q{g % 2}"

                def g_wq(e, g=g, tt=tt, wqfree=wqfree, sq=sq):
                    if g + tt > 0:
                        e.wait_ge(S["s_pe"], max(0, wqfree - 1))
                    else:
                        e.wait_ge(S["s_dqv"], DQ_NIB_ALL)
                        e.wait_ge(S["s_pe"], C_PE)
                    e.dma_start(out=wqb[g % 2][:],
                                in_=wq_r.rearrange("(g p) o -> g p o", p=128)[g]
                                ).then_inc(S[sq], 16)
                p.on("gpsimd", g_wq)
                WQ = p.bump(sq, 16)

                evq = p.val("s_evk")

                def t_q(e, g=g, tt=tt, WQ=WQ, XN_RDY=XN_RDY, evq=evq, sq=sq):
                    e.wait_ge(S[sq], WQ)
                    e.wait_ge(S["s_xn"], XN_RDY)
                    e.wait_ge(S["s_xch"], KV_RDY)
                    if g + tt > 0:
                        e.wait_ge(S["s_evk"], max(0, evq - 1))
                    else:
                        e.wait_ge(S["s_ev"], RFF_RDY)
                    pq = pb[g % 2]
                    for ch in range(CH):
                        mm = e.matmul(pq[:], wqb[g % 2][:, 128 * ch:128 * ch + 128], xn[ch][:],
                                      start=(ch == 0), stop=(ch == CH - 1))
                    mm.then_inc(S["s_pe"], 1)
                p.on("tensor", t_q)
                QPE = p.bump("s_pe", 1)

                def a_q(e, g=g, tt=tt, QPE=QPE):
                    e.wait_ge(S["s_pe"], QPE)
                    if tt > 0:
                        e.wait_ge(S["s_pe2"], pass2_marks[tt - 1])
                    e.activation(out=qsb[g][:], in_=pb[g % 2][:], func=AF.Relu).then_inc(S["s_evk"], 1)
                p.on("scalar", a_q)
                QEV = p.bump("s_evk", 1)

                def t_den(e, g=g, tt=tt, QEV=QEV):
                    e.wait_ge(S["s_evk"], QEV)
                    if g == 0 and tt > 0:
                        e.wait_ge(S["s_den"], den_marks[tt - 1])
                    e.matmul(pb[2][0:64, :], denl[:, 64 * g:64 * g + 64], qsb[g][:],
                             start=(g == 0), stop=(g == NG - 1)).then_inc(S["s_pe2"], 1)
                p.on("tensor", t_den)
                p.bump("s_pe2", 1)

            DEN = p.val("s_pe2")

            def a_den(e, tt=tt, DEN=DEN):
                e.wait_ge(S["s_pe2"], DEN)
                if tt > 0:
                    e.wait_ge(S["s_den"], den_marks[tt - 1])
                e.activation(out=dsb[:], in_=pb[2][0:64, :], func=AF.Relu,
                             bias=cst[0:64, 6:7]).then_inc(S["s_den"], 1)
            p.on("scalar", a_den)
            DE1 = p.bump("s_den", 1)

            def v_rcp(e, tt=tt, DE1=DE1):
                e.wait_ge(S["s_den"], DE1)
                if tt > 0:
                    e.wait_ge(S["s_pe2"], pass2_marks[tt - 1])
                e.reciprocal(out=rcp[:], in_=dsb[:])
                e.tensor_copy(out=rcpb[:], in_=rcp[:]).then_inc(S["s_den"], 1)
            p.on("vector", v_rcp)
            DE2 = p.bump("s_den", 1)
            den_marks[tt] = DE2

            for g in range(NG):
                evn = p.val("s_evv")
                atv = p.val("s_at")

                def t_num(e, g=g, tt=tt, DE2=DE2, evn=evn, atv=atv):
                    e.wait_ge(S["s_den"], DE2)
                    if g + tt > 0:
                        e.wait_ge(S["s_evv"], evn)
                        e.wait_ge(S["s_at"], atv)
                    e.matmul(pb[3][:], KVb[:, 128 * g:128 * g + 128], qsb[g][:],
                             start=True, stop=True)
                    e.matmul(pb[4][:], bmap_s[:, 128 * g:128 * g + 128], rcpb[:],
                             start=True, stop=True).then_inc(S["s_pe2"], 1)
                p.on("tensor", t_num)
                NPE = p.bump("s_pe2", 1)

                def a_num(e, g=g, NPE=NPE, atv=atv):
                    e.wait_ge(S["s_pe2"], NPE)
                    if g >= 1:
                        e.wait_ge(S["s_at"], atv)
                    e.activation(out=nsb[:], in_=pb[3][:], func=AF.Copy).then_inc(S["s_evv"], 1)
                p.on("scalar", a_num)
                NEV = p.bump("s_evv", 1)

                def v_att(e, g=g, tt=tt, NEV=NEV):
                    e.wait_ge(S["s_evv"], NEV)
                    if tt > 0 and g == 0:
                        e.wait_ge(S["s_pe"], pj_all[tt - 1])
                    if tt == 0 and g == 0:
                        e.wait_ge(S["s_xch"], KV_RDY)
                    e.tensor_tensor(out=att[g][:], in0=nsb[:], in1=pb[4][:],
                                    op=ALU.mult).then_inc(S["s_at"], 1)
                p.on("vector", v_att)
                p.bump("s_at", 1)

            pass2_marks[tt] = p.val("s_pe2")
            AT_TT = p.val("s_at")

            for Hc in range(CH):
                wqfree = p.val("s_pe")

                sq = f"s_q{Hc % 2}"
                se = f"s_e{Hc % 2}"

                def g_wp(e, Hc=Hc, wqfree=wqfree, sq=sq):
                    e.wait_ge(S["s_pe"], max(0, wqfree - 1))
                    e.dma_start(out=wqb[Hc % 2][:],
                                in_=wp_r.rearrange("(g p) o -> g p o", p=128)[Hc]
                                ).then_inc(S[sq], 16)
                p.on("gpsimd", g_wp)
                WP = p.bump(sq, 16)

                eo = p.val("s_eo")

                def s_ex(e, Hc=Hc, tt=tt, eo=eo, se=se):
                    for k in range(2):
                        if FFW_ALL[k]:
                            e.wait_ge(S[f"s_ffw{k}"], FFW_ALL[k])
                    if 16 * tt + Hc >= 2:
                        e.wait_ge(S["s_eo"], eo - 1)
                    e.dma_start(out=exb[Hc % 2][:],
                                in_=x_kv.rearrange("(c p) t -> p c t", p=128)[:, Hc, 512 * tt:512 * tt + 512]
                                ).then_inc(S[se], 16)
                    e.dma_start(out=fchunk[Hc % 2][:],
                                in_=ffn_d.rearrange("(c p) t -> c p t", p=128)[Hc, :, 512 * tt:512 * tt + 512]
                                ).then_inc(S[se], 16)
                p.on("sync", s_ex)
                EXR = p.bump(se, 32)

                evp = p.val("s_evk")

                def t_pj(e, Hc=Hc, WP=WP, AT_TT=AT_TT, evp=evp, tt=tt, sq=sq):
                    e.wait_ge(S[sq], WP)
                    e.wait_ge(S["s_at"], AT_TT)
                    if 16 * tt + Hc >= 2:
                        e.wait_ge(S["s_evk"], max(0, evp - 1))
                    pp = pb[Hc % 2]
                    for ch in range(CH):
                        mm = e.matmul(pp[:], wqb[Hc % 2][:, 128 * ch:128 * ch + 128],
                                      att[ch][:], start=(ch == 0), stop=(ch == CH - 1))
                    mm.then_inc(S["s_pe"], 1)
                p.on("tensor", t_pj)
                PJ = p.bump("s_pe", 1)

                eod = p.val("s_eo")

                def a_pj(e, Hc=Hc, PJ=PJ, eod=eod):
                    e.wait_ge(S["s_pe"], PJ)
                    if eod:
                        e.wait_ge(S["s_eo"], eod)
                    e.activation(out=tsb[:], in_=pb[Hc % 2][:],
                                 func=AF.Copy).then_inc(S["s_evk"], 1)
                p.on("scalar", a_pj)
                PJE = p.bump("s_evk", 1)

                sout = f"s_out{Hc % 2}"
                oo = p.val(sout)

                def v_e(e, Hc=Hc, tt=tt, PJE=PJE, EXR=EXR, oo=oo, se=se, sout=sout):
                    e.wait_ge(S["s_evk"], PJE)
                    e.wait_ge(S[se], EXR)
                    if oo:
                        e.wait_ge(S[sout], oo)
                    e.scalar_tensor_tensor(out=t2sb[:], in0=tsb[:], scalar=bp_s[:, Hc:Hc + 1],
                                           in1=exb[Hc % 2][:], op0=ALU.add, op1=ALU.add)
                    e.tensor_tensor(out=t3sb[:], in0=fchunk[Hc % 2][:], in1=rffb4[tt][:], op=ALU.mult)
                    e.tensor_tensor(out=osb[Hc % 2][:], in0=t2sb[:], in1=t3sb[:],
                                    op=ALU.add).then_inc(S["s_eo"], 1)
                p.on("vector", v_e)
                p.bump("s_eo", 1)

                EOD2 = p.val("s_eo")

                def s_out_f(e, Hc=Hc, tt=tt, EOD2=EOD2, sout=sout):
                    e.wait_ge(S["s_eo"], EOD2)
                    e.dma_start(out=out.rearrange("(c p) t -> c p t", p=128)[Hc, :, 512 * tt:512 * tt + 512],
                                in_=osb[Hc % 2][:]).then_inc(S[sout], 16)
                p.on("sync", s_out_f)
                p.bump(sout, 16)

            pj_all[tt] = p.val("s_pe")

        OUT_ALL = (p.val("s_out0"), p.val("s_out1"))
        DBG_ALL = p.val("s_dbg")

        def s_fin(e, OUT_ALL=OUT_ALL, DBG_ALL=DBG_ALL):
            e.wait_ge(S["s_out0"], OUT_ALL[0])
            e.wait_ge(S["s_out1"], OUT_ALL[1])
            if DBG_ALL:
                e.wait_ge(S["s_dbg"], DBG_ALL)
        p.on("sync", s_fin)

    def s_retire(e):
        for n in sem_names:
            if n in ("s_poll", "s_poll2", "s_poll3"):
                continue
            v = p.val(n)
            if v:
                e.wait_ge(S[n], v)
    p.on("sync", s_retire)

    p.finalize(sems=list(S.values()))
    nc.all_engine_barrier()
    nc.clear_and_free_semaphores(list(S.values()))
    nc.all_engine_barrier()
    estack.pop_all()
    return nc


def _host_prep(L0_attn_x, L0_attn_w_qkv, L0_attn_w_proj, L0_attn_b_proj,
               L0_ffn_gate_up_packed, L0_ffn_gate_up_scales,
               L0_ffn_down_packed, L0_ffn_down_scales):
    pi = _perm(C)
    pii = _perm(II)

    x = np.asarray(L0_attn_x)
    wqkv = np.asarray(L0_attn_w_qkv)
    wproj = np.asarray(L0_attn_w_proj)
    bp = np.asarray(L0_attn_b_proj)
    gup = np.asarray(L0_ffn_gate_up_packed)
    gus = np.asarray(L0_ffn_gate_up_scales)
    dnp = np.asarray(L0_ffn_down_packed)
    dns = np.asarray(L0_ffn_down_scales)

    wqkvT = np.ascontiguousarray(wqkv.T[pi, :], dtype=np.float32)
    wprojT = np.ascontiguousarray(wproj.T[:, pi], dtype=np.float32)
    bp_dev = np.ascontiguousarray(bp[pi].reshape(16, 128).T, dtype=np.float32)

    def _blocks(src, width):
        n = src.shape[1] // width
        b = src.reshape(16, 128, n * width)
        out = np.empty((n, 128, 16 * width), np.float32)
        for i in range(n):
            out[i] = b[:, :, width * i:width * (i + 1)].transpose(1, 0, 2).reshape(128, 16 * width)
        return out.reshape(n * 128, 16 * width)

    wkv_r = np.ascontiguousarray(_blocks(wqkvT[:, 2048:6144], 256))
    wq_r = np.ascontiguousarray(_blocks(wqkvT[:, 0:2048], 128))
    wp_r = np.ascontiguousarray(_blocks(wprojT, 128))

    rowperm = np.concatenate([pii, II + pii])
    gu_bytes_f = gup[rowperm].view(np.uint8).reshape(2 * II, C // 2).T
    gu_scales_f = gus[:, rowperm]
    dn_bytes_f = dnp[pi].view(np.uint8).reshape(C, II // 2).T
    dn_scales_f = dns[:, pi]
    gu_bytes_h = [np.ascontiguousarray(gu_bytes_f[:, 5632 * s:5632 * (s + 1)]) for s in range(2)]
    gu_scales_h = [np.ascontiguousarray(gu_scales_f[:, 5632 * s:5632 * (s + 1)]) for s in range(2)]
    dn_bytes_h = [np.ascontiguousarray(dn_bytes_f[:, 1024 * s:1024 * (s + 1)]) for s in range(2)]
    dn_scales_h = [np.ascontiguousarray(dn_scales_f[:, 1024 * s:1024 * (s + 1)]) for s in range(2)]
    gu_off = [0, 44 * 16 * 16384]
    dn_off = [0, 8 * ICn * 16384]

    consts = np.zeros((128, 8), np.float32)
    consts[:, 0] = 1e-5
    consts[:, 1] = -0.5
    consts[:, 2] = -2.0
    consts[:, 3] = -6.0
    consts[:, 4] = 2.0
    consts[:, 5] = 1e-6
    consts[:, 6] = 1e-15

    maskbd = np.zeros((128, 128), ml_dtypes.bfloat16)
    for hh in range(4):
        maskbd[32 * hh:32 * hh + 32, 32 * hh:32 * hh + 32] = 1.0
    hm4_a = np.zeros((128, 4), ml_dtypes.bfloat16)
    for hh in range(4):
        hm4_a[32 * hh:32 * hh + 32, hh] = 1.0
    bmap_a = np.zeros((64, NG * 128), ml_dtypes.bfloat16)
    for g in range(NG):
        for hh in range(4):
            bmap_a[4 * g + hh, 128 * g + 32 * hh:128 * g + 32 * hh + 32] = 1.0
    ones_a = np.ones((128, 2), ml_dtypes.bfloat16)
    ones1_a = np.ones((1, 128), np.float32)
    def _sctiles(sc_h, nob, nt):
        tiles = np.empty((nob * nt, 128, 512), np.float16)
        for ob in range(nob):
            for t in range(nt):
                tiles[nt * ob + t] = np.repeat(sc_h[2 * t:2 * t + 2, 512 * ob:512 * ob + 512],
                                               64, axis=0)
        return np.ascontiguousarray(tiles.reshape(nob * nt * 128, 512))

    gu_sct_h = [_sctiles(gu_scales_h[s], 11, 8) for s in range(2)]
    dn_sct_h = [_sctiles(dn_scales_h[s], 2, 22) for s in range(2)]

    shared = dict(wkv_r=wkv_r, wq_r=wq_r, wp_r=wp_r, bproj=bp_dev,
                  consts=consts, maskbd=maskbd, hm4=hm4_a, bmap=bmap_a,
                  onesc=ones_a, ones1f=ones1_a)

    nonce = (int.from_bytes(os.urandom(4), "little") & 0x7FFFFFFF) | 1

    in_maps = []
    tokmaps = []
    for c in range(8):
        b = c // 4
        kh = c % 2
        pidx = (c // 2) % 2
        out_lo = 4096 * kh + 2048 * pidx
        kvr = np.arange(4096 * kh, 4096 * kh + 4096)
        outr = np.arange(out_lo, out_lo + 2048)
        rest = np.setdiff1d(kvr, outr)
        order = np.concatenate([outr, rest])
        xs = np.ascontiguousarray(x[b][order][:, pi].T.astype(np.float32))
        s = c % 2
        m = dict(shared)
        m["x_kv"] = xs
        m["gu_by"] = gu_bytes_h[s]
        m["gu_sct"] = gu_sct_h[s]
        m["dn_by"] = dn_bytes_h[s]
        m["dn_sct"] = dn_sct_h[s]
        m["slots"] = np.array([[s, s ^ 1, gu_off[s], dn_off[s], nonce]], np.int32)
        in_maps.append(m)
        tokmaps.append((b, out_lo))
    return in_maps, tokmaps, pi


def kernel(L0_attn_x, L0_attn_w_qkv, L0_attn_w_proj, L0_attn_b_proj,
           L0_ffn_gate_up_packed, L0_ffn_gate_up_scales,
           L0_ffn_down_packed, L0_ffn_down_scales):
    in_maps, tokmaps, pi = _host_prep(
        L0_attn_x, L0_attn_w_qkv, L0_attn_w_proj, L0_attn_b_proj,
        L0_ffn_gate_up_packed, L0_ffn_gate_up_scales,
        L0_ffn_down_packed, L0_ffn_down_scales)
    nc = build()
    res = run_bass_kernel_spmd(nc, in_maps, core_ids=list(range(8)))
    B, N = 2, 8192
    full = np.empty((B, N, C), np.float32)
    for c in range(8):
        b, lo = tokmaps[c]
        o = res.results[c]["out"]
        full[b, lo:lo + TOUT][:, pi] = o.T
    return full


# version 16
# speedup vs baseline: 1.5569x; 1.5569x over previous
import os
import numpy as np
import ml_dtypes
import concourse.bass as bass
import concourse.mybir as mybir
from concourse.bass_utils import run_bass_kernel_spmd
from concourse.expressions_rust import RuntimeValue

F32 = mybir.dt.float32
BF16 = mybir.dt.bfloat16
F16 = mybir.dt.float16
U8 = mybir.dt.uint8
I32 = mybir.dt.int32
AF = mybir.ActivationFunctionType
ALU = mybir.AluOpType

C = 2048
CH = 16
TKV = 4096
TOUT = 2048
NSL = 8
NTK = 4
NRA = 8
II = 5632
ICn = 44
NJ = 44
NQ = 4
NG = 16
GU_BT = 8
GU_OB = 11
DN_BT = 22
DN_OB = 2
NTILES = DN_OB * DN_BT + GU_OB * GU_BT
MAGIC = 6053

DBG = []


def _perm(n):
    q = np.arange(n).reshape(-1, 256)
    return np.concatenate([q[:, 0::2], q[:, 1::2]], axis=1).reshape(-1)


class Prog:
    def __init__(self, nc):
        self.nc = nc
        self.q = {e: [] for e in ("sync", "tensor", "vector", "scalar", "gpsimd")}
        self.cnt = {}

    def on(self, eng, fn):
        self.q[eng].append(fn)

    def bump(self, sem, n=1):
        self.cnt[sem] = self.cnt.get(sem, 0) + n
        return self.cnt[sem]

    def val(self, sem):
        return self.cnt.get(sem, 0)

    def finalize(self, sems=()):
        if sems:
            nums = sorted(s.num for s in sems)
            rng = range(nums[0], nums[-1] + 1)
            assert len(nums) == len(rng), nums
            self.nc.gpsimd.dma_reset(rng)
            self.nc.gpsimd.sem_clear(rng)
            self.nc.all_engine_barrier()
        with self.nc.Block() as blk:
            @blk.sync
            def _(e):
                for f in self.q["sync"]:
                    f(e)

            @blk.tensor
            def _(e):
                for f in self.q["tensor"]:
                    f(e)

            @blk.vector
            def _(e):
                for f in self.q["vector"]:
                    f(e)

            @blk.scalar
            def _(e):
                for f in self.q["scalar"]:
                    f(e)

            @blk.gpsimd
            def _(e):
                for f in self.q["gpsimd"]:
                    f(e)


def build(stages=("0", "A", "X", "Q", "C", "D")):
    nc = bass.Bass("TRN2", target_bir_lowering=False, debug=False)

    x_kv = nc.dram_tensor("x_kv", [C, TKV], F32, kind="ExternalInput").ap()
    wkv_r = nc.dram_tensor("wkv_r", [16 * 128, 4096], F32, kind="ExternalInput").ap()
    wq_r = nc.dram_tensor("wq_r", [16 * 128, 2048], F32, kind="ExternalInput").ap()
    wp_r = nc.dram_tensor("wp_r", [16 * 128, 2048], F32, kind="ExternalInput").ap()
    bproj = nc.dram_tensor("bproj", [128, 16], F32, kind="ExternalInput").ap()
    gu_by = nc.dram_tensor("gu_by", [1024, 5632], U8, kind="ExternalInput").ap()
    gu_sct = nc.dram_tensor("gu_sct", [88 * 128, 512], F16, kind="ExternalInput").ap()
    dn_by = nc.dram_tensor("dn_by", [2816, 1024], U8, kind="ExternalInput").ap()
    dn_sct = nc.dram_tensor("dn_sct", [44 * 128, 512], F16, kind="ExternalInput").ap()
    consts = nc.dram_tensor("consts", [128, 8], F32, kind="ExternalInput").ap()
    maskbd = nc.dram_tensor("maskbd", [128, 128], BF16, kind="ExternalInput").ap()
    hm4 = nc.dram_tensor("hm4", [128, 4], BF16, kind="ExternalInput").ap()
    bmap = nc.dram_tensor("bmap", [64, NG * 128], BF16, kind="ExternalInput").ap()
    onesc = nc.dram_tensor("onesc", [128, 2], BF16, kind="ExternalInput").ap()
    ones1f = nc.dram_tensor("ones1f", [1, 128], F32, kind="ExternalInput").ap()
    slots = nc.dram_tensor("slots", [1, 5], I32, kind="ExternalInput").ap()

    out = nc.dram_tensor("out", [C, TOUT], F32, kind="ExternalOutput").ap()
    dbg_t = {}
    if "stats" in DBG:
        dbg_t["d_ab"] = nc.dram_tensor("d_ab", [2 * 128, TKV], F32, kind="ExternalOutput").ap()
    if "kv" in DBG:
        dbg_t["d_kv"] = nc.dram_tensor("d_kv", [2 * 128, 2064], F32, kind="ExternalOutput").ap()
    if "kvpre" in DBG:
        dbg_t["d_kvp"] = nc.dram_tensor("d_kvp", [128, 2064], F32, kind="ExternalOutput").ap()
    if "wt" in DBG:
        dbg_t["d_wtg"] = nc.dram_tensor("d_wtg", [88 * 16 * 128, 128], F32, kind="ExternalOutput").ap()
        dbg_t["d_wtd"] = nc.dram_tensor("d_wtd", [16 * ICn * 128, 128], F32, kind="ExternalOutput").ap()
    if "ffn" in DBG:
        dbg_t["d_ffn"] = nc.dram_tensor("d_ffn", [C, TOUT], F32, kind="ExternalOutput").ap()

    wt_gu = nc.dram_tensor("wt_gu", [88 * 16 * 128, 128], BF16, kind="Internal", addr_space="Shared")
    wt_dn = nc.dram_tensor("wt_dn", [16 * ICn * 128, 128], BF16, kind="Internal", addr_space="Shared")
    wl_gu = nc.dram_tensor("wl_gu", [44 * 16 * 128, 128], BF16, kind="Internal")
    wl_dn = nc.dram_tensor("wl_dn", [8 * ICn * 128, 128], BF16, kind="Internal")
    ffn_d = nc.dram_tensor("ffn_d", [C, TOUT], BF16, kind="Internal")
    kvx = nc.dram_tensor("kvx", [2 * 128, 2064], F32, kind="Internal", addr_space="Shared")
    flg = nc.dram_tensor("flg", [6, 128], I32, kind="Internal", addr_space="Shared")

    p = Prog(nc)
    sem_names = ["s_c", "s_x0", "s_x1", "s_e0", "s_e1", "s_sq", "s_acc", "s_pe",
                 "s_ev", "s_evk", "s_evv",
                 "s_pe2", "s_ab", "s_xn", "s_wkv0", "s_wkv1", "s_wkv2", "s_wkv3", "s_kvw", "s_poll", "s_xch",
                 "s_by0", "s_by1", "s_by2", "s_by3", "s_dqp", "s_dqp2", "s_dqa", "s_dqv",
                 "s_dqw0", "s_dqw1", "s_dqw2", "s_dqw3",
                 "s_dqc", "s_sc0", "s_sc1", "s_sc2", "s_sc3", "s_poll2", "s_poll3",
                 "s_gu0", "s_gu1", "s_dn0", "s_dn1", "s_ffw0", "s_ffw1",
                 "s_q0", "s_q1", "s_q2", "s_q3", "s_den", "s_at", "s_eo", "s_out0", "s_out1", "s_dbg"]
    import contextlib
    estack = contextlib.ExitStack()
    S = {n: estack.enter_context(nc.semaphore(n)) for n in sem_names}

    A = nc.alloc_sbuf_tensor

    cst = A("cst", [128, 8], F32)
    mk_s = A("mk_s", [128, 128], BF16)
    hm4_s = A("hm4_s", [128, 4], BF16)
    bmap_s = A("bmap_s", [64, NG * 128], BF16)
    ones_s = A("ones_s", [128, 2], BF16)
    ones1_s = A("ones1_s", [1, 128], F32)
    bp_s = A("bp_s", [128, 16], F32)
    slot_s = A("slot_s", [1, 5], I32)
    flag_s = A("flag_s", [1, 128], I32)

    xbuf = [A(f"xb{i}", [128, 512], F32) for i in range(2)]
    rowb = A("rowb", [128, 1536], F32)
    A_s = A("A_s", [128, TKV], BF16)
    B_s = A("B_s", [128, TKV], BF16)
    xn = [A(f"xn{i}", [128, 512], BF16) for i in range(CH)]
    xjit = A("xjit", [128, 512], F32)

    reg1 = A("reg1", [128, 16384], U8)

    def carve(reg, off, cols, dt):
        bl = {F32: 4, BF16: 2, F16: 2, U8: 1, I32: 4}[dt]
        return reg[:, off:off + cols * bl].bitcast(dt)

    xacc = carve(reg1, 0, TKV, BF16)
    xqacc = carve(reg1, 8192, TKV, BF16)
    wkv = [carve(reg1, 8192 * i, CH * 256, BF16) for i in range(2)]
    gurd = [carve(reg1, 4096 * i, 2048, BF16) for i in range(4)]
    exb = [carve(reg1, 2048 * i, 512, F32) for i in range(2)]
    fchunk = [carve(reg1, 4096 + 1024 * i, 512, BF16) for i in range(2)]
    osb = [carve(reg1, 6144 + 2048 * i, 512, F32) for i in range(2)]
    tsb = carve(reg1, 10240, 512, F32)
    t2sb = carve(reg1, 12288, 512, F32)
    t3sb = carve(reg1, 14336, 512, F32)

    reg2 = A("reg2", [128, 16384], U8)
    xsqt = [carve(reg2, 1024 * i, 512, BF16) for i in range(2)]
    kvK = [carve(reg2, 512 * i, 256, BF16) for i in range(NTK)]
    kvV = [carve(reg2, 2048 + 512 * i, 256, BF16) for i in range(4)]
    KVp = carve(reg2, 4096, 2048, F32)
    ksp = carve(reg2, 12288, 16, F32)
    att = [carve(reg2, 1024 * g, 512, BF16) for g in range(NG)]

    wkv_x = [A(f"wkvx{i}", [128, CH * 256], BF16) for i in range(2)]
    KVm = A("KVm", [128, 2048], F32)
    ksm = A("ksm", [128, 16], F32)
    KVb = A("KVb", [128, 2048], BF16)
    denl = A("denl", [128, NG * 64], BF16)

    reg3 = A("reg3", [128, 26624], U8)
    reg4 = A("reg4", [128, 18432], U8)
    hsb = [carve(reg3, 1024 * i, 512, BF16) if i < 26 else
           carve(reg4, 1024 * (i - 26), 512, BF16) for i in range(ICn)]
    wqb_x = [carve(reg3, 4096 * i, 2048, BF16) for i in range(2)]
    qsb = [carve(reg4, 1024 * g, 512, BF16) for g in range(NG)]

    byb = [A(f"byb{i}", [128, 512], U8) for i in range(4)]
    u8a = [A(f"u8a{i}", [128, 512], U8) for i in range(2)]
    u8b = [A(f"u8b{i}", [128, 512], U8) for i in range(2)]
    u8c = [A(f"u8c{i}", [128, 512], U8) for i in range(2)]
    u8d = [A(f"u8d{i}", [128, 512], U8) for i in range(2)]
    thl = [A(f"thl{i}", [128, 512], BF16) for i in range(2)]
    thh = [A(f"thh{i}", [128, 512], BF16) for i in range(2)]
    sml = [A(f"sml{i}", [128, 512], BF16) for i in range(2)]
    smh = [A(f"smh{i}", [128, 512], BF16) for i in range(2)]
    r1b = [A(f"r1t{i}", [128, 512], BF16) for i in range(2)]
    r2b = [A(f"r2t{i}", [128, 512], BF16) for i in range(2)]
    mt = A("mt", [128, 512], BF16)
    sst = A("sst", [128, 512], BF16)
    sct = [A(f"sct{i}", [128, 512], BF16) for i in range(4)]
    dqw = [A(f"dqw{i}", [128, 512], BF16) for i in range(4)]

    dnreg = A("dnreg", [128, 11264], U8)
    dnrd = [carve(dnreg, 5632 * i, 22 * 128, BF16) for i in range(2)]
    wqb = [carve(dnreg, 4096 * i, 2048, BF16) for i in range(2)]
    dsb = carve(dnreg, 8192, 512, F32)[0:64, :]
    rcp = A("rcp", [64, 512], F32)
    rcpb = A("rcpb", [64, 512], BF16)
    nsb = A("nsb", [128, 512], BF16)

    silb = A("silb", [128, 512], BF16)
    yraw = [A(f"yraw{i}", [128, 512], BF16) for i in range(2)]
    ysq = [A(f"ysq{i}", [128, 512], BF16) for i in range(2)]
    rffb4 = [A(f"rffb{i}", [128, 512], BF16) for i in range(4)]

    pb = [nc.alloc_psum_tensor(f"pb{i}", [128, 512], F32) for i in range(8)]

    try:
        print(f"[kernel build] sbuf bytes remaining: {nc.sbuf_bytes_remaining}")
    except Exception:
        pass

    rowm = rowb[0:1, 0:512]
    rowv = rowb[0:1, 512:1024]
    rowt = rowb[0:1, 1024:1536]

    const_pairs = ((cst[:], consts), (mk_s[:], maskbd), (hm4_s[:], hm4),
                   (bmap_s[:], bmap), (ones_s[:], onesc), (ones1_s[:], ones1f),
                   (bp_s[:], bproj), (slot_s[:], slots))

    def init_consts(e):
        for dst, src in const_pairs:
            e.dma_start(out=dst, in_=src).then_inc(S["s_c"], 16)
    p.on("sync", init_consts)
    C_RDY = p.bump("s_c", len(const_pairs) * 16)

    def g_init(e):
        e.wait_ge(S["s_c"], C_RDY)
    p.on("gpsimd", g_init)

    wregs = {}

    def s_initregs(e):
        e.wait_ge(S["s_c"], C_RDY)
        wregs["gu"] = e.alloc_register("rs_gu")
        wregs["dn"] = e.alloc_register("rs_dn")
        wregs["tmp"] = e.alloc_register("rs_tmp")
        e.reg_load(wregs["gu"], slot_s[0:1, 2:3])
        e.reg_load(wregs["dn"], slot_s[0:1, 3:4])
    p.on("sync", s_initregs)

    if "0" in stages:
        for u in range(128):
            ch, hf = u // 8, u % 8
            buf = xbuf[u % 2]
            aw = p.val("s_acc")

            sx = f"s_x{u % 2}"

            def s_load(e, u=u, ch=ch, hf=hf, buf=buf, aw=aw, sx=sx):
                if u >= 2:
                    e.wait_ge(S["s_acc"], aw - 1)
                e.dma_start(out=buf[:],
                            in_=x_kv.rearrange("(c p) t -> p c t", p=128)[:, ch, 512 * hf:512 * hf + 512]
                            ).then_inc(S[sx], 16)
            p.on("sync", s_load)
            XW = p.bump(sx, 16)

            def a_sq(e, buf=buf, u=u, XW=XW, sx=sx):
                e.wait_ge(S[sx], XW)
                e.activation(out=xsqt[u % 2][:], in_=buf[:], func=AF.Square).then_inc(S["s_sq"], 1)
            p.on("scalar", a_sq)
            SQ = p.bump("s_sq", 1)

            def v_acc(e, buf=buf, ch=ch, hf=hf, u=u, XW=XW, SQ=SQ, sx=sx):
                e.wait_ge(S[sx], XW)
                sl = slice(512 * hf, 512 * hf + 512)
                if ch == 0:
                    e.tensor_copy(out=xacc[:, sl], in_=buf[:])
                else:
                    e.tensor_tensor(out=xacc[:, sl], in0=xacc[:, sl], in1=buf[:], op=ALU.add)
                e.wait_ge(S["s_sq"], SQ)
                if ch == 0:
                    e.tensor_copy(out=xqacc[:, sl], in_=xsqt[u % 2][:]).then_inc(S["s_acc"], 1)
                else:
                    e.tensor_tensor(out=xqacc[:, sl], in0=xqacc[:, sl], in1=xsqt[u % 2][:],
                                    op=ALU.add).then_inc(S["s_acc"], 1)
            p.on("vector", v_acc)
            p.bump("s_acc", 1)

        ACC_ALL = p.val("s_acc")

        for tb in range(8):
            evw = p.val("s_ev")
            abw = p.val("s_ab")

            def t_st(e, tb=tb, evw=evw):
                e.wait_ge(S["s_acc"], ACC_ALL)
                e.wait_ge(S["s_c"], C_RDY)
                if tb >= 1:
                    e.wait_ge(S["s_ev"], evw)
                e.matmul(pb[0][0:1, :], ones_s[:, 0:1], xacc[:, 512 * tb:512 * tb + 512],
                         start=True, stop=True)
                e.matmul(pb[1][0:1, :], ones_s[:, 0:1], xqacc[:, 512 * tb:512 * tb + 512],
                         start=True, stop=True).then_inc(S["s_pe"], 1)
            p.on("tensor", t_st)
            PEW = p.bump("s_pe", 1)

            def a_ev(e, tb=tb, PEW=PEW, abw=abw):
                e.wait_ge(S["s_pe"], PEW)
                if tb >= 1:
                    e.wait_ge(S["s_ab"], abw)
                e.activation(out=rowm, in_=pb[0][0:1, :], func=AF.Copy)
                e.activation(out=rowv, in_=pb[1][0:1, :], func=AF.Copy).then_inc(S["s_ev"], 1)
            p.on("scalar", a_ev)
            EV1 = p.bump("s_ev", 1)

            def v_row(e, EV1=EV1):
                e.wait_ge(S["s_ev"], EV1)
                e.tensor_scalar(out=rowm, in0=rowm, scalar1=1.0 / 2048.0, scalar2=None, op0=ALU.mult)
                e.tensor_scalar(out=rowv, in0=rowv, scalar1=1.0 / 2048.0, scalar2=None, op0=ALU.mult)
                e.tensor_tensor(out=rowt, in0=rowm, in1=rowm, op=ALU.mult)
                e.tensor_tensor(out=rowv, in0=rowv, in1=rowt, op=ALU.subtract).then_inc(S["s_ev"], 1)
            p.on("vector", v_row)
            EV2 = p.bump("s_ev", 1)

            def a_row(e, EV2=EV2):
                e.wait_ge(S["s_ev"], EV2)
                e.activation(out=rowv, in_=rowv, func=AF.Ln, bias=cst[0:1, 0:1])
                e.activation(out=rowv, in_=rowv, func=AF.Exp, scale=cst[0:1, 1:2]).then_inc(S["s_ev"], 1)
            p.on("scalar", a_row)
            EV3 = p.bump("s_ev", 1)

            def v_row2(e, EV3=EV3):
                e.wait_ge(S["s_ev"], EV3)
                e.tensor_tensor(out=rowm, in0=rowm, in1=rowv, op=ALU.mult)
                e.tensor_scalar(out=rowm, in0=rowm, scalar1=-1.0, scalar2=None,
                                op0=ALU.mult).then_inc(S["s_ev"], 1)
            p.on("vector", v_row2)
            EV4 = p.bump("s_ev", 1)

            def t_bc(e, EV4=EV4):
                e.wait_ge(S["s_ev"], EV4)
                e.matmul(pb[2][:], ones1_s[:], rowv, start=True, stop=True)
                e.matmul(pb[3][:], ones1_s[:], rowm, start=True, stop=True).then_inc(S["s_pe"], 1)
            p.on("tensor", t_bc)
            PEB = p.bump("s_pe", 1)

            def a_bc(e, tb=tb, PEB=PEB):
                e.wait_ge(S["s_pe"], PEB)
                e.activation(out=A_s[:, 512 * tb:512 * tb + 512], in_=pb[2][:], func=AF.Copy)
                e.activation(out=B_s[:, 512 * tb:512 * tb + 512], in_=pb[3][:],
                             func=AF.Copy).then_inc(S["s_ab"], 1)
            p.on("scalar", a_bc)
            p.bump("s_ab", 1)

        AB_RDY = p.val("s_ab")
        ST0_PE = p.val("s_pe")

        if "stats" in DBG:
            def dbg_ab(e):
                e.wait_ge(S["s_ab"], AB_RDY)
                e.dma_start(out=dbg_t["d_ab"][0:128, :], in_=A_s[:]).then_inc(S["s_dbg"], 16)
                e.dma_start(out=dbg_t["d_ab"][128:256, :], in_=B_s[:]).then_inc(S["s_dbg"], 16)
            p.on("gpsimd", dbg_ab)
            p.bump("s_dbg", 32)
    else:
        AB_RDY = p.val("s_ab")
        ACC_ALL = p.val("s_acc")
        ST0_PE = p.val("s_pe")

    xn_ctr = [0]

    def emit_xn_jit(c0, pe_guard):
        PARA = globals().get('PARA', 0)
        for ch in range(CH):
            buf = xbuf[ch % 2]
            gidx = xn_ctr[0]
            aw = p.val("s_xn")
            sx = f"s_x{gidx % 2}"

            def s_ld(e, ch=ch, buf=buf, c0=c0, aw=aw, gidx=gidx, PARA=PARA, sx=sx):
                if gidx >= 2:
                    e.wait_ge(S["s_xn"], aw if (PARA & 1) else aw - 1)
                else:
                    e.wait_ge(S["s_acc"], ACC_ALL)
                e.dma_start(out=buf[:],
                            in_=x_kv.rearrange("(c p) t -> p c t", p=128)[:, ch, c0:c0 + 512]
                            ).then_inc(S[sx], 16)
            p.on("sync", s_ld)
            XW = p.bump(sx, 16)

            def v_xn(e, ch=ch, buf=buf, c0=c0, XW=XW, pe_guard=pe_guard, sx=sx):
                e.wait_ge(S[sx], XW)
                e.wait_ge(S["s_ab"], AB_RDY)
                if pe_guard:
                    e.wait_ge(S["s_pe"], pe_guard)
                e.tensor_tensor(out=xjit[:], in0=buf[:], in1=A_s[:, c0:c0 + 512], op=ALU.mult)
                e.tensor_tensor(out=xn[ch][:], in0=xjit[:], in1=B_s[:, c0:c0 + 512],
                                op=ALU.add).then_inc(S["s_xn"], 1)
            p.on("vector", v_xn)
            p.bump("s_xn", 1)
            xn_ctr[0] += 1
        return p.val("s_xn")

    seq = []
    for ob in range(GU_OB):
        for t in range(GU_BT):
            seq.append(("gu", t, ob))
    for obd in range(DN_OB):
        for t in range(DN_BT):
            seq.append(("dn", t, obd))

    dqs = {"idx": 0, "int": [], "casta": [], "nib2": [], "nibm": [], "w": []}
    dqw_gu, dqw_dn = {}, {}

    def _dq_phase1(ti):
        kind, t, ob = seq[ti]
        i = ti % 2
        i4 = ti % 4
        bybuf = byb[i4]
        src_by = dn_by if kind == "dn" else gu_by
        src_sc = dn_sct if kind == "dn" else gu_sct
        k_sc = (22 * ob + t) if kind == "dn" else (8 * ob + t)

        int4 = dqs["int"][ti - 4] if ti >= 4 else 0
        casta2 = dqs["casta"][ti - 2] if ti >= 2 else 0
        nib2_2 = dqs["nib2"][ti - 2] if ti >= 2 else 0
        nib2_4 = dqs["nib2"][ti - 4] if ti >= 4 else 0
        dqa_prev = p.val("s_dqa")

        sby = f"s_by{i4}"

        def s_by_ld(e, t=t, ob=ob, bybuf=bybuf, src_by=src_by, int4=int4, ti=ti, sby=sby):
            if int4:
                e.wait_ge(S["s_dqv"], int4)
            e.dma_start(out=bybuf[:], in_=src_by[128 * t:128 * t + 128, 512 * ob:512 * ob + 512]
                        ).then_inc(S[sby], 16)
        p.on("sync", s_by_ld)
        BY = p.bump(sby, 16)

        ssc = f"s_sc{i4}"

        def g_scl(e, i4=i4, src_sc=src_sc, k_sc=k_sc, nib2_4=nib2_4, ti=ti, ssc=ssc):
            if nib2_4:
                e.wait_ge(S["s_dqv"], nib2_4)
            e.dma_start(out=sct[i4][:], in_=src_sc[128 * k_sc:128 * k_sc + 128, :]
                        ).then_inc(S[ssc], 16)
        p.on("gpsimd", g_scl)
        SCL = p.bump(ssc, 16)

        def v_int(e, i=i, bybuf=bybuf, BY=BY, casta2=casta2, ti=ti, sby=sby):
            e.wait_ge(S[sby], BY)
            if ti >= 2:
                e.wait_ge(S["s_dqp2"], casta2)
            e.tensor_scalar(out=u8a[i][:], in0=bybuf[:], scalar1=7, scalar2=None, op0=ALU.bitwise_and)
            e.tensor_scalar(out=u8b[i][:], in0=bybuf[:], scalar1=4, scalar2=7,
                            op0=ALU.logical_shift_right, op1=ALU.bitwise_and)
            e.tensor_scalar(out=u8c[i][:], in0=bybuf[:], scalar1=8, scalar2=None, op0=ALU.bitwise_and)
            e.tensor_scalar(out=u8d[i][:], in0=bybuf[:], scalar1=7, scalar2=None,
                            op0=ALU.logical_shift_right).then_inc(S["s_dqv"], 1)
        p.on("vector", v_int)
        INTM = p.bump("s_dqv", 1)
        dqs["int"].append(INTM)

        def v_cast(e, i=i, dqa_prev=dqa_prev, ti=ti):
            if dqa_prev:
                e.wait_ge(S["s_dqa"], dqa_prev)
            e.tensor_scalar(out=thl[i][:], in0=u8a[i][:], scalar1=0.5, scalar2=None, op0=ALU.mult)
            e.tensor_scalar(out=thh[i][:], in0=u8b[i][:], scalar1=0.5, scalar2=None,
                            op0=ALU.mult).then_inc(S["s_dqp"], 1)
        p.on("vector", v_cast)
        CASTV = p.bump("s_dqp", 1)

        def a_cast(e, i=i, INTM=INTM, nib2_2=nib2_2, ti=ti):
            e.wait_ge(S["s_dqv"], INTM)
            if ti >= 2:
                e.wait_ge(S["s_dqv"], nib2_2)
            e.activation(out=sml[i][:], in_=u8c[i][:], func=AF.Copy, scale=-0.25)
            e.activation(out=smh[i][:], in_=u8d[i][:], func=AF.Copy,
                         scale=-2.0).then_inc(S["s_dqp2"], 1)
        p.on("scalar", a_cast)
        CASTA = p.bump("s_dqp2", 1)
        dqs["casta"].append(CASTA)
        return dict(ti=ti, kind=kind, t=t, ob=ob, i=i, i4=i4, CASTV=CASTV, SCL=SCL, ssc=ssc)

    def _dq_phase2(ctx):
        ti, kind, t, ob, i = ctx["ti"], ctx["kind"], ctx["t"], ctx["ob"], ctx["i"]
        CASTV, SCL, ssc, i4 = ctx["CASTV"], ctx["SCL"], ctx["ssc"], ctx["i4"]
        for nib in range(2):
            th_t = [thl, thh][nib][i]
            sm_t = [sml, smh][nib][i]
            r1 = r1b[nib]
            r2 = r2b[nib]
            wt = dqw[(2 * ti + nib) % 4]
            prevn = dqs["nibm"][2 * (ti - 1) + nib] if ti >= 1 else 0

            def a_relu(e, th_t=th_t, r1=r1, r2=r2, CASTV=CASTV, prevn=prevn):
                e.wait_ge(S["s_dqp"], CASTV)
                if prevn:
                    e.wait_ge(S["s_dqv"], prevn)
                e.activation(out=r1[:], in_=th_t[:], func=AF.Relu, bias=cst[:, 2:3])
                e.activation(out=r2[:], in_=th_t[:], func=AF.Relu, bias=cst[:, 3:4],
                             scale=cst[:, 4:5]).then_inc(S["s_dqa"], 1)
            p.on("scalar", a_relu)
            DQA = p.bump("s_dqa", 1)

            wm_idx = 2 * ti + nib
            sdqw = f"s_dqw{wm_idx % 4}"
            w_prev = p.val(sdqw) if wm_idx >= 4 else 0

            def v_nib(e, th_t=th_t, sm_t=sm_t, wt=wt, i4=i4, DQA=DQA, SCL=SCL, ssc=ssc,
                      w_prev=w_prev, sdqw=sdqw, r1=r1, r2=r2):
                e.wait_ge(S["s_dqa"], DQA)
                e.wait_ge(S[ssc], SCL)
                if w_prev:
                    e.wait_ge(S[sdqw], w_prev)
                e.scalar_tensor_tensor(out=mt[:], in0=th_t[:], scalar=1.0, in1=r1[:],
                                       op0=ALU.mult, op1=ALU.add)
                e.tensor_tensor(out=mt[:], in0=mt[:], in1=r2[:], op=ALU.add)
                e.scalar_tensor_tensor(out=sst[:], in0=sm_t[:], scalar=1.0, in1=sct[i4][:],
                                       op0=ALU.add, op1=ALU.mult)
                e.tensor_tensor(out=wt[:], in0=mt[:], in1=sst[:], op=ALU.mult).then_inc(S["s_dqv"], 1)
            p.on("vector", v_nib)
            NIB = p.bump("s_dqv", 1)
            dqs["nibm"].append(NIB)
            if nib == 1:
                dqs["nib2"].append(NIB)

            if kind == "gu":
                jj0 = 4 * ob
                hc = 2 * t + nib
                dst = wl_gu.ap().rearrange("(jj p hc) o -> jj p hc o", p=128, hc=16)[jj0:jj0 + 4, :, hc, :]
            else:
                hc0 = 4 * ob
                ic = 2 * t + nib
                dst = wl_dn.ap().rearrange("(Hc p ic) o -> Hc p ic o", p=128, ic=ICn)[hc0:hc0 + 4, :, ic, :]

            def s_wt_w(e, wt=wt, dst=dst, NIB=NIB, sdqw=sdqw):
                e.wait_ge(S["s_dqv"], NIB)
                e.dma_start(out=dst.transpose([1, 0, 2]),
                            in_=wt[:].rearrange("p (q o) -> p q o", q=4)
                            ).then_inc(S[sdqw], 16)
            p.on("sync", s_wt_w)
            p.bump(sdqw, 16)
            dqs["w"].append(tuple(p.val(f"s_dqw{k}") for k in range(4)))

    dq_pend = []

    def emit_dq_tiles(n):
        if "Q" not in stages:
            return
        for _ in range(n):
            ti = dqs["idx"]
            if ti >= len(seq):
                return
            dqs["idx"] += 1
            dq_pend.append(_dq_phase1(ti))
            if len(dq_pend) >= 2:
                _dq_phase2(dq_pend.pop(0))

    def flush_dq():
        while dq_pend:
            _dq_phase2(dq_pend.pop(0))

    gu_chunks = [(0, 3), (3, 6), (6, 9), (9, 11)]
    gu_copied = set()

    def emit_gu_copy(ci):
        if "Q" not in stages or ci in gu_copied:
            return
        gu_copied.add(ci)
        o0, o1 = gu_chunks[ci]
        need4 = dqs["w"][2 * (8 * o1) - 1]
        e0 = (4 * o0) * 16 * 16384
        e1 = (4 * o1) * 16 * 16384

        def s_cp(e, need4=need4, e0=e0, e1=e1):
            for k in range(4):
                if need4[k]:
                    e.wait_ge(S[f"s_dqw{k}"], need4[k])
            e.reg_add(wregs["tmp"], wregs["gu"], e0)
            dst = bass.AP(wt_gu, wregs["tmp"], [[1, e1 - e0]])
            e.dma_start(out=dst, in_=bass.AP(wl_gu, e0, [[1, e1 - e0]])
                        ).then_inc(S["s_dqc"], 16)
        p.on("sync", s_cp)
        p.bump("s_dqc", 16)

    if "A" in stages:
        kv_mm_marks = []
        tkv_pend = []
        for sl in range(NSL):
            pe_guard = p.val("s_pe") if sl > 0 else ST0_PE
            XN_RDY = emit_xn_jit(512 * sl, pe_guard)

            for r in range(NRA):
                for kv in range(2):
                    ridx = (sl * NRA + r) * 2 + kv
                    wbuf = (wkv + wkv_x)[ridx % 4]
                    if ridx >= 4:
                        wfree = kv_mm_marks[ridx - 4]
                    elif ridx < 2:
                        wfree = ST0_PE
                    else:
                        wfree = 0
                    col0 = 2048 + 2048 * kv + 256 * r

                    wfull = kv_mm_marks[ridx - 1] if ridx >= 1 else ST0_PE
                    swkv = f"s_wkv{ridx % 4}"
                    blk = 8 * kv + r

                    def g_w(e, wbuf=wbuf, wfree=wfree, blk=blk, wfull=wfull, swkv=swkv):
                        e.wait_ge(S["s_c"], C_RDY)
                        wf = wfull if (globals().get('PARA', 0) & 8) else wfree
                        if wf:
                            e.wait_ge(S["s_pe"], wf)
                        e.dma_start(out=wbuf[:],
                                    in_=wkv_r.rearrange("(b p) o -> b p o", p=128)[blk]
                                    ).then_inc(S[swkv], 16)
                    p.on("gpsimd", g_w)
                    WKV = p.bump(swkv, 16)
                    pe2_prev = p.val("s_pe2")

                    for tk in range(NTK):
                        evsem = "s_evk" if kv == 0 else "s_evv"
                        evm = p.val(evsem)

                        PARA = globals().get('PARA', 0)

                        def t_mm(e, tk=tk, kv=kv, wbuf=wbuf, WKV=WKV, XN_RDY=XN_RDY,
                                 evm=evm, ridx=ridx, evsem=evsem, PARA=PARA, swkv=swkv):
                            e.wait_ge(S["s_xn"], XN_RDY)
                            e.wait_ge(S[swkv], WKV)
                            if ridx + tk > 0:
                                e.wait_ge(S[evsem], evm if (PARA & 2) else max(0, evm - 1))
                            pq = pb[2 * kv + tk % 2]
                            wv = wbuf[:].rearrange("p (c o) -> p c o", o=256)
                            for ch in range(CH):
                                mm = e.matmul(pq[:, 0:256], xn[ch][:, 128 * tk:128 * tk + 128],
                                              wv[:, ch, :], start=(ch == 0), stop=(ch == CH - 1))
                            mm.then_inc(S["s_pe"], 1)
                        p.on("tensor", t_mm)
                        PEM = p.bump("s_pe", 1)

                        if kv == 0:
                            def a_evk(e, tk=tk, PEM=PEM, pe2_prev=pe2_prev):
                                e.wait_ge(S["s_pe"], PEM)
                                if pe2_prev:
                                    e.wait_ge(S["s_pe2"], pe2_prev)
                                e.activation(out=kvK[tk][:], in_=pb[tk % 2][:, 0:256],
                                             func=AF.Relu).then_inc(S["s_evk"], 1)
                            p.on("scalar", a_evk)
                            p.bump("s_evk", 1)
                        else:
                            def v_evv(e, tk=tk, PEM=PEM, pe2_rs=pe2_prev):
                                e.wait_ge(S["s_pe"], PEM)
                                if pe2_rs:
                                    e.wait_ge(S["s_pe2"], pe2_rs)
                                e.activation(out=kvV[tk][:], in_=pb[2 + tk % 2][:, 0:256],
                                             func=AF.Copy).then_inc(S["s_evv"], 1)
                            p.on("scalar", v_evv)
                            EVV = p.bump("s_evv", 1)
                            evk_need = p.val("s_evk")

                            def t_KV(e, tk=tk, EVV=EVV, evk_need=evk_need):
                                e.wait_ge(S["s_evv"], EVV)
                                e.wait_ge(S["s_evk"], evk_need)
                                kb = kvK[tk]
                                vb = kvV[tk]
                                for gl in range(2):
                                    e.matmul(pb[4][:, 128 * gl:128 * gl + 128],
                                             kb[:, 128 * gl:128 * gl + 128],
                                             vb[:, 128 * gl:128 * gl + 128],
                                             start=(tk == 0 and gl == 0), stop=(tk == NTK - 1))
                                e.matmul(pb[5][:, 0:1], kb[:, 0:128], ones_s[:, 0:1],
                                         start=(tk == 0), stop=(tk == NTK - 1))
                                e.matmul(pb[5][:, 1:2], kb[:, 128:256], ones_s[:, 0:1],
                                         start=False, stop=(tk == NTK - 1)).then_inc(S["s_pe2"], 1)
                            if tkv_pend:
                                p.on("tensor", tkv_pend.pop())
                                p.bump("s_pe2", 1)
                            tkv_pend.append(t_KV)

                    if kv == 1 and tkv_pend:
                        p.on("tensor", tkv_pend.pop())
                        p.bump("s_pe2", 1)
                    kv_mm_marks.append(p.val("s_pe"))

                PE2 = p.val("s_pe2")

                def v_kvadd(e, sl=sl, r=r, PE2=PE2):
                    e.wait_ge(S["s_pe2"], PE2)
                    slc = slice(256 * r, 256 * r + 256)
                    if sl == 0:
                        e.tensor_copy(out=KVm[:, slc], in_=pb[4][:, 0:256])
                        e.tensor_copy(out=ksm[:, 2 * r:2 * r + 2], in_=pb[5][:, 0:2]).then_inc(S["s_ev"], 1)
                    else:
                        e.tensor_tensor(out=KVm[:, slc], in0=KVm[:, slc], in1=pb[4][:, 0:256], op=ALU.add)
                        e.tensor_tensor(out=ksm[:, 2 * r:2 * r + 2], in0=ksm[:, 2 * r:2 * r + 2],
                                        in1=pb[5][:, 0:2], op=ALU.add).then_inc(S["s_ev"], 1)
                p.on("vector", v_kvadd)
                KVEV = p.bump("s_ev", 1)

                def t_guard(e, KVEV=KVEV):
                    e.wait_ge(S["s_ev"], KVEV)
                p.on("tensor", t_guard)

                if not globals().get('NO_ILV'):
                    emit_dq_tiles((2, 1, 2, 1, 2, 1, 1, 1)[r])

            if not globals().get('NO_ILV'):
                if sl == 3:
                    emit_gu_copy(0)
                elif sl == 5:
                    emit_gu_copy(1)
                elif sl == 6:
                    emit_gu_copy(2)

        A_DONE_EV = p.val("s_ev")
        A_PE = p.val("s_pe")
    else:
        A_DONE_EV = p.val("s_ev")
        A_PE = p.val("s_pe")

    if "X" in stages:
        if "kvpre" in DBG:
            def dbg_kvpre(e):
                e.wait_ge(S["s_ev"], A_DONE_EV)
                e.dma_start(out=dbg_t["d_kvp"][:, 0:2048], in_=KVm[:]).then_inc(S["s_dbg"], 16)
                e.dma_start(out=dbg_t["d_kvp"][:, 2048:2064], in_=ksm[:]).then_inc(S["s_dbg"], 16)
            p.on("sync", dbg_kvpre)
            KVPRE = p.bump("s_dbg", 32)

        def s_xw(e):
            e.wait_ge(S["s_ev"], A_DONE_EV)
            with e.register("mo") as mo, e.register("mf") as mf:
                e.reg_load(mo, slot_s[0:1, 0:1])
                e.reg_mul(mf, mo, 128 * 2064)
                e.dma_start(out=bass.AP(kvx, mf, [[2064, 128], [1, 2048]]),
                            in_=KVm[:]).then_inc(S["s_kvw"], 16)
                with e.register("mf2") as mf2:
                    e.reg_add(mf2, mf, 2048)
                    e.dma_start(out=bass.AP(kvx, mf2, [[2064, 128], [1, 16]]),
                                in_=ksm[:]).then_inc(S["s_kvw"], 16)
        p.on("sync", s_xw)
        KVW = p.bump("s_kvw", 32)

        def g_flag(e, KVW=KVW):
            e.wait_ge(S["s_kvw"], KVW)
            with e.register("f_mo") as mo, e.register("f_mf") as mf:
                e.reg_load(mo, slot_s[0:1, 0:1])
                e.reg_mul(mf, mo, 128)
                e.dma_start(out=bass.AP(flg, mf, [[128, 1], [1, 1]]),
                            in_=slot_s[0:1, 4:5]).then_inc(S["s_kvw"], 16)
        p.on("gpsimd", g_flag)
        p.bump("s_kvw", 16)

        def g_spin(e):
            with e.register("po") as po, e.register("pf") as pf, \
                 e.register("t") as t, e.register("f") as f, \
                 e.register("nn") as nn:
                e.reg_load(nn, slot_s[0:1, 4:5])
                e.reg_load(po, slot_s[0:1, 1:2])
                e.reg_mul(pf, po, 128)
                fsrc = bass.AP(flg, pf, [[128, 1], [1, 1]])
                e.reg_mov(t, 16)
                e.reg_mov(f, 0)

                def cond():
                    e.dma_start(out=flag_s[0:1, 0:1], in_=fsrc).then_inc(S["s_poll"], 16)
                    e.wait_ge(S["s_poll"], t)
                    e.reg_add(t, t, 16)
                    e.reg_load(f, flag_s[0:1, 0:1])
                    return RuntimeValue(f) != RuntimeValue(nn)

                with e.While(cond):
                    pass
                with e.register("pk") as pk:
                    e.reg_mul(pk, po, 128 * 2064)
                    e.dma_start(out=KVp[:], in_=bass.AP(kvx, pk, [[2064, 128], [1, 2048]])
                                ).then_inc(S["s_xch"], 16)
                    with e.register("pk2") as pk2:
                        e.reg_add(pk2, pk, 2048)
                        e.dma_start(out=ksp[:], in_=bass.AP(kvx, pk2, [[2064, 128], [1, 16]])
                                    ).then_inc(S["s_xch"], 16)
        p.on("gpsimd", g_spin)
        XCH = p.bump("s_xch", 32)

        def v_kvfull(e, XCH=XCH):
            e.wait_ge(S["s_xch"], XCH)
            if "kvpre" in DBG:
                e.wait_ge(S["s_dbg"], KVPRE)
            e.tensor_tensor(out=KVm[:], in0=KVm[:], in1=KVp[:], op=ALU.add)
            e.tensor_tensor(out=ksm[:], in0=ksm[:], in1=ksp[:], op=ALU.add)
            for g in range(NG):
                e.tensor_tensor(out=KVb[:, 128 * g:128 * g + 128],
                                in0=KVm[:, 128 * g:128 * g + 128], in1=mk_s[:], op=ALU.mult)
            e.memset(denl[:], 0.0)
            last = None
            for g in range(NG):
                for hh in range(4):
                    col = 64 * g + 4 * g + hh
                    last = e.tensor_scalar(out=denl[:, col:col + 1], in0=hm4_s[:, hh:hh + 1],
                                           scalar1=ksm[:, g:g + 1], scalar2=None, op0=ALU.mult)
            last.then_inc(S["s_xch"], 1)
        p.on("vector", v_kvfull)
        KV_RDY = p.bump("s_xch", 1)

        if "kv" in DBG:
            def dbg_kv(e):
                e.wait_ge(S["s_xch"], KV_RDY)
                e.dma_start(out=dbg_t["d_kv"][0:128, 0:2048], in_=KVm[:]).then_inc(S["s_dbg"], 16)
                e.dma_start(out=dbg_t["d_kv"][0:128, 2048:2064], in_=ksm[:]).then_inc(S["s_dbg"], 16)
                e.dma_start(out=dbg_t["d_kv"][128:256, 0:2048], in_=KVb[:]).then_inc(S["s_dbg"], 16)
            p.on("gpsimd", dbg_kv)
            p.bump("s_dbg", 48)
    else:
        KV_RDY = p.val("s_xch")

    if "Q" in stages:
        emit_dq_tiles(88 - dqs["idx"])
        flush_dq()
        for ci in range(4):
            emit_gu_copy(ci)
        DQC_GU = p.val("s_dqc")

        def g_flag_dq(e):
            e.wait_ge(S["s_dqc"], DQC_GU)
            with e.register("fq_mo") as mo, e.register("fq_mf") as mf:
                e.reg_load(mo, slot_s[0:1, 0:1])
                e.reg_mul(mf, mo, 128)
                with e.register("fq_mf2") as mf2:
                    e.reg_add(mf2, mf, 256)
                    e.dma_start(out=bass.AP(flg, mf2, [[128, 1], [1, 1]]),
                                in_=slot_s[0:1, 4:5]).then_inc(S["s_kvw"], 16)
        p.on("gpsimd", g_flag_dq)
        p.bump("s_kvw", 16)

        def g_spin_dq(e):
            with e.register("dq_po") as po, e.register("dq_pf") as pf, \
                 e.register("dq_t") as t, e.register("dq_f") as f, \
                 e.register("dq_nn") as nn:
                e.reg_load(nn, slot_s[0:1, 4:5])
                e.reg_load(po, slot_s[0:1, 1:2])
                e.reg_mul(pf, po, 128)
                with e.register("dq_pf2") as pf2:
                    e.reg_add(pf2, pf, 256)
                    fsrc = bass.AP(flg, pf2, [[128, 1], [1, 1]])
                    e.reg_mov(t, 16)
                    e.reg_mov(f, 0)

                    def cond():
                        e.dma_start(out=flag_s[0:1, 1:2], in_=fsrc).then_inc(S["s_poll2"], 16)
                        e.wait_ge(S["s_poll2"], t)
                        e.reg_add(t, t, 16)
                        e.reg_load(f, flag_s[0:1, 1:2])
                        return RuntimeValue(f) != RuntimeValue(nn)

                    with e.While(cond):
                        pass
                e.sem_inc(S["s_xch"], 1)
        p.on("gpsimd", g_spin_dq)
        DQ_RDY2 = p.bump("s_xch", 1)
    else:
        DQ_RDY2 = p.val("s_xch")

    if "C" in stages:
        for qt in range(NQ):
            pe_guard = p.val("s_pe") if (qt > 0 or "A" in stages) else ST0_PE
            XN_RDY = emit_xn_jit(512 * qt, pe_guard)

            gu_marks = []
            for j in range(NJ):
                sgu = f"s_gu{j % 2}"
                for half, jj in ((0, j), (1, 44 + j)):
                    rb = gurd[2 * (j % 2) + half]
                    need = DQ_RDY2 if qt == 0 and j < 2 else 0
                    if j >= 2:
                        gufree = gu_marks[j - 2]
                    elif qt == 0:
                        gufree = A_PE
                    else:
                        gufree = gu_prev_last

                    def s_rd(e, rb=rb, jj=jj, need=need, gufree=gufree, sgu=sgu):
                        if gufree:
                            e.wait_ge(S["s_pe"], gufree)
                        if need:
                            e.wait_ge(S["s_xch"], need)
                        e.dma_start(out=rb[:],
                                    in_=wt_gu.ap().rearrange("(jj p r) o -> jj p (r o)", jj=88, p=128)[jj]
                                    ).then_inc(S[sgu], 16)
                    p.on("sync", s_rd)
                p.bump(sgu, 32)
                WT = p.val(sgu)
                evg = p.val("s_ev")

                def t_gu(e, j=j, WT=WT, XN_RDY=XN_RDY, evg=evg, sgu=sgu):
                    e.wait_ge(S[sgu], WT)
                    e.wait_ge(S["s_xn"], XN_RDY)
                    e.wait_ge(S["s_ev"], max(0, evg - 2))
                    pg = pb[2 * (j % 2)]
                    pu = pb[2 * (j % 2) + 1]
                    for hc in range(CH):
                        e.matmul(pg[:], gurd[2 * (j % 2)][:, 128 * hc:128 * hc + 128], xn[hc][:],
                                 start=(hc == 0), stop=(hc == CH - 1))
                    for hc in range(CH):
                        mm = e.matmul(pu[:], gurd[2 * (j % 2) + 1][:, 128 * hc:128 * hc + 128],
                                      xn[hc][:], start=(hc == 0), stop=(hc == CH - 1))
                    mm.then_inc(S["s_pe"], 1)
                p.on("tensor", t_gu)
                GU = p.bump("s_pe", 1)
                gu_marks.append(GU)

                def a_sil(e, j=j, GU=GU):
                    e.wait_ge(S["s_pe"], GU)
                    e.activation(out=silb[:], in_=pb[2 * (j % 2)][:], func=AF.Silu).then_inc(S["s_ev"], 1)
                p.on("scalar", a_sil)
                SIL = p.bump("s_ev", 1)

                def v_h(e, j=j, SIL=SIL):
                    e.wait_ge(S["s_ev"], SIL)
                    e.tensor_tensor(out=hsb[j][:], in0=silb[:], in1=pb[2 * (j % 2) + 1][:],
                                    op=ALU.mult).then_inc(S["s_ev"], 1)
                p.on("vector", v_h)
                p.bump("s_ev", 1)

                if qt == 0 and j < 22:
                    emit_dq_tiles(2)

            gu_prev_last = gu_marks[-1]
            H_RDY = p.val("s_ev")

            if qt == 0 and "Q" in stages:
                assert dqs["idx"] == len(seq), dqs["idx"]
                flush_dq()
                for obd in range(2):
                    need4 = dqs["w"][2 * (88 + 22 * (obd + 1)) - 1]
                    e0 = (4 * obd) * ICn * 16384
                    e1 = (4 * (obd + 1)) * ICn * 16384

                    def g_cpd(e, need4=need4, e0=e0, e1=e1, obd=obd):
                        for k in range(4):
                            if need4[k]:
                                e.wait_ge(S[f"s_dqw{k}"], need4[k])
                        with e.register(f"cd_b{obd}") as rb_, e.register(f"cd_t{obd}") as rt_:
                            e.reg_load(rb_, slot_s[0:1, 3:4])
                            e.reg_add(rt_, rb_, e0)
                            dst = bass.AP(wt_dn, rt_, [[1, e1 - e0]])
                            e.dma_start(out=dst, in_=bass.AP(wl_dn, e0, [[1, e1 - e0]])
                                        ).then_inc(S["s_dqc"], 16)
                    p.on("gpsimd", g_cpd)
                    p.bump("s_dqc", 16)
                DQC_DN = p.val("s_dqc")

                def g_flag_dn(e, DQC_DN=DQC_DN):
                    e.wait_ge(S["s_dqc"], DQC_DN)
                    with e.register("fd_mo") as mo, e.register("fd_mf") as mf:
                        e.reg_load(mo, slot_s[0:1, 0:1])
                        e.reg_mul(mf, mo, 128)
                        with e.register("fd_mf2") as mf2:
                            e.reg_add(mf2, mf, 512)
                            e.dma_start(out=bass.AP(flg, mf2, [[128, 1], [1, 1]]),
                                        in_=slot_s[0:1, 4:5]).then_inc(S["s_kvw"], 16)
                p.on("gpsimd", g_flag_dn)
                p.bump("s_kvw", 16)

                def g_spin_dn(e):
                    with e.register("dn_po") as po, e.register("dn_pf") as pf, \
                         e.register("dn_t") as t, e.register("dn_f") as f, \
                         e.register("dn_nn") as nn:
                        e.reg_load(nn, slot_s[0:1, 4:5])
                        e.reg_load(po, slot_s[0:1, 1:2])
                        e.reg_mul(pf, po, 128)
                        with e.register("dn_pf2") as pf2:
                            e.reg_add(pf2, pf, 512)
                            fsrc = bass.AP(flg, pf2, [[128, 1], [1, 1]])
                            e.reg_mov(t, 16)
                            e.reg_mov(f, 0)

                            def cond():
                                e.dma_start(out=flag_s[0:1, 2:3], in_=fsrc).then_inc(S["s_poll3"], 16)
                                e.wait_ge(S["s_poll3"], t)
                                e.reg_add(t, t, 16)
                                e.reg_load(f, flag_s[0:1, 2:3])
                                return RuntimeValue(f) != RuntimeValue(nn)

                            with e.While(cond):
                                pass
                    e.sem_inc(S["s_xch"], 1)
                p.on("gpsimd", g_spin_dn)
                DN_RDY = p.bump("s_xch", 1)
            elif qt == 0:
                DN_RDY = p.val("s_xch")

            dn_h0 = []
            dn_marks = []
            sff_marks = []
            sff_pending = []
            for Hc in range(CH):
                need = DN_RDY if qt == 0 and Hc < 2 else 0
                dn_cnt = [0, 0]
                for hv in range(2):
                    rb = dnrd[hv]
                    if Hc >= 1:
                        dnfree = dn_h0[Hc - 1] if hv == 0 else dn_marks[Hc - 1]
                    else:
                        dnfree = (A_PE if qt == 0 else dn_prev_last)

                    def s_dnl(e, rb=rb, Hc=Hc, hv=hv, need=need, dnfree=dnfree):
                        if dnfree:
                            e.wait_ge(S["s_pe"], dnfree)
                        if need:
                            e.wait_ge(S["s_xch"], need)
                        e.dma_start(out=rb[:],
                                    in_=wt_dn.ap().rearrange("(Hc p r) o -> Hc p (r o)", Hc=16, p=128)
                                    [Hc][:, 22 * 128 * hv:22 * 128 * (hv + 1)]
                                    ).then_inc(S[f"s_dn{hv}"], 16)
                    p.on("sync", s_dnl)
                    dn_cnt[hv] = p.bump(f"s_dn{hv}", 16)
                evd = p.val("s_ev")

                def t_dn_a(e, Hc=Hc, WT=dn_cnt[0], H_RDY=H_RDY, evd=evd):
                    e.wait_ge(S["s_dn0"], WT)
                    e.wait_ge(S["s_ev"], H_RDY)
                    e.wait_ge(S["s_ev"], max(0, evd - 1))
                    pd = pb[4 + Hc % 2]
                    for ic in range(22):
                        mm = e.matmul(pd[:], dnrd[0][:, 128 * ic:128 * ic + 128], hsb[ic][:],
                                      start=(ic == 0), stop=False)
                    mm.then_inc(S["s_pe"], 1)
                p.on("tensor", t_dn_a)
                dn_h0.append(p.bump("s_pe", 1))

                def t_dn_b(e, Hc=Hc, WT=dn_cnt[1]):
                    e.wait_ge(S["s_dn1"], WT)
                    pd = pb[4 + Hc % 2]
                    for ic in range(22, ICn):
                        mm = e.matmul(pd[:], dnrd[1][:, 128 * (ic - 22):128 * (ic - 22) + 128], hsb[ic][:],
                                      start=False, stop=(ic == ICn - 1))
                    mm.then_inc(S["s_pe"], 1)
                p.on("tensor", t_dn_b)
                DN = p.bump("s_pe", 1)
                dn_marks.append(DN)

                if sff_pending:
                    p.on("tensor", sff_pending.pop())
                    sff_marks.append(p.bump("s_pe2", 1))

                sffw = f"s_ffw{Hc % 2}"
                ffw = p.val(sffw)
                sffp = sff_marks[Hc - 2] if Hc >= 2 else 0

                def a_y(e, Hc=Hc, DN=DN, ffw=ffw, sffp=sffp, sffw=sffw):
                    e.wait_ge(S["s_pe"], DN)
                    if ffw:
                        e.wait_ge(S[sffw], ffw)
                    if Hc >= 2:
                        e.wait_ge(S["s_pe2"], sffp)
                    e.activation(out=yraw[Hc % 2][:], in_=pb[4 + Hc % 2][:], func=AF.Copy)
                    e.activation(out=ysq[Hc % 2][:], in_=yraw[Hc % 2][:], func=AF.Square).then_inc(S["s_ev"], 1)
                p.on("scalar", a_y)
                YEV = p.bump("s_ev", 1)

                def t_sff(e, Hc=Hc, YEV=YEV):
                    e.wait_ge(S["s_ev"], YEV)
                    e.matmul(pb[6][0:1, :], ones_s[:, 0:1], ysq[Hc % 2][:],
                             start=(Hc == 0), stop=(Hc == CH - 1)).then_inc(S["s_pe2"], 1)
                sff_pending.append(t_sff)

                def s_ffw_f(e, Hc=Hc, qt=qt, YEV=YEV, sffw=sffw):
                    e.wait_ge(S["s_ev"], YEV)
                    e.dma_start(out=ffn_d.rearrange("(c p) t -> c p t", p=128)[Hc, :, 512 * qt:512 * qt + 512],
                                in_=yraw[Hc % 2][:]).then_inc(S[sffw], 16)
                p.on("sync", s_ffw_f)
                p.bump(sffw, 16)

            dn_prev_last = dn_marks[-1]
            p.on("tensor", sff_pending.pop())
            sff_marks.append(p.bump("s_pe2", 1))
            SFF = p.val("s_pe2")

            def v_sff(e, SFF=SFF):
                e.wait_ge(S["s_pe2"], SFF)
                e.tensor_scalar(out=rowt, in0=pb[6][0:1, :], scalar1=1.0 / 2048.0,
                                scalar2=None, op0=ALU.mult).then_inc(S["s_ev"], 1)
            p.on("vector", v_sff)
            VS = p.bump("s_ev", 1)

            def a_sff(e, VS=VS):
                e.wait_ge(S["s_ev"], VS)
                e.activation(out=rowt, in_=rowt, func=AF.Ln, bias=cst[0:1, 5:6])
                e.activation(out=rowt, in_=rowt, func=AF.Exp, scale=cst[0:1, 1:2]).then_inc(S["s_ev"], 1)
            p.on("scalar", a_sff)
            RFQ = p.bump("s_ev", 1)

            def t_rbq(e, RFQ=RFQ):
                e.wait_ge(S["s_ev"], RFQ)
                e.matmul(pb[6][:], ones1_s[:], rowt, start=True, stop=True).then_inc(S["s_pe2"], 1)
            p.on("tensor", t_rbq)
            RBQ = p.bump("s_pe2", 1)

            def a_rbq(e, qt=qt, RBQ=RBQ):
                e.wait_ge(S["s_pe2"], RBQ)
                e.activation(out=rffb4[qt][:], in_=pb[6][:], func=AF.Copy).then_inc(S["s_ev"], 1)
            p.on("scalar", a_rbq)
            RFF = p.bump("s_ev", 1)

            def t_g2(e, RFF=RFF):
                e.wait_ge(S["s_ev"], RFF)
            p.on("tensor", t_g2)

        FFW_ALL = (p.val("s_ffw0"), p.val("s_ffw1"))
        RFF_RDY = p.val("s_ev")
        C_PE = p.val("s_pe")
    else:
        FFW_ALL = (p.val("s_ffw0"), p.val("s_ffw1"))
        RFF_RDY = p.val("s_ev")
        C_PE = p.val("s_pe")

    if "wt" in DBG:
        DQW_ALL4 = tuple(p.val(f"s_dqw{k}") for k in range(4))

        def dbg_wt(e, DQW_ALL4=DQW_ALL4):
            for k in range(4):
                if DQW_ALL4[k]:
                    e.wait_ge(S[f"s_dqw{k}"], DQW_ALL4[k])
            e.dma_start(out=dbg_t["d_wtg"], in_=wt_gu.ap()).then_inc(S["s_dbg"], 16)
            e.dma_start(out=dbg_t["d_wtd"], in_=wt_dn.ap()).then_inc(S["s_dbg"], 16)
        p.on("gpsimd", dbg_wt)
        p.bump("s_dbg", 32)

    DQ_NIB_ALL = p.val("s_dqv")

    if "D" in stages:
        q_marks = []
        den_marks = [0] * 4
        pass2_marks = [0] * 4
        pj_all = [0] * 4
        for tt in range(4):
            pe_guard = p.val("s_pe")
            XN_RDY = emit_xn_jit(512 * tt, pe_guard)

            for g in range(NG):
                wk = 32 * tt + g
                wbq = (wqb + wqb_x)[wk % 4]
                wqfree = q_marks[wk - 4] if wk >= 4 else 0

                sq = f"s_q{wk % 4}"

                def g_wq(e, g=g, tt=tt, wqfree=wqfree, sq=sq, wbq=wbq, wk=wk):
                    if wqfree:
                        e.wait_ge(S["s_pe"], wqfree)
                    elif wk < 4:
                        e.wait_ge(S["s_dqv"], DQ_NIB_ALL)
                        e.wait_ge(S["s_pe"], C_PE)
                    e.dma_start(out=wbq[:],
                                in_=wq_r.rearrange("(g p) o -> g p o", p=128)[g]
                                ).then_inc(S[sq], 16)
                p.on("gpsimd", g_wq)
                WQ = p.bump(sq, 16)

                evq = p.val("s_evk")

                def t_q(e, g=g, tt=tt, WQ=WQ, XN_RDY=XN_RDY, evq=evq, sq=sq, wbq=wbq):
                    e.wait_ge(S[sq], WQ)
                    e.wait_ge(S["s_xn"], XN_RDY)
                    e.wait_ge(S["s_xch"], KV_RDY)
                    if g + tt > 0:
                        e.wait_ge(S["s_evk"], max(0, evq - 1))
                    else:
                        e.wait_ge(S["s_ev"], RFF_RDY)
                    pq = pb[g % 2]
                    for ch in range(CH):
                        mm = e.matmul(pq[:], wbq[:, 128 * ch:128 * ch + 128], xn[ch][:],
                                      start=(ch == 0), stop=(ch == CH - 1))
                    mm.then_inc(S["s_pe"], 1)
                p.on("tensor", t_q)
                QPE = p.bump("s_pe", 1)
                q_marks.append(QPE)

                def a_q(e, g=g, tt=tt, QPE=QPE):
                    e.wait_ge(S["s_pe"], QPE)
                    if tt > 0:
                        e.wait_ge(S["s_pe2"], pass2_marks[tt - 1])
                    e.activation(out=qsb[g][:], in_=pb[g % 2][:], func=AF.Relu).then_inc(S["s_evk"], 1)
                p.on("scalar", a_q)
                QEV = p.bump("s_evk", 1)

                def t_den(e, g=g, tt=tt, QEV=QEV):
                    e.wait_ge(S["s_evk"], QEV)
                    if g == 0 and tt > 0:
                        e.wait_ge(S["s_den"], den_marks[tt - 1])
                    e.matmul(pb[2][0:64, :], denl[:, 64 * g:64 * g + 64], qsb[g][:],
                             start=(g == 0), stop=(g == NG - 1)).then_inc(S["s_pe2"], 1)
                p.on("tensor", t_den)
                p.bump("s_pe2", 1)

            DEN = p.val("s_pe2")

            def a_den(e, tt=tt, DEN=DEN):
                e.wait_ge(S["s_pe2"], DEN)
                if tt > 0:
                    e.wait_ge(S["s_den"], den_marks[tt - 1])
                e.activation(out=dsb[:], in_=pb[2][0:64, :], func=AF.Relu,
                             bias=cst[0:64, 6:7]).then_inc(S["s_den"], 1)
            p.on("scalar", a_den)
            DE1 = p.bump("s_den", 1)

            def v_rcp(e, tt=tt, DE1=DE1):
                e.wait_ge(S["s_den"], DE1)
                if tt > 0:
                    e.wait_ge(S["s_pe2"], pass2_marks[tt - 1])
                e.reciprocal(out=rcp[:], in_=dsb[:])
                e.tensor_copy(out=rcpb[:], in_=rcp[:]).then_inc(S["s_den"], 1)
            p.on("vector", v_rcp)
            DE2 = p.bump("s_den", 1)
            den_marks[tt] = DE2

            for g in range(NG):
                evn = p.val("s_evv")
                atv = p.val("s_at")

                def t_num(e, g=g, tt=tt, DE2=DE2, evn=evn, atv=atv):
                    e.wait_ge(S["s_den"], DE2)
                    if g + tt > 0:
                        e.wait_ge(S["s_evv"], evn)
                        e.wait_ge(S["s_at"], atv)
                    e.matmul(pb[3][:], KVb[:, 128 * g:128 * g + 128], qsb[g][:],
                             start=True, stop=True)
                    e.matmul(pb[4][:], bmap_s[:, 128 * g:128 * g + 128], rcpb[:],
                             start=True, stop=True).then_inc(S["s_pe2"], 1)
                p.on("tensor", t_num)
                NPE = p.bump("s_pe2", 1)

                def a_num(e, g=g, NPE=NPE, atv=atv):
                    e.wait_ge(S["s_pe2"], NPE)
                    if g >= 1:
                        e.wait_ge(S["s_at"], atv)
                    e.activation(out=nsb[:], in_=pb[3][:], func=AF.Copy).then_inc(S["s_evv"], 1)
                p.on("scalar", a_num)
                NEV = p.bump("s_evv", 1)

                def v_att(e, g=g, tt=tt, NEV=NEV):
                    e.wait_ge(S["s_evv"], NEV)
                    if tt > 0 and g == 0:
                        e.wait_ge(S["s_pe"], pj_all[tt - 1])
                    if tt == 0 and g == 0:
                        e.wait_ge(S["s_xch"], KV_RDY)
                    e.tensor_tensor(out=att[g][:], in0=nsb[:], in1=pb[4][:],
                                    op=ALU.mult).then_inc(S["s_at"], 1)
                p.on("vector", v_att)
                p.bump("s_at", 1)

            pass2_marks[tt] = p.val("s_pe2")
            AT_TT = p.val("s_at")

            for Hc in range(CH):
                wk = 32 * tt + 16 + Hc
                wbq = (wqb + wqb_x)[wk % 4]
                wqfree = q_marks[wk - 4]

                sq = f"s_q{wk % 4}"
                se = f"s_e{Hc % 2}"

                def g_wp(e, Hc=Hc, wqfree=wqfree, sq=sq, wbq=wbq):
                    e.wait_ge(S["s_pe"], wqfree)
                    e.dma_start(out=wbq[:],
                                in_=wp_r.rearrange("(g p) o -> g p o", p=128)[Hc]
                                ).then_inc(S[sq], 16)
                p.on("gpsimd", g_wp)
                WP = p.bump(sq, 16)

                eo = p.val("s_eo")

                def s_ex(e, Hc=Hc, tt=tt, eo=eo, se=se):
                    for k in range(2):
                        if FFW_ALL[k]:
                            e.wait_ge(S[f"s_ffw{k}"], FFW_ALL[k])
                    if 16 * tt + Hc >= 2:
                        e.wait_ge(S["s_eo"], eo - 1)
                    e.dma_start(out=exb[Hc % 2][:],
                                in_=x_kv.rearrange("(c p) t -> p c t", p=128)[:, Hc, 512 * tt:512 * tt + 512]
                                ).then_inc(S[se], 16)
                    e.dma_start(out=fchunk[Hc % 2][:],
                                in_=ffn_d.rearrange("(c p) t -> c p t", p=128)[Hc, :, 512 * tt:512 * tt + 512]
                                ).then_inc(S[se], 16)
                p.on("sync", s_ex)
                EXR = p.bump(se, 32)

                evp = p.val("s_evk")

                def t_pj(e, Hc=Hc, WP=WP, AT_TT=AT_TT, evp=evp, tt=tt, sq=sq, wbq=wbq):
                    e.wait_ge(S[sq], WP)
                    e.wait_ge(S["s_at"], AT_TT)
                    if 16 * tt + Hc >= 2:
                        e.wait_ge(S["s_evk"], max(0, evp - 1))
                    pp = pb[Hc % 2]
                    for ch in range(CH):
                        mm = e.matmul(pp[:], wbq[:, 128 * ch:128 * ch + 128],
                                      att[ch][:], start=(ch == 0), stop=(ch == CH - 1))
                    mm.then_inc(S["s_pe"], 1)
                p.on("tensor", t_pj)
                PJ = p.bump("s_pe", 1)
                q_marks.append(PJ)

                eod = p.val("s_eo")

                def a_pj(e, Hc=Hc, PJ=PJ, eod=eod):
                    e.wait_ge(S["s_pe"], PJ)
                    if eod:
                        e.wait_ge(S["s_eo"], eod)
                    e.activation(out=tsb[:], in_=pb[Hc % 2][:],
                                 func=AF.Copy).then_inc(S["s_evk"], 1)
                p.on("scalar", a_pj)
                PJE = p.bump("s_evk", 1)

                sout = f"s_out{Hc % 2}"
                oo = p.val(sout)

                def v_e(e, Hc=Hc, tt=tt, PJE=PJE, EXR=EXR, oo=oo, se=se, sout=sout):
                    e.wait_ge(S["s_evk"], PJE)
                    e.wait_ge(S[se], EXR)
                    if oo:
                        e.wait_ge(S[sout], oo)
                    e.scalar_tensor_tensor(out=t2sb[:], in0=tsb[:], scalar=bp_s[:, Hc:Hc + 1],
                                           in1=exb[Hc % 2][:], op0=ALU.add, op1=ALU.add)
                    e.tensor_tensor(out=t3sb[:], in0=fchunk[Hc % 2][:], in1=rffb4[tt][:], op=ALU.mult)
                    e.tensor_tensor(out=osb[Hc % 2][:], in0=t2sb[:], in1=t3sb[:],
                                    op=ALU.add).then_inc(S["s_eo"], 1)
                p.on("vector", v_e)
                p.bump("s_eo", 1)

                EOD2 = p.val("s_eo")

                def s_out_f(e, Hc=Hc, tt=tt, EOD2=EOD2, sout=sout):
                    e.wait_ge(S["s_eo"], EOD2)
                    e.dma_start(out=out.rearrange("(c p) t -> c p t", p=128)[Hc, :, 512 * tt:512 * tt + 512],
                                in_=osb[Hc % 2][:]).then_inc(S[sout], 16)
                p.on("sync", s_out_f)
                p.bump(sout, 16)

            pj_all[tt] = p.val("s_pe")

        OUT_ALL = (p.val("s_out0"), p.val("s_out1"))
        DBG_ALL = p.val("s_dbg")

        def s_fin(e, OUT_ALL=OUT_ALL, DBG_ALL=DBG_ALL):
            e.wait_ge(S["s_out0"], OUT_ALL[0])
            e.wait_ge(S["s_out1"], OUT_ALL[1])
            if DBG_ALL:
                e.wait_ge(S["s_dbg"], DBG_ALL)
        p.on("sync", s_fin)

    def s_retire(e):
        for n in sem_names:
            if n in ("s_poll", "s_poll2", "s_poll3"):
                continue
            v = p.val(n)
            if v:
                e.wait_ge(S[n], v)
    p.on("sync", s_retire)

    p.finalize(sems=list(S.values()))
    nc.all_engine_barrier()
    nc.clear_and_free_semaphores(list(S.values()))
    nc.all_engine_barrier()
    estack.pop_all()
    return nc


def _host_prep(L0_attn_x, L0_attn_w_qkv, L0_attn_w_proj, L0_attn_b_proj,
               L0_ffn_gate_up_packed, L0_ffn_gate_up_scales,
               L0_ffn_down_packed, L0_ffn_down_scales):
    pi = _perm(C)
    pii = _perm(II)

    x = np.asarray(L0_attn_x)
    wqkv = np.asarray(L0_attn_w_qkv)
    wproj = np.asarray(L0_attn_w_proj)
    bp = np.asarray(L0_attn_b_proj)
    gup = np.asarray(L0_ffn_gate_up_packed)
    gus = np.asarray(L0_ffn_gate_up_scales)
    dnp = np.asarray(L0_ffn_down_packed)
    dns = np.asarray(L0_ffn_down_scales)

    wqkvT = np.ascontiguousarray(wqkv.T[pi, :], dtype=np.float32)
    wprojT = np.ascontiguousarray(wproj.T[:, pi], dtype=np.float32)
    bp_dev = np.ascontiguousarray(bp[pi].reshape(16, 128).T, dtype=np.float32)

    def _blocks(src, width):
        n = src.shape[1] // width
        b = src.reshape(16, 128, n * width)
        out = np.empty((n, 128, 16 * width), np.float32)
        for i in range(n):
            out[i] = b[:, :, width * i:width * (i + 1)].transpose(1, 0, 2).reshape(128, 16 * width)
        return out.reshape(n * 128, 16 * width)

    wkv_r = np.ascontiguousarray(_blocks(wqkvT[:, 2048:6144], 256))
    wq_r = np.ascontiguousarray(_blocks(wqkvT[:, 0:2048], 128))
    wp_r = np.ascontiguousarray(_blocks(wprojT, 128))

    rowperm = np.concatenate([pii, II + pii])
    gu_bytes_f = gup[rowperm].view(np.uint8).reshape(2 * II, C // 2).T
    gu_scales_f = gus[:, rowperm]
    dn_bytes_f = dnp[pi].view(np.uint8).reshape(C, II // 2).T
    dn_scales_f = dns[:, pi]
    gu_bytes_h = [np.ascontiguousarray(gu_bytes_f[:, 5632 * s:5632 * (s + 1)]) for s in range(2)]
    gu_scales_h = [np.ascontiguousarray(gu_scales_f[:, 5632 * s:5632 * (s + 1)]) for s in range(2)]
    dn_bytes_h = [np.ascontiguousarray(dn_bytes_f[:, 1024 * s:1024 * (s + 1)]) for s in range(2)]
    dn_scales_h = [np.ascontiguousarray(dn_scales_f[:, 1024 * s:1024 * (s + 1)]) for s in range(2)]
    gu_off = [0, 44 * 16 * 16384]
    dn_off = [0, 8 * ICn * 16384]

    consts = np.zeros((128, 8), np.float32)
    consts[:, 0] = 1e-5
    consts[:, 1] = -0.5
    consts[:, 2] = -2.0
    consts[:, 3] = -6.0
    consts[:, 4] = 2.0
    consts[:, 5] = 1e-6
    consts[:, 6] = 1e-15

    maskbd = np.zeros((128, 128), ml_dtypes.bfloat16)
    for hh in range(4):
        maskbd[32 * hh:32 * hh + 32, 32 * hh:32 * hh + 32] = 1.0
    hm4_a = np.zeros((128, 4), ml_dtypes.bfloat16)
    for hh in range(4):
        hm4_a[32 * hh:32 * hh + 32, hh] = 1.0
    bmap_a = np.zeros((64, NG * 128), ml_dtypes.bfloat16)
    for g in range(NG):
        for hh in range(4):
            bmap_a[4 * g + hh, 128 * g + 32 * hh:128 * g + 32 * hh + 32] = 1.0
    ones_a = np.ones((128, 2), ml_dtypes.bfloat16)
    ones1_a = np.ones((1, 128), np.float32)
    def _sctiles(sc_h, nob, nt):
        tiles = np.empty((nob * nt, 128, 512), np.float16)
        for ob in range(nob):
            for t in range(nt):
                tiles[nt * ob + t] = np.repeat(sc_h[2 * t:2 * t + 2, 512 * ob:512 * ob + 512],
                                               64, axis=0)
        return np.ascontiguousarray(tiles.reshape(nob * nt * 128, 512))

    gu_sct_h = [_sctiles(gu_scales_h[s], 11, 8) for s in range(2)]
    dn_sct_h = [_sctiles(dn_scales_h[s], 2, 22) for s in range(2)]

    shared = dict(wkv_r=wkv_r, wq_r=wq_r, wp_r=wp_r, bproj=bp_dev,
                  consts=consts, maskbd=maskbd, hm4=hm4_a, bmap=bmap_a,
                  onesc=ones_a, ones1f=ones1_a)

    nonce = (int.from_bytes(os.urandom(4), "little") & 0x7FFFFFFF) | 1

    in_maps = []
    tokmaps = []
    for c in range(8):
        b = c // 4
        kh = c % 2
        pidx = (c // 2) % 2
        out_lo = 4096 * kh + 2048 * pidx
        kvr = np.arange(4096 * kh, 4096 * kh + 4096)
        outr = np.arange(out_lo, out_lo + 2048)
        rest = np.setdiff1d(kvr, outr)
        order = np.concatenate([outr, rest])
        xs = np.ascontiguousarray(x[b][order][:, pi].T.astype(np.float32))
        s = c % 2
        m = dict(shared)
        m["x_kv"] = xs
        m["gu_by"] = gu_bytes_h[s]
        m["gu_sct"] = gu_sct_h[s]
        m["dn_by"] = dn_bytes_h[s]
        m["dn_sct"] = dn_sct_h[s]
        m["slots"] = np.array([[s, s ^ 1, gu_off[s], dn_off[s], nonce]], np.int32)
        in_maps.append(m)
        tokmaps.append((b, out_lo))
    return in_maps, tokmaps, pi


def kernel(L0_attn_x, L0_attn_w_qkv, L0_attn_w_proj, L0_attn_b_proj,
           L0_ffn_gate_up_packed, L0_ffn_gate_up_scales,
           L0_ffn_down_packed, L0_ffn_down_scales):
    in_maps, tokmaps, pi = _host_prep(
        L0_attn_x, L0_attn_w_qkv, L0_attn_w_proj, L0_attn_b_proj,
        L0_ffn_gate_up_packed, L0_ffn_gate_up_scales,
        L0_ffn_down_packed, L0_ffn_down_scales)
    nc = build()
    res = run_bass_kernel_spmd(nc, in_maps, core_ids=list(range(8)))
    B, N = 2, 8192
    full = np.empty((B, N, C), np.float32)
    for c in range(8):
        b, lo = tokmaps[c]
        o = res.results[c]["out"]
        full[b, lo:lo + TOUT][:, pi] = o.T
    return full


# version 17
# speedup vs baseline: 1.5900x; 1.0213x over previous
import os
import numpy as np
import ml_dtypes
import concourse.bass as bass
import concourse.mybir as mybir
from concourse.bass_utils import run_bass_kernel_spmd
from concourse.expressions_rust import RuntimeValue

F32 = mybir.dt.float32
BF16 = mybir.dt.bfloat16
F16 = mybir.dt.float16
U8 = mybir.dt.uint8
I32 = mybir.dt.int32
AF = mybir.ActivationFunctionType
ALU = mybir.AluOpType

C = 2048
CH = 16
TKV = 4096
TOUT = 2048
NSL = 8
NTK = 4
NRA = 8
II = 5632
ICn = 44
NJ = 44
NQ = 4
NG = 16
GU_BT = 8
GU_OB = 11
DN_BT = 22
DN_OB = 2
NTILES = DN_OB * DN_BT + GU_OB * GU_BT
MAGIC = 6053

DBG = []


def _perm(n):
    q = np.arange(n).reshape(-1, 256)
    return np.concatenate([q[:, 0::2], q[:, 1::2]], axis=1).reshape(-1)


class Prog:
    def __init__(self, nc):
        self.nc = nc
        self.q = {e: [] for e in ("sync", "tensor", "vector", "scalar", "gpsimd")}
        self.cnt = {}

    def on(self, eng, fn):
        self.q[eng].append(fn)

    def bump(self, sem, n=1):
        self.cnt[sem] = self.cnt.get(sem, 0) + n
        return self.cnt[sem]

    def val(self, sem):
        return self.cnt.get(sem, 0)

    def finalize(self, sems=()):
        if sems:
            nums = sorted(s.num for s in sems)
            rng = range(nums[0], nums[-1] + 1)
            assert len(nums) == len(rng), nums
            self.nc.gpsimd.dma_reset(rng)
            self.nc.gpsimd.sem_clear(rng)
            self.nc.all_engine_barrier()
        with self.nc.Block() as blk:
            @blk.sync
            def _(e):
                for f in self.q["sync"]:
                    f(e)

            @blk.tensor
            def _(e):
                for f in self.q["tensor"]:
                    f(e)

            @blk.vector
            def _(e):
                for f in self.q["vector"]:
                    f(e)

            @blk.scalar
            def _(e):
                for f in self.q["scalar"]:
                    f(e)

            @blk.gpsimd
            def _(e):
                for f in self.q["gpsimd"]:
                    f(e)


def build(stages=("0", "A", "X", "Q", "C", "D")):
    nc = bass.Bass("TRN2", target_bir_lowering=False, debug=False)

    x_kv = nc.dram_tensor("x_kv", [C, TKV], F32, kind="ExternalInput").ap()
    wkv_r = nc.dram_tensor("wkv_r", [16 * 128, 4096], F32, kind="ExternalInput").ap()
    wq_r = nc.dram_tensor("wq_r", [16 * 128, 2048], F32, kind="ExternalInput").ap()
    wp_r = nc.dram_tensor("wp_r", [16 * 128, 2048], F32, kind="ExternalInput").ap()
    bproj = nc.dram_tensor("bproj", [128, 16], F32, kind="ExternalInput").ap()
    gu_by = nc.dram_tensor("gu_by", [1024, 5632], U8, kind="ExternalInput").ap()
    gu_sct = nc.dram_tensor("gu_sct", [88 * 128, 512], F16, kind="ExternalInput").ap()
    dn_by = nc.dram_tensor("dn_by", [2816, 1024], U8, kind="ExternalInput").ap()
    dn_sct = nc.dram_tensor("dn_sct", [44 * 128, 512], F16, kind="ExternalInput").ap()
    consts = nc.dram_tensor("consts", [128, 8], F32, kind="ExternalInput").ap()
    maskbd = nc.dram_tensor("maskbd", [128, 128], BF16, kind="ExternalInput").ap()
    hm4 = nc.dram_tensor("hm4", [128, 4], BF16, kind="ExternalInput").ap()
    bmap = nc.dram_tensor("bmap", [64, NG * 128], BF16, kind="ExternalInput").ap()
    onesc = nc.dram_tensor("onesc", [128, 2], BF16, kind="ExternalInput").ap()
    ones1f = nc.dram_tensor("ones1f", [1, 128], F32, kind="ExternalInput").ap()
    slots = nc.dram_tensor("slots", [1, 5], I32, kind="ExternalInput").ap()

    out = nc.dram_tensor("out", [C, TOUT], F32, kind="ExternalOutput").ap()
    dbg_t = {}
    if "stats" in DBG:
        dbg_t["d_ab"] = nc.dram_tensor("d_ab", [2 * 128, TKV], F32, kind="ExternalOutput").ap()
    if "kv" in DBG:
        dbg_t["d_kv"] = nc.dram_tensor("d_kv", [2 * 128, 2064], F32, kind="ExternalOutput").ap()
    if "kvpre" in DBG:
        dbg_t["d_kvp"] = nc.dram_tensor("d_kvp", [128, 2064], F32, kind="ExternalOutput").ap()
    if "wt" in DBG:
        dbg_t["d_wtg"] = nc.dram_tensor("d_wtg", [88 * 16 * 128, 128], F32, kind="ExternalOutput").ap()
        dbg_t["d_wtd"] = nc.dram_tensor("d_wtd", [16 * ICn * 128, 128], F32, kind="ExternalOutput").ap()
    if "ffn" in DBG:
        dbg_t["d_ffn"] = nc.dram_tensor("d_ffn", [C, TOUT], F32, kind="ExternalOutput").ap()

    wt_gu = nc.dram_tensor("wt_gu", [88 * 16 * 128, 128], BF16, kind="Internal", addr_space="Shared")
    wt_dn = nc.dram_tensor("wt_dn", [16 * ICn * 128, 128], BF16, kind="Internal", addr_space="Shared")
    wl_gu = nc.dram_tensor("wl_gu", [44 * 16 * 128, 128], BF16, kind="Internal")
    wl_dn = nc.dram_tensor("wl_dn", [8 * ICn * 128, 128], BF16, kind="Internal")
    ffn_d = nc.dram_tensor("ffn_d", [C, TOUT], BF16, kind="Internal")
    kvx = nc.dram_tensor("kvx", [2 * 128, 2064], F32, kind="Internal", addr_space="Shared")
    flg = nc.dram_tensor("flg", [6, 128], I32, kind="Internal", addr_space="Shared")

    p = Prog(nc)
    sem_names = ["s_c", "s_x0", "s_x1", "s_e0", "s_e1", "s_sq", "s_acc", "s_pe",
                 "s_ev", "s_evk", "s_evv",
                 "s_pe2", "s_ab", "s_xn", "s_wkv0", "s_wkv1", "s_wkv2", "s_wkv3", "s_kvw", "s_poll", "s_xch",
                 "s_by0", "s_by1", "s_by2", "s_by3", "s_dqp", "s_dqp2", "s_dqa", "s_dqv",
                 "s_dqw0", "s_dqw1", "s_dqw2", "s_dqw3",
                 "s_dqc", "s_sc0", "s_sc1", "s_sc2", "s_sc3", "s_poll2", "s_poll3",
                 "s_gu0", "s_gu1", "s_dn0", "s_dn1", "s_ffw0", "s_ffw1",
                 "s_q0", "s_q1", "s_q2", "s_q3", "s_den", "s_at", "s_eo", "s_out0", "s_out1", "s_dbg"]
    import contextlib
    estack = contextlib.ExitStack()
    S = {n: estack.enter_context(nc.semaphore(n)) for n in sem_names}

    A = nc.alloc_sbuf_tensor

    cst = A("cst", [128, 8], F32)
    mk_s = A("mk_s", [128, 128], BF16)
    hm4_s = A("hm4_s", [128, 4], BF16)
    bmap_s = A("bmap_s", [64, NG * 128], BF16)
    ones_s = A("ones_s", [128, 2], BF16)
    ones1_s = A("ones1_s", [1, 128], F32)
    bp_s = A("bp_s", [128, 16], F32)
    slot_s = A("slot_s", [1, 5], I32)
    flag_s = A("flag_s", [1, 128], I32)

    xbuf = [A(f"xb{i}", [128, 512], F32) for i in range(2)]
    rowb = A("rowb", [128, 1536], F32)
    A_s = A("A_s", [128, TKV], BF16)
    B_s = A("B_s", [128, TKV], BF16)
    xn = [A(f"xn{i}", [128, 512], BF16) for i in range(CH)]
    xjit = A("xjit", [128, 512], F32)

    reg1 = A("reg1", [128, 16384], U8)

    def carve(reg, off, cols, dt):
        bl = {F32: 4, BF16: 2, F16: 2, U8: 1, I32: 4}[dt]
        return reg[:, off:off + cols * bl].bitcast(dt)

    xacc = carve(reg1, 0, TKV, BF16)
    xqacc = carve(reg1, 8192, TKV, BF16)
    wkv = [carve(reg1, 8192 * i, CH * 256, BF16) for i in range(2)]
    gurd = [carve(reg1, 4096 * i, 2048, BF16) for i in range(4)]
    exb = [carve(reg1, 2048 * i, 512, F32) for i in range(2)]
    fchunk = [carve(reg1, 4096 + 1024 * i, 512, BF16) for i in range(2)]
    osb = [carve(reg1, 6144 + 2048 * i, 512, F32) for i in range(2)]
    tsb = carve(reg1, 10240, 512, F32)
    t2sb = carve(reg1, 12288, 512, F32)
    t3sb = carve(reg1, 14336, 512, F32)

    reg2 = A("reg2", [128, 16384], U8)
    xsqt = [carve(reg2, 1024 * i, 512, BF16) for i in range(2)]
    kvK = [carve(reg2, 512 * i, 256, BF16) for i in range(NTK)]
    kvV = [carve(reg2, 2048 + 512 * i, 256, BF16) for i in range(4)]
    KVp = carve(reg2, 4096, 2048, F32)
    ksp = carve(reg2, 12288, 16, F32)
    att = [carve(reg2, 1024 * g, 512, BF16) for g in range(NG)]

    wkv_x = [A(f"wkvx{i}", [128, CH * 256], BF16) for i in range(2)]
    KVm = A("KVm", [128, 2048], F32)
    ksm = A("ksm", [128, 16], F32)
    KVb = A("KVb", [128, 2048], BF16)
    denl = A("denl", [128, NG * 64], BF16)

    reg3 = A("reg3", [128, 26624], U8)
    reg4 = A("reg4", [128, 18432], U8)
    hsb = [carve(reg3, 1024 * i, 512, BF16) if i < 26 else
           carve(reg4, 1024 * (i - 26), 512, BF16) for i in range(ICn)]
    wqb_x = [carve(reg3, 4096 * i, 2048, BF16) for i in range(2)]
    qsb = [carve(reg4, 1024 * g, 512, BF16) for g in range(NG)]

    byb = [A(f"byb{i}", [128, 512], U8) for i in range(4)]
    u8a = [A(f"u8a{i}", [128, 512], U8) for i in range(2)]
    u8b = [A(f"u8b{i}", [128, 512], U8) for i in range(2)]
    u8c = [A(f"u8c{i}", [128, 512], U8) for i in range(2)]
    u8d = [A(f"u8d{i}", [128, 512], U8) for i in range(2)]
    thl = [A(f"thl{i}", [128, 512], BF16) for i in range(2)]
    thh = [A(f"thh{i}", [128, 512], BF16) for i in range(2)]
    sml = [A(f"sml{i}", [128, 512], BF16) for i in range(2)]
    smh = [A(f"smh{i}", [128, 512], BF16) for i in range(2)]
    r1b = [A(f"r1t{i}", [128, 512], BF16) for i in range(2)]
    r2b = [A(f"r2t{i}", [128, 512], BF16) for i in range(2)]
    mt = A("mt", [128, 512], BF16)
    sst = A("sst", [128, 512], BF16)
    sct = [A(f"sct{i}", [128, 512], BF16) for i in range(4)]
    dqw = [A(f"dqw{i}", [128, 512], BF16) for i in range(4)]

    dnreg = A("dnreg", [128, 11264], U8)
    dnrd = [carve(dnreg, 5632 * i, 22 * 128, BF16) for i in range(2)]
    wqb = [carve(dnreg, 4096 * i, 2048, BF16) for i in range(2)]
    dsb = carve(dnreg, 8192, 512, F32)[0:64, :]
    rcp = A("rcp", [64, 512], F32)
    rcpb = A("rcpb", [64, 512], BF16)
    nsb = A("nsb", [128, 512], BF16)

    silb = A("silb", [128, 512], BF16)
    yraw = [A(f"yraw{i}", [128, 512], BF16) for i in range(2)]
    ysq = [A(f"ysq{i}", [128, 512], BF16) for i in range(2)]
    rffb4 = [A(f"rffb{i}", [128, 512], BF16) for i in range(4)]

    pb = [nc.alloc_psum_tensor(f"pb{i}", [128, 512], F32) for i in range(8)]

    try:
        print(f"[kernel build] sbuf bytes remaining: {nc.sbuf_bytes_remaining}")
    except Exception:
        pass

    rowm = rowb[0:1, 0:512]
    rowv = rowb[0:1, 512:1024]
    rowt = rowb[0:1, 1024:1536]

    const_pairs = ((cst[:], consts), (mk_s[:], maskbd), (hm4_s[:], hm4),
                   (bmap_s[:], bmap), (ones_s[:], onesc), (ones1_s[:], ones1f),
                   (bp_s[:], bproj), (slot_s[:], slots))

    def init_consts(e):
        for dst, src in const_pairs:
            e.dma_start(out=dst, in_=src).then_inc(S["s_c"], 16)
    p.on("sync", init_consts)
    C_RDY = p.bump("s_c", len(const_pairs) * 16)

    def g_init(e):
        e.wait_ge(S["s_c"], C_RDY)
    p.on("gpsimd", g_init)

    wregs = {}

    def s_initregs(e):
        e.wait_ge(S["s_c"], C_RDY)
        wregs["gu"] = e.alloc_register("rs_gu")
        wregs["dn"] = e.alloc_register("rs_dn")
        wregs["tmp"] = e.alloc_register("rs_tmp")
        e.reg_load(wregs["gu"], slot_s[0:1, 2:3])
        e.reg_load(wregs["dn"], slot_s[0:1, 3:4])
    p.on("sync", s_initregs)

    if "0" in stages:
        for u in range(128):
            ch, hf = u // 8, u % 8
            buf = xbuf[u % 2]
            aw = p.val("s_acc")

            sx = f"s_x{u % 2}"

            def s_load(e, u=u, ch=ch, hf=hf, buf=buf, aw=aw, sx=sx):
                if u >= 2:
                    e.wait_ge(S["s_acc"], aw - 1)
                e.dma_start(out=buf[:],
                            in_=x_kv.rearrange("(c p) t -> p c t", p=128)[:, ch, 512 * hf:512 * hf + 512]
                            ).then_inc(S[sx], 16)
            p.on("sync", s_load)
            XW = p.bump(sx, 16)

            def a_sq(e, buf=buf, u=u, XW=XW, sx=sx):
                e.wait_ge(S[sx], XW)
                e.activation(out=xsqt[u % 2][:], in_=buf[:], func=AF.Square).then_inc(S["s_sq"], 1)
            p.on("scalar", a_sq)
            SQ = p.bump("s_sq", 1)

            def v_acc(e, buf=buf, ch=ch, hf=hf, u=u, XW=XW, SQ=SQ, sx=sx):
                e.wait_ge(S[sx], XW)
                sl = slice(512 * hf, 512 * hf + 512)
                if ch == 0:
                    e.tensor_copy(out=xacc[:, sl], in_=buf[:])
                else:
                    e.tensor_tensor(out=xacc[:, sl], in0=xacc[:, sl], in1=buf[:], op=ALU.add)
                e.wait_ge(S["s_sq"], SQ)
                if ch == 0:
                    e.tensor_copy(out=xqacc[:, sl], in_=xsqt[u % 2][:]).then_inc(S["s_acc"], 1)
                else:
                    e.tensor_tensor(out=xqacc[:, sl], in0=xqacc[:, sl], in1=xsqt[u % 2][:],
                                    op=ALU.add).then_inc(S["s_acc"], 1)
            p.on("vector", v_acc)
            p.bump("s_acc", 1)

        ACC_ALL = p.val("s_acc")

        for tb in range(8):
            evw = p.val("s_ev")
            abw = p.val("s_ab")

            def t_st(e, tb=tb, evw=evw):
                e.wait_ge(S["s_acc"], ACC_ALL)
                e.wait_ge(S["s_c"], C_RDY)
                if tb >= 1:
                    e.wait_ge(S["s_ev"], evw)
                e.matmul(pb[0][0:1, :], ones_s[:, 0:1], xacc[:, 512 * tb:512 * tb + 512],
                         start=True, stop=True)
                e.matmul(pb[1][0:1, :], ones_s[:, 0:1], xqacc[:, 512 * tb:512 * tb + 512],
                         start=True, stop=True).then_inc(S["s_pe"], 1)
            p.on("tensor", t_st)
            PEW = p.bump("s_pe", 1)

            def a_ev(e, tb=tb, PEW=PEW, abw=abw):
                e.wait_ge(S["s_pe"], PEW)
                if tb >= 1:
                    e.wait_ge(S["s_ab"], abw)
                e.activation(out=rowm, in_=pb[0][0:1, :], func=AF.Copy)
                e.activation(out=rowv, in_=pb[1][0:1, :], func=AF.Copy).then_inc(S["s_ev"], 1)
            p.on("scalar", a_ev)
            EV1 = p.bump("s_ev", 1)

            def v_row(e, EV1=EV1):
                e.wait_ge(S["s_ev"], EV1)
                e.tensor_scalar(out=rowm, in0=rowm, scalar1=1.0 / 2048.0, scalar2=None, op0=ALU.mult)
                e.tensor_scalar(out=rowv, in0=rowv, scalar1=1.0 / 2048.0, scalar2=None, op0=ALU.mult)
                e.tensor_tensor(out=rowt, in0=rowm, in1=rowm, op=ALU.mult)
                e.tensor_tensor(out=rowv, in0=rowv, in1=rowt, op=ALU.subtract).then_inc(S["s_ev"], 1)
            p.on("vector", v_row)
            EV2 = p.bump("s_ev", 1)

            def a_row(e, EV2=EV2):
                e.wait_ge(S["s_ev"], EV2)
                e.activation(out=rowv, in_=rowv, func=AF.Ln, bias=cst[0:1, 0:1])
                e.activation(out=rowv, in_=rowv, func=AF.Exp, scale=cst[0:1, 1:2]).then_inc(S["s_ev"], 1)
            p.on("scalar", a_row)
            EV3 = p.bump("s_ev", 1)

            def v_row2(e, EV3=EV3):
                e.wait_ge(S["s_ev"], EV3)
                e.tensor_tensor(out=rowm, in0=rowm, in1=rowv, op=ALU.mult)
                e.tensor_scalar(out=rowm, in0=rowm, scalar1=-1.0, scalar2=None,
                                op0=ALU.mult).then_inc(S["s_ev"], 1)
            p.on("vector", v_row2)
            EV4 = p.bump("s_ev", 1)

            def t_bc(e, EV4=EV4):
                e.wait_ge(S["s_ev"], EV4)
                e.matmul(pb[2][:], ones1_s[:], rowv, start=True, stop=True)
                e.matmul(pb[3][:], ones1_s[:], rowm, start=True, stop=True).then_inc(S["s_pe"], 1)
            p.on("tensor", t_bc)
            PEB = p.bump("s_pe", 1)

            def a_bc(e, tb=tb, PEB=PEB):
                e.wait_ge(S["s_pe"], PEB)
                e.activation(out=A_s[:, 512 * tb:512 * tb + 512], in_=pb[2][:], func=AF.Copy)
                e.activation(out=B_s[:, 512 * tb:512 * tb + 512], in_=pb[3][:],
                             func=AF.Copy).then_inc(S["s_ab"], 1)
            p.on("scalar", a_bc)
            p.bump("s_ab", 1)

        AB_RDY = p.val("s_ab")
        ST0_PE = p.val("s_pe")

        if "stats" in DBG:
            def dbg_ab(e):
                e.wait_ge(S["s_ab"], AB_RDY)
                e.dma_start(out=dbg_t["d_ab"][0:128, :], in_=A_s[:]).then_inc(S["s_dbg"], 16)
                e.dma_start(out=dbg_t["d_ab"][128:256, :], in_=B_s[:]).then_inc(S["s_dbg"], 16)
            p.on("gpsimd", dbg_ab)
            p.bump("s_dbg", 32)
    else:
        AB_RDY = p.val("s_ab")
        ACC_ALL = p.val("s_acc")
        ST0_PE = p.val("s_pe")

    xn_ctr = [0]

    def emit_xn_jit(c0, pe_guard):
        PARA = globals().get('PARA', 0)
        for ch in range(CH):
            buf = xbuf[ch % 2]
            gidx = xn_ctr[0]
            aw = p.val("s_xn")
            sx = f"s_x{gidx % 2}"

            def s_ld(e, ch=ch, buf=buf, c0=c0, aw=aw, gidx=gidx, PARA=PARA, sx=sx):
                if gidx >= 2:
                    e.wait_ge(S["s_xn"], aw if (PARA & 1) else aw - 1)
                else:
                    e.wait_ge(S["s_acc"], ACC_ALL)
                e.dma_start(out=buf[:],
                            in_=x_kv.rearrange("(c p) t -> p c t", p=128)[:, ch, c0:c0 + 512]
                            ).then_inc(S[sx], 16)
            p.on("sync", s_ld)
            XW = p.bump(sx, 16)

            def v_xn(e, ch=ch, buf=buf, c0=c0, XW=XW, pe_guard=pe_guard, sx=sx):
                e.wait_ge(S[sx], XW)
                e.wait_ge(S["s_ab"], AB_RDY)
                if pe_guard:
                    e.wait_ge(S["s_pe"], pe_guard)
                e.tensor_tensor(out=xjit[:], in0=buf[:], in1=A_s[:, c0:c0 + 512], op=ALU.mult)
                e.tensor_tensor(out=xn[ch][:], in0=xjit[:], in1=B_s[:, c0:c0 + 512],
                                op=ALU.add).then_inc(S["s_xn"], 1)
            p.on("vector", v_xn)
            p.bump("s_xn", 1)
            xn_ctr[0] += 1
        return p.val("s_xn")

    seq = []
    for ob in range(GU_OB):
        for t in range(GU_BT):
            seq.append(("gu", t, ob))
    for obd in range(DN_OB):
        for t in range(DN_BT):
            seq.append(("dn", t, obd))

    dqs = {"idx": 0, "int": [], "casta": [], "nib2": [], "nibm": [], "w": []}
    dqw_gu, dqw_dn = {}, {}

    def _dq_phase1(ti):
        kind, t, ob = seq[ti]
        i = ti % 2
        i4 = ti % 4
        bybuf = byb[i4]
        src_by = dn_by if kind == "dn" else gu_by
        src_sc = dn_sct if kind == "dn" else gu_sct
        k_sc = (22 * ob + t) if kind == "dn" else (8 * ob + t)

        int4 = dqs["int"][ti - 4] if ti >= 4 else 0
        casta2 = dqs["casta"][ti - 2] if ti >= 2 else 0
        nib2_2 = dqs["nib2"][ti - 2] if ti >= 2 else 0
        nib2_4 = dqs["nib2"][ti - 4] if ti >= 4 else 0
        dqa_prev = p.val("s_dqa")

        sby = f"s_by{i4}"

        def s_by_ld(e, t=t, ob=ob, bybuf=bybuf, src_by=src_by, int4=int4, ti=ti, sby=sby):
            if int4:
                e.wait_ge(S["s_dqv"], int4)
            e.dma_start(out=bybuf[:], in_=src_by[128 * t:128 * t + 128, 512 * ob:512 * ob + 512]
                        ).then_inc(S[sby], 16)
        p.on("sync", s_by_ld)
        BY = p.bump(sby, 16)

        ssc = f"s_sc{i4}"

        def g_scl(e, i4=i4, src_sc=src_sc, k_sc=k_sc, nib2_4=nib2_4, ti=ti, ssc=ssc):
            if nib2_4:
                e.wait_ge(S["s_dqv"], nib2_4)
            e.dma_start(out=sct[i4][:], in_=src_sc[128 * k_sc:128 * k_sc + 128, :]
                        ).then_inc(S[ssc], 16)
        p.on("gpsimd", g_scl)
        SCL = p.bump(ssc, 16)

        def v_int(e, i=i, bybuf=bybuf, BY=BY, casta2=casta2, ti=ti, sby=sby):
            e.wait_ge(S[sby], BY)
            if ti >= 2:
                e.wait_ge(S["s_dqp2"], casta2)
            e.tensor_scalar(out=u8a[i][:], in0=bybuf[:], scalar1=7, scalar2=None, op0=ALU.bitwise_and)
            e.tensor_scalar(out=u8b[i][:], in0=bybuf[:], scalar1=4, scalar2=7,
                            op0=ALU.logical_shift_right, op1=ALU.bitwise_and)
            e.tensor_scalar(out=u8c[i][:], in0=bybuf[:], scalar1=8, scalar2=None, op0=ALU.bitwise_and)
            e.tensor_scalar(out=u8d[i][:], in0=bybuf[:], scalar1=7, scalar2=None,
                            op0=ALU.logical_shift_right).then_inc(S["s_dqv"], 1)
        p.on("vector", v_int)
        INTM = p.bump("s_dqv", 1)
        dqs["int"].append(INTM)

        def v_cast(e, i=i, dqa_prev=dqa_prev, ti=ti):
            if dqa_prev:
                e.wait_ge(S["s_dqa"], dqa_prev)
            e.tensor_scalar(out=thl[i][:], in0=u8a[i][:], scalar1=0.5, scalar2=None, op0=ALU.mult)
            e.tensor_scalar(out=thh[i][:], in0=u8b[i][:], scalar1=0.5, scalar2=None,
                            op0=ALU.mult).then_inc(S["s_dqp"], 1)
        p.on("vector", v_cast)
        CASTV = p.bump("s_dqp", 1)

        def a_cast(e, i=i, INTM=INTM, nib2_2=nib2_2, ti=ti):
            e.wait_ge(S["s_dqv"], INTM)
            if ti >= 2:
                e.wait_ge(S["s_dqv"], nib2_2)
            e.activation(out=sml[i][:], in_=u8c[i][:], func=AF.Copy, scale=-0.25)
            e.activation(out=smh[i][:], in_=u8d[i][:], func=AF.Copy,
                         scale=-2.0).then_inc(S["s_dqp2"], 1)
        p.on("scalar", a_cast)
        CASTA = p.bump("s_dqp2", 1)
        dqs["casta"].append(CASTA)
        return dict(ti=ti, kind=kind, t=t, ob=ob, i=i, i4=i4, CASTV=CASTV, SCL=SCL, ssc=ssc)

    def _dq_phase2(ctx):
        ti, kind, t, ob, i = ctx["ti"], ctx["kind"], ctx["t"], ctx["ob"], ctx["i"]
        CASTV, SCL, ssc, i4 = ctx["CASTV"], ctx["SCL"], ctx["ssc"], ctx["i4"]
        for nib in range(2):
            th_t = [thl, thh][nib][i]
            sm_t = [sml, smh][nib][i]
            r1 = r1b[nib]
            r2 = r2b[nib]
            wt = dqw[(2 * ti + nib) % 4]
            prevn = dqs["nibm"][2 * (ti - 1) + nib] if ti >= 1 else 0

            def a_relu(e, th_t=th_t, r1=r1, r2=r2, CASTV=CASTV, prevn=prevn):
                e.wait_ge(S["s_dqp"], CASTV)
                if prevn:
                    e.wait_ge(S["s_dqv"], prevn)
                e.activation(out=r1[:], in_=th_t[:], func=AF.Relu, bias=cst[:, 2:3])
                e.activation(out=r2[:], in_=th_t[:], func=AF.Relu, bias=cst[:, 3:4],
                             scale=cst[:, 4:5]).then_inc(S["s_dqa"], 1)
            p.on("scalar", a_relu)
            DQA = p.bump("s_dqa", 1)

            wm_idx = 2 * ti + nib
            sdqw = f"s_dqw{wm_idx % 4}"
            w_prev = p.val(sdqw) if wm_idx >= 4 else 0

            def v_nib(e, th_t=th_t, sm_t=sm_t, wt=wt, i4=i4, DQA=DQA, SCL=SCL, ssc=ssc,
                      w_prev=w_prev, sdqw=sdqw, r1=r1, r2=r2):
                e.wait_ge(S["s_dqa"], DQA)
                e.wait_ge(S[ssc], SCL)
                if w_prev:
                    e.wait_ge(S[sdqw], w_prev)
                e.scalar_tensor_tensor(out=mt[:], in0=th_t[:], scalar=1.0, in1=r1[:],
                                       op0=ALU.mult, op1=ALU.add)
                e.tensor_tensor(out=mt[:], in0=mt[:], in1=r2[:], op=ALU.add)
                e.scalar_tensor_tensor(out=sst[:], in0=sm_t[:], scalar=1.0, in1=sct[i4][:],
                                       op0=ALU.add, op1=ALU.mult)
                e.tensor_tensor(out=wt[:], in0=mt[:], in1=sst[:], op=ALU.mult).then_inc(S["s_dqv"], 1)
            p.on("vector", v_nib)
            NIB = p.bump("s_dqv", 1)
            dqs["nibm"].append(NIB)
            if nib == 1:
                dqs["nib2"].append(NIB)

            if kind == "gu":
                jj0 = 4 * ob
                hc = 2 * t + nib
                dst = wl_gu.ap().rearrange("(jj p hc) o -> jj p hc o", p=128, hc=16)[jj0:jj0 + 4, :, hc, :]
            else:
                hc0 = 4 * ob
                ic = 2 * t + nib
                dst = wl_dn.ap().rearrange("(Hc p ic) o -> Hc p ic o", p=128, ic=ICn)[hc0:hc0 + 4, :, ic, :]

            def s_wt_w(e, wt=wt, dst=dst, NIB=NIB, sdqw=sdqw):
                e.wait_ge(S["s_dqv"], NIB)
                e.dma_start(out=dst.transpose([1, 0, 2]),
                            in_=wt[:].rearrange("p (q o) -> p q o", q=4)
                            ).then_inc(S[sdqw], 16)
            p.on("sync", s_wt_w)
            p.bump(sdqw, 16)
            dqs["w"].append(tuple(p.val(f"s_dqw{k}") for k in range(4)))

    dq_pend = []

    def emit_dq_tiles(n):
        if "Q" not in stages:
            return
        for _ in range(n):
            ti = dqs["idx"]
            if ti >= len(seq):
                return
            dqs["idx"] += 1
            dq_pend.append(_dq_phase1(ti))
            if len(dq_pend) >= 2:
                _dq_phase2(dq_pend.pop(0))

    def flush_dq():
        while dq_pend:
            _dq_phase2(dq_pend.pop(0))

    gu_chunks = [(0, 3), (3, 6), (6, 9), (9, 11)]
    gu_copied = set()

    def emit_gu_copy(ci):
        if "Q" not in stages or ci in gu_copied:
            return
        gu_copied.add(ci)
        o0, o1 = gu_chunks[ci]
        need4 = dqs["w"][2 * (8 * o1) - 1]
        e0 = (4 * o0) * 16 * 16384
        e1 = (4 * o1) * 16 * 16384

        def s_cp(e, need4=need4, e0=e0, e1=e1):
            for k in range(4):
                if need4[k]:
                    e.wait_ge(S[f"s_dqw{k}"], need4[k])
            e.reg_add(wregs["tmp"], wregs["gu"], e0)
            dst = bass.AP(wt_gu, wregs["tmp"], [[1, e1 - e0]])
            e.dma_start(out=dst, in_=bass.AP(wl_gu, e0, [[1, e1 - e0]])
                        ).then_inc(S["s_dqc"], 16)
        p.on("sync", s_cp)
        p.bump("s_dqc", 16)

    if "A" in stages:
        kv_mm_marks = []
        tkv_pend = []
        for sl in range(NSL):
            pe_guard = p.val("s_pe") if sl > 0 else ST0_PE
            XN_RDY = emit_xn_jit(512 * sl, pe_guard)

            for r in range(NRA):
                for kv in range(2):
                    ridx = (sl * NRA + r) * 2 + kv
                    wbuf = (wkv + wkv_x)[ridx % 4]
                    if ridx >= 4:
                        wfree = kv_mm_marks[ridx - 4]
                    elif ridx < 2:
                        wfree = ST0_PE
                    else:
                        wfree = 0
                    col0 = 2048 + 2048 * kv + 256 * r

                    wfull = kv_mm_marks[ridx - 1] if ridx >= 1 else ST0_PE
                    swkv = f"s_wkv{ridx % 4}"
                    blk = 8 * kv + r

                    def g_w(e, wbuf=wbuf, wfree=wfree, blk=blk, wfull=wfull, swkv=swkv):
                        e.wait_ge(S["s_c"], C_RDY)
                        wf = wfull if (globals().get('PARA', 0) & 8) else wfree
                        if wf:
                            e.wait_ge(S["s_pe"], wf)
                        e.dma_start(out=wbuf[:],
                                    in_=wkv_r.rearrange("(b p) o -> b p o", p=128)[blk]
                                    ).then_inc(S[swkv], 16)
                    p.on("gpsimd", g_w)
                    WKV = p.bump(swkv, 16)
                    pe2_prev = p.val("s_pe2")

                    for tk in range(NTK):
                        evsem = "s_evk" if kv == 0 else "s_evv"
                        evm = p.val(evsem)

                        PARA = globals().get('PARA', 0)

                        def t_mm(e, tk=tk, kv=kv, wbuf=wbuf, WKV=WKV, XN_RDY=XN_RDY,
                                 evm=evm, ridx=ridx, evsem=evsem, PARA=PARA, swkv=swkv):
                            e.wait_ge(S["s_xn"], XN_RDY)
                            e.wait_ge(S[swkv], WKV)
                            if ridx + tk > 0:
                                e.wait_ge(S[evsem], evm if (PARA & 2) else max(0, evm - 1))
                            pq = pb[2 * kv + tk % 2]
                            wv = wbuf[:].rearrange("p (c o) -> p c o", o=256)
                            for ch in range(CH):
                                mm = e.matmul(pq[:, 0:256], xn[ch][:, 128 * tk:128 * tk + 128],
                                              wv[:, ch, :], start=(ch == 0), stop=(ch == CH - 1))
                            mm.then_inc(S["s_pe"], 1)
                        p.on("tensor", t_mm)
                        PEM = p.bump("s_pe", 1)

                        if kv == 0:
                            def a_evk(e, tk=tk, PEM=PEM, pe2_prev=pe2_prev):
                                e.wait_ge(S["s_pe"], PEM)
                                if pe2_prev:
                                    e.wait_ge(S["s_pe2"], pe2_prev)
                                e.activation(out=kvK[tk][:], in_=pb[tk % 2][:, 0:256],
                                             func=AF.Relu).then_inc(S["s_evk"], 1)
                            p.on("scalar", a_evk)
                            p.bump("s_evk", 1)
                        else:
                            def v_evv(e, tk=tk, PEM=PEM, pe2_rs=pe2_prev):
                                e.wait_ge(S["s_pe"], PEM)
                                if pe2_rs:
                                    e.wait_ge(S["s_pe2"], pe2_rs)
                                e.activation(out=kvV[tk][:], in_=pb[2 + tk % 2][:, 0:256],
                                             func=AF.Copy).then_inc(S["s_evv"], 1)
                            p.on("scalar", v_evv)
                            EVV = p.bump("s_evv", 1)
                            evk_need = p.val("s_evk")

                            def t_KV(e, tk=tk, EVV=EVV, evk_need=evk_need):
                                e.wait_ge(S["s_evv"], EVV)
                                e.wait_ge(S["s_evk"], evk_need)
                                kb = kvK[tk]
                                vb = kvV[tk]
                                for gl in range(2):
                                    e.matmul(pb[4][:, 128 * gl:128 * gl + 128],
                                             kb[:, 128 * gl:128 * gl + 128],
                                             vb[:, 128 * gl:128 * gl + 128],
                                             start=(tk == 0 and gl == 0), stop=(tk == NTK - 1))
                                e.matmul(pb[5][:, 0:1], kb[:, 0:128], ones_s[:, 0:1],
                                         start=(tk == 0), stop=(tk == NTK - 1))
                                e.matmul(pb[5][:, 1:2], kb[:, 128:256], ones_s[:, 0:1],
                                         start=False, stop=(tk == NTK - 1)).then_inc(S["s_pe2"], 1)
                            if tkv_pend:
                                p.on("tensor", tkv_pend.pop())
                                p.bump("s_pe2", 1)
                            tkv_pend.append(t_KV)

                    if kv == 1 and tkv_pend:
                        p.on("tensor", tkv_pend.pop())
                        p.bump("s_pe2", 1)
                    kv_mm_marks.append(p.val("s_pe"))

                PE2 = p.val("s_pe2")

                def v_kvadd(e, sl=sl, r=r, PE2=PE2):
                    e.wait_ge(S["s_pe2"], PE2)
                    slc = slice(256 * r, 256 * r + 256)
                    if sl == 0:
                        e.tensor_copy(out=KVm[:, slc], in_=pb[4][:, 0:256])
                        e.tensor_copy(out=ksm[:, 2 * r:2 * r + 2], in_=pb[5][:, 0:2]).then_inc(S["s_ev"], 1)
                    else:
                        e.tensor_tensor(out=KVm[:, slc], in0=KVm[:, slc], in1=pb[4][:, 0:256], op=ALU.add)
                        e.tensor_tensor(out=ksm[:, 2 * r:2 * r + 2], in0=ksm[:, 2 * r:2 * r + 2],
                                        in1=pb[5][:, 0:2], op=ALU.add).then_inc(S["s_ev"], 1)
                p.on("vector", v_kvadd)
                KVEV = p.bump("s_ev", 1)

                def t_guard(e, KVEV=KVEV):
                    e.wait_ge(S["s_ev"], KVEV)
                p.on("tensor", t_guard)

                if not globals().get('NO_ILV'):
                    emit_dq_tiles((2, 1, 2, 1, 2, 1, 1, 1)[r])

            if not globals().get('NO_ILV'):
                if sl == 3:
                    emit_gu_copy(0)
                elif sl == 5:
                    emit_gu_copy(1)
                elif sl == 6:
                    emit_gu_copy(2)

        A_DONE_EV = p.val("s_ev")
        A_PE = p.val("s_pe")
    else:
        A_DONE_EV = p.val("s_ev")
        A_PE = p.val("s_pe")

    if "X" in stages:
        if "kvpre" in DBG:
            def dbg_kvpre(e):
                e.wait_ge(S["s_ev"], A_DONE_EV)
                e.dma_start(out=dbg_t["d_kvp"][:, 0:2048], in_=KVm[:]).then_inc(S["s_dbg"], 16)
                e.dma_start(out=dbg_t["d_kvp"][:, 2048:2064], in_=ksm[:]).then_inc(S["s_dbg"], 16)
            p.on("sync", dbg_kvpre)
            KVPRE = p.bump("s_dbg", 32)

        def s_xw(e):
            e.wait_ge(S["s_ev"], A_DONE_EV)
            with e.register("mo") as mo, e.register("mf") as mf:
                e.reg_load(mo, slot_s[0:1, 0:1])
                e.reg_mul(mf, mo, 128 * 2064)
                e.dma_start(out=bass.AP(kvx, mf, [[2064, 128], [1, 2048]]),
                            in_=KVm[:]).then_inc(S["s_kvw"], 16)
                with e.register("mf2") as mf2:
                    e.reg_add(mf2, mf, 2048)
                    e.dma_start(out=bass.AP(kvx, mf2, [[2064, 128], [1, 16]]),
                                in_=ksm[:]).then_inc(S["s_kvw"], 16)
        p.on("sync", s_xw)
        KVW = p.bump("s_kvw", 32)

        def g_flag(e, KVW=KVW):
            e.wait_ge(S["s_kvw"], KVW)
            with e.register("f_mo") as mo, e.register("f_mf") as mf:
                e.reg_load(mo, slot_s[0:1, 0:1])
                e.reg_mul(mf, mo, 128)
                e.dma_start(out=bass.AP(flg, mf, [[128, 1], [1, 1]]),
                            in_=slot_s[0:1, 4:5]).then_inc(S["s_kvw"], 16)
        p.on("gpsimd", g_flag)
        p.bump("s_kvw", 16)

        def g_spin(e):
            with e.register("po") as po, e.register("pf") as pf, \
                 e.register("t") as t, e.register("f") as f, \
                 e.register("nn") as nn:
                e.reg_load(nn, slot_s[0:1, 4:5])
                e.reg_load(po, slot_s[0:1, 1:2])
                e.reg_mul(pf, po, 128)
                fsrc = bass.AP(flg, pf, [[128, 1], [1, 1]])
                e.reg_mov(t, 16)
                e.reg_mov(f, 0)

                def cond():
                    e.dma_start(out=flag_s[0:1, 0:1], in_=fsrc).then_inc(S["s_poll"], 16)
                    e.wait_ge(S["s_poll"], t)
                    e.reg_add(t, t, 16)
                    e.reg_load(f, flag_s[0:1, 0:1])
                    return RuntimeValue(f) != RuntimeValue(nn)

                with e.While(cond):
                    pass
                with e.register("pk") as pk:
                    e.reg_mul(pk, po, 128 * 2064)
                    e.dma_start(out=KVp[:], in_=bass.AP(kvx, pk, [[2064, 128], [1, 2048]])
                                ).then_inc(S["s_xch"], 16)
                    with e.register("pk2") as pk2:
                        e.reg_add(pk2, pk, 2048)
                        e.dma_start(out=ksp[:], in_=bass.AP(kvx, pk2, [[2064, 128], [1, 16]])
                                    ).then_inc(S["s_xch"], 16)
        p.on("gpsimd", g_spin)
        XCH = p.bump("s_xch", 32)

        def v_kvfull(e, XCH=XCH):
            e.wait_ge(S["s_xch"], XCH)
            if "kvpre" in DBG:
                e.wait_ge(S["s_dbg"], KVPRE)
            e.tensor_tensor(out=KVm[:], in0=KVm[:], in1=KVp[:], op=ALU.add)
            e.tensor_tensor(out=ksm[:], in0=ksm[:], in1=ksp[:], op=ALU.add)
            for g in range(NG):
                e.tensor_tensor(out=KVb[:, 128 * g:128 * g + 128],
                                in0=KVm[:, 128 * g:128 * g + 128], in1=mk_s[:], op=ALU.mult)
            e.memset(denl[:], 0.0)
            last = None
            for g in range(NG):
                for hh in range(4):
                    col = 64 * g + 4 * g + hh
                    last = e.tensor_scalar(out=denl[:, col:col + 1], in0=hm4_s[:, hh:hh + 1],
                                           scalar1=ksm[:, g:g + 1], scalar2=None, op0=ALU.mult)
            last.then_inc(S["s_xch"], 1)
        p.on("vector", v_kvfull)
        KV_RDY = p.bump("s_xch", 1)

        if "kv" in DBG:
            def dbg_kv(e):
                e.wait_ge(S["s_xch"], KV_RDY)
                e.dma_start(out=dbg_t["d_kv"][0:128, 0:2048], in_=KVm[:]).then_inc(S["s_dbg"], 16)
                e.dma_start(out=dbg_t["d_kv"][0:128, 2048:2064], in_=ksm[:]).then_inc(S["s_dbg"], 16)
                e.dma_start(out=dbg_t["d_kv"][128:256, 0:2048], in_=KVb[:]).then_inc(S["s_dbg"], 16)
            p.on("gpsimd", dbg_kv)
            p.bump("s_dbg", 48)
    else:
        KV_RDY = p.val("s_xch")

    if "Q" in stages:
        emit_dq_tiles(88 - dqs["idx"])
        flush_dq()
        for ci in range(4):
            emit_gu_copy(ci)
        DQC_GU = p.val("s_dqc")

        def g_flag_dq(e):
            e.wait_ge(S["s_dqc"], DQC_GU)
            with e.register("fq_mo") as mo, e.register("fq_mf") as mf:
                e.reg_load(mo, slot_s[0:1, 0:1])
                e.reg_mul(mf, mo, 128)
                with e.register("fq_mf2") as mf2:
                    e.reg_add(mf2, mf, 256)
                    e.dma_start(out=bass.AP(flg, mf2, [[128, 1], [1, 1]]),
                                in_=slot_s[0:1, 4:5]).then_inc(S["s_kvw"], 16)
        p.on("gpsimd", g_flag_dq)
        p.bump("s_kvw", 16)

        def g_spin_dq(e):
            with e.register("dq_po") as po, e.register("dq_pf") as pf, \
                 e.register("dq_t") as t, e.register("dq_f") as f, \
                 e.register("dq_nn") as nn:
                e.reg_load(nn, slot_s[0:1, 4:5])
                e.reg_load(po, slot_s[0:1, 1:2])
                e.reg_mul(pf, po, 128)
                with e.register("dq_pf2") as pf2:
                    e.reg_add(pf2, pf, 256)
                    fsrc = bass.AP(flg, pf2, [[128, 1], [1, 1]])
                    e.reg_mov(t, 16)
                    e.reg_mov(f, 0)

                    def cond():
                        e.dma_start(out=flag_s[0:1, 1:2], in_=fsrc).then_inc(S["s_poll2"], 16)
                        e.wait_ge(S["s_poll2"], t)
                        e.reg_add(t, t, 16)
                        e.reg_load(f, flag_s[0:1, 1:2])
                        return RuntimeValue(f) != RuntimeValue(nn)

                    with e.While(cond):
                        pass
                e.sem_inc(S["s_xch"], 1)
        p.on("gpsimd", g_spin_dq)
        DQ_RDY2 = p.bump("s_xch", 1)
    else:
        DQ_RDY2 = p.val("s_xch")

    if "C" in stages:
        for qt in range(NQ):
            pe_guard = p.val("s_pe") if (qt > 0 or "A" in stages) else ST0_PE
            XN_RDY = emit_xn_jit(512 * qt, pe_guard)

            gu_marks = []
            for j in range(NJ):
                if qt == 0 and j < 22:
                    emit_dq_tiles(2)
                sgu = f"s_gu{j % 2}"
                for half, jj in ((0, j), (1, 44 + j)):
                    rb = gurd[2 * (j % 2) + half]
                    need = DQ_RDY2 if qt == 0 and j < 2 else 0
                    if j >= 2:
                        gufree = gu_marks[j - 2]
                    elif qt == 0:
                        gufree = A_PE
                    else:
                        gufree = gu_prev_last

                    def s_rd(e, rb=rb, jj=jj, need=need, gufree=gufree, sgu=sgu):
                        if gufree:
                            e.wait_ge(S["s_pe"], gufree)
                        if need:
                            e.wait_ge(S["s_xch"], need)
                        e.dma_start(out=rb[:],
                                    in_=wt_gu.ap().rearrange("(jj p r) o -> jj p (r o)", jj=88, p=128)[jj]
                                    ).then_inc(S[sgu], 16)
                    p.on("sync", s_rd)
                p.bump(sgu, 32)
                WT = p.val(sgu)
                evg = p.val("s_ev")

                def t_gu(e, j=j, WT=WT, XN_RDY=XN_RDY, evg=evg, sgu=sgu):
                    e.wait_ge(S[sgu], WT)
                    e.wait_ge(S["s_xn"], XN_RDY)
                    e.wait_ge(S["s_ev"], max(0, evg - 2))
                    pg = pb[2 * (j % 2)]
                    pu = pb[2 * (j % 2) + 1]
                    for hc in range(CH):
                        e.matmul(pg[:], gurd[2 * (j % 2)][:, 128 * hc:128 * hc + 128], xn[hc][:],
                                 start=(hc == 0), stop=(hc == CH - 1))
                    for hc in range(CH):
                        mm = e.matmul(pu[:], gurd[2 * (j % 2) + 1][:, 128 * hc:128 * hc + 128],
                                      xn[hc][:], start=(hc == 0), stop=(hc == CH - 1))
                    mm.then_inc(S["s_pe"], 1)
                p.on("tensor", t_gu)
                GU = p.bump("s_pe", 1)
                gu_marks.append(GU)

                def a_sil(e, j=j, GU=GU):
                    e.wait_ge(S["s_pe"], GU)
                    e.activation(out=silb[:], in_=pb[2 * (j % 2)][:], func=AF.Silu).then_inc(S["s_ev"], 1)
                p.on("scalar", a_sil)
                SIL = p.bump("s_ev", 1)

                def v_h(e, j=j, SIL=SIL):
                    e.wait_ge(S["s_ev"], SIL)
                    e.tensor_tensor(out=hsb[j][:], in0=silb[:], in1=pb[2 * (j % 2) + 1][:],
                                    op=ALU.mult).then_inc(S["s_ev"], 1)
                p.on("vector", v_h)
                p.bump("s_ev", 1)

            gu_prev_last = gu_marks[-1]
            H_RDY = p.val("s_ev")

            if qt == 0 and "Q" in stages:
                assert dqs["idx"] == len(seq), dqs["idx"]
                flush_dq()
                for obd in range(2):
                    need4 = dqs["w"][2 * (88 + 22 * (obd + 1)) - 1]
                    e0 = (4 * obd) * ICn * 16384
                    e1 = (4 * (obd + 1)) * ICn * 16384

                    def g_cpd(e, need4=need4, e0=e0, e1=e1, obd=obd):
                        for k in range(4):
                            if need4[k]:
                                e.wait_ge(S[f"s_dqw{k}"], need4[k])
                        with e.register(f"cd_b{obd}") as rb_, e.register(f"cd_t{obd}") as rt_:
                            e.reg_load(rb_, slot_s[0:1, 3:4])
                            e.reg_add(rt_, rb_, e0)
                            dst = bass.AP(wt_dn, rt_, [[1, e1 - e0]])
                            e.dma_start(out=dst, in_=bass.AP(wl_dn, e0, [[1, e1 - e0]])
                                        ).then_inc(S["s_dqc"], 16)
                    p.on("gpsimd", g_cpd)
                    p.bump("s_dqc", 16)
                DQC_DN = p.val("s_dqc")

                def g_flag_dn(e, DQC_DN=DQC_DN):
                    e.wait_ge(S["s_dqc"], DQC_DN)
                    with e.register("fd_mo") as mo, e.register("fd_mf") as mf:
                        e.reg_load(mo, slot_s[0:1, 0:1])
                        e.reg_mul(mf, mo, 128)
                        with e.register("fd_mf2") as mf2:
                            e.reg_add(mf2, mf, 512)
                            e.dma_start(out=bass.AP(flg, mf2, [[128, 1], [1, 1]]),
                                        in_=slot_s[0:1, 4:5]).then_inc(S["s_kvw"], 16)
                p.on("gpsimd", g_flag_dn)
                p.bump("s_kvw", 16)

                def g_spin_dn(e):
                    with e.register("dn_po") as po, e.register("dn_pf") as pf, \
                         e.register("dn_t") as t, e.register("dn_f") as f, \
                         e.register("dn_nn") as nn:
                        e.reg_load(nn, slot_s[0:1, 4:5])
                        e.reg_load(po, slot_s[0:1, 1:2])
                        e.reg_mul(pf, po, 128)
                        with e.register("dn_pf2") as pf2:
                            e.reg_add(pf2, pf, 512)
                            fsrc = bass.AP(flg, pf2, [[128, 1], [1, 1]])
                            e.reg_mov(t, 16)
                            e.reg_mov(f, 0)

                            def cond():
                                e.dma_start(out=flag_s[0:1, 2:3], in_=fsrc).then_inc(S["s_poll3"], 16)
                                e.wait_ge(S["s_poll3"], t)
                                e.reg_add(t, t, 16)
                                e.reg_load(f, flag_s[0:1, 2:3])
                                return RuntimeValue(f) != RuntimeValue(nn)

                            with e.While(cond):
                                pass
                    e.sem_inc(S["s_xch"], 1)
                p.on("gpsimd", g_spin_dn)
                DN_RDY = p.bump("s_xch", 1)
            elif qt == 0:
                DN_RDY = p.val("s_xch")

            dn_h0 = []
            dn_marks = []
            sff_marks = []
            sff_pending = []
            for Hc in range(CH):
                need = DN_RDY if qt == 0 and Hc < 2 else 0
                dn_cnt = [0, 0]
                for hv in range(2):
                    rb = dnrd[hv]
                    if Hc >= 1:
                        dnfree = dn_h0[Hc - 1] if hv == 0 else dn_marks[Hc - 1]
                    else:
                        dnfree = (A_PE if qt == 0 else dn_prev_last)

                    def s_dnl(e, rb=rb, Hc=Hc, hv=hv, need=need, dnfree=dnfree):
                        if dnfree:
                            e.wait_ge(S["s_pe"], dnfree)
                        if need:
                            e.wait_ge(S["s_xch"], need)
                        e.dma_start(out=rb[:],
                                    in_=wt_dn.ap().rearrange("(Hc p r) o -> Hc p (r o)", Hc=16, p=128)
                                    [Hc][:, 22 * 128 * hv:22 * 128 * (hv + 1)]
                                    ).then_inc(S[f"s_dn{hv}"], 16)
                    p.on("sync", s_dnl)
                    dn_cnt[hv] = p.bump(f"s_dn{hv}", 16)
                evd = p.val("s_ev")

                def t_dn_a(e, Hc=Hc, WT=dn_cnt[0], H_RDY=H_RDY, evd=evd):
                    e.wait_ge(S["s_dn0"], WT)
                    e.wait_ge(S["s_ev"], H_RDY)
                    e.wait_ge(S["s_ev"], max(0, evd - 1))
                    pd = pb[4 + Hc % 2]
                    for ic in range(22):
                        mm = e.matmul(pd[:], dnrd[0][:, 128 * ic:128 * ic + 128], hsb[ic][:],
                                      start=(ic == 0), stop=False)
                    mm.then_inc(S["s_pe"], 1)
                p.on("tensor", t_dn_a)
                dn_h0.append(p.bump("s_pe", 1))

                def t_dn_b(e, Hc=Hc, WT=dn_cnt[1]):
                    e.wait_ge(S["s_dn1"], WT)
                    pd = pb[4 + Hc % 2]
                    for ic in range(22, ICn):
                        mm = e.matmul(pd[:], dnrd[1][:, 128 * (ic - 22):128 * (ic - 22) + 128], hsb[ic][:],
                                      start=False, stop=(ic == ICn - 1))
                    mm.then_inc(S["s_pe"], 1)
                p.on("tensor", t_dn_b)
                DN = p.bump("s_pe", 1)
                dn_marks.append(DN)

                if sff_pending:
                    p.on("tensor", sff_pending.pop())
                    sff_marks.append(p.bump("s_pe2", 1))

                sffw = f"s_ffw{Hc % 2}"
                ffw = p.val(sffw)
                sffp = sff_marks[Hc - 2] if Hc >= 2 else 0

                def a_y(e, Hc=Hc, DN=DN, ffw=ffw, sffp=sffp, sffw=sffw):
                    e.wait_ge(S["s_pe"], DN)
                    if ffw:
                        e.wait_ge(S[sffw], ffw)
                    if Hc >= 2:
                        e.wait_ge(S["s_pe2"], sffp)
                    e.activation(out=yraw[Hc % 2][:], in_=pb[4 + Hc % 2][:], func=AF.Copy)
                    e.activation(out=ysq[Hc % 2][:], in_=yraw[Hc % 2][:], func=AF.Square).then_inc(S["s_ev"], 1)
                p.on("scalar", a_y)
                YEV = p.bump("s_ev", 1)

                def t_sff(e, Hc=Hc, YEV=YEV):
                    e.wait_ge(S["s_ev"], YEV)
                    e.matmul(pb[6][0:1, :], ones_s[:, 0:1], ysq[Hc % 2][:],
                             start=(Hc == 0), stop=(Hc == CH - 1)).then_inc(S["s_pe2"], 1)
                sff_pending.append(t_sff)

                def s_ffw_f(e, Hc=Hc, qt=qt, YEV=YEV, sffw=sffw):
                    e.wait_ge(S["s_ev"], YEV)
                    e.dma_start(out=ffn_d.rearrange("(c p) t -> c p t", p=128)[Hc, :, 512 * qt:512 * qt + 512],
                                in_=yraw[Hc % 2][:]).then_inc(S[sffw], 16)
                p.on("sync", s_ffw_f)
                p.bump(sffw, 16)

            dn_prev_last = dn_marks[-1]
            p.on("tensor", sff_pending.pop())
            sff_marks.append(p.bump("s_pe2", 1))
            SFF = p.val("s_pe2")

            def v_sff(e, SFF=SFF):
                e.wait_ge(S["s_pe2"], SFF)
                e.tensor_scalar(out=rowt, in0=pb[6][0:1, :], scalar1=1.0 / 2048.0,
                                scalar2=None, op0=ALU.mult).then_inc(S["s_ev"], 1)
            p.on("vector", v_sff)
            VS = p.bump("s_ev", 1)

            def a_sff(e, VS=VS):
                e.wait_ge(S["s_ev"], VS)
                e.activation(out=rowt, in_=rowt, func=AF.Ln, bias=cst[0:1, 5:6])
                e.activation(out=rowt, in_=rowt, func=AF.Exp, scale=cst[0:1, 1:2]).then_inc(S["s_ev"], 1)
            p.on("scalar", a_sff)
            RFQ = p.bump("s_ev", 1)

            def t_rbq(e, RFQ=RFQ):
                e.wait_ge(S["s_ev"], RFQ)
                e.matmul(pb[6][:], ones1_s[:], rowt, start=True, stop=True).then_inc(S["s_pe2"], 1)
            p.on("tensor", t_rbq)
            RBQ = p.bump("s_pe2", 1)

            def a_rbq(e, qt=qt, RBQ=RBQ):
                e.wait_ge(S["s_pe2"], RBQ)
                e.activation(out=rffb4[qt][:], in_=pb[6][:], func=AF.Copy).then_inc(S["s_ev"], 1)
            p.on("scalar", a_rbq)
            RFF = p.bump("s_ev", 1)

            def t_g2(e, RFF=RFF):
                e.wait_ge(S["s_ev"], RFF)
            p.on("tensor", t_g2)

        FFW_ALL = (p.val("s_ffw0"), p.val("s_ffw1"))
        RFF_RDY = p.val("s_ev")
        C_PE = p.val("s_pe")
    else:
        FFW_ALL = (p.val("s_ffw0"), p.val("s_ffw1"))
        RFF_RDY = p.val("s_ev")
        C_PE = p.val("s_pe")

    if "wt" in DBG:
        DQW_ALL4 = tuple(p.val(f"s_dqw{k}") for k in range(4))

        def dbg_wt(e, DQW_ALL4=DQW_ALL4):
            for k in range(4):
                if DQW_ALL4[k]:
                    e.wait_ge(S[f"s_dqw{k}"], DQW_ALL4[k])
            e.dma_start(out=dbg_t["d_wtg"], in_=wt_gu.ap()).then_inc(S["s_dbg"], 16)
            e.dma_start(out=dbg_t["d_wtd"], in_=wt_dn.ap()).then_inc(S["s_dbg"], 16)
        p.on("gpsimd", dbg_wt)
        p.bump("s_dbg", 32)

    DQ_NIB_ALL = p.val("s_dqv")

    if "D" in stages:
        q_marks = []
        den_marks = [0] * 4
        pass2_marks = [0] * 4
        pj_all = [0] * 4
        for tt in range(4):
            pe_guard = p.val("s_pe")
            XN_RDY = emit_xn_jit(512 * tt, pe_guard)

            for g in range(NG):
                wk = 32 * tt + g
                wbq = (wqb + wqb_x)[wk % 4]
                wqfree = q_marks[wk - 4] if wk >= 4 else 0

                sq = f"s_q{wk % 4}"

                def g_wq(e, g=g, tt=tt, wqfree=wqfree, sq=sq, wbq=wbq, wk=wk):
                    if wqfree:
                        e.wait_ge(S["s_pe"], wqfree)
                    elif wk < 4:
                        e.wait_ge(S["s_dqv"], DQ_NIB_ALL)
                        e.wait_ge(S["s_pe"], C_PE)
                    e.dma_start(out=wbq[:],
                                in_=wq_r.rearrange("(g p) o -> g p o", p=128)[g]
                                ).then_inc(S[sq], 16)
                p.on("gpsimd", g_wq)
                WQ = p.bump(sq, 16)

                evq = p.val("s_evk")

                def t_q(e, g=g, tt=tt, WQ=WQ, XN_RDY=XN_RDY, evq=evq, sq=sq, wbq=wbq):
                    e.wait_ge(S[sq], WQ)
                    e.wait_ge(S["s_xn"], XN_RDY)
                    e.wait_ge(S["s_xch"], KV_RDY)
                    if g + tt > 0:
                        e.wait_ge(S["s_evk"], max(0, evq - 1))
                    else:
                        e.wait_ge(S["s_ev"], RFF_RDY)
                    pq = pb[g % 2]
                    for ch in range(CH):
                        mm = e.matmul(pq[:], wbq[:, 128 * ch:128 * ch + 128], xn[ch][:],
                                      start=(ch == 0), stop=(ch == CH - 1))
                    mm.then_inc(S["s_pe"], 1)
                p.on("tensor", t_q)
                QPE = p.bump("s_pe", 1)
                q_marks.append(QPE)

                def a_q(e, g=g, tt=tt, QPE=QPE):
                    e.wait_ge(S["s_pe"], QPE)
                    if tt > 0:
                        e.wait_ge(S["s_pe2"], pass2_marks[tt - 1])
                    e.activation(out=qsb[g][:], in_=pb[g % 2][:], func=AF.Relu).then_inc(S["s_evk"], 1)
                p.on("scalar", a_q)
                QEV = p.bump("s_evk", 1)

                def t_den(e, g=g, tt=tt, QEV=QEV):
                    e.wait_ge(S["s_evk"], QEV)
                    if g == 0 and tt > 0:
                        e.wait_ge(S["s_den"], den_marks[tt - 1])
                    e.matmul(pb[2][0:64, :], denl[:, 64 * g:64 * g + 64], qsb[g][:],
                             start=(g == 0), stop=(g == NG - 1)).then_inc(S["s_pe2"], 1)
                p.on("tensor", t_den)
                p.bump("s_pe2", 1)

            DEN = p.val("s_pe2")

            def a_den(e, tt=tt, DEN=DEN):
                e.wait_ge(S["s_pe2"], DEN)
                if tt > 0:
                    e.wait_ge(S["s_den"], den_marks[tt - 1])
                e.activation(out=dsb[:], in_=pb[2][0:64, :], func=AF.Relu,
                             bias=cst[0:64, 6:7]).then_inc(S["s_den"], 1)
            p.on("scalar", a_den)
            DE1 = p.bump("s_den", 1)

            def v_rcp(e, tt=tt, DE1=DE1):
                e.wait_ge(S["s_den"], DE1)
                if tt > 0:
                    e.wait_ge(S["s_pe2"], pass2_marks[tt - 1])
                e.reciprocal(out=rcp[:], in_=dsb[:])
                e.tensor_copy(out=rcpb[:], in_=rcp[:]).then_inc(S["s_den"], 1)
            p.on("vector", v_rcp)
            DE2 = p.bump("s_den", 1)
            den_marks[tt] = DE2

            for g in range(NG):
                evn = p.val("s_evv")
                atv = p.val("s_at")

                def t_num(e, g=g, tt=tt, DE2=DE2, evn=evn, atv=atv):
                    e.wait_ge(S["s_den"], DE2)
                    if g + tt > 0:
                        e.wait_ge(S["s_evv"], evn)
                        e.wait_ge(S["s_at"], atv)
                    e.matmul(pb[3][:], KVb[:, 128 * g:128 * g + 128], qsb[g][:],
                             start=True, stop=True)
                    e.matmul(pb[4][:], bmap_s[:, 128 * g:128 * g + 128], rcpb[:],
                             start=True, stop=True).then_inc(S["s_pe2"], 1)
                p.on("tensor", t_num)
                NPE = p.bump("s_pe2", 1)

                def a_num(e, g=g, NPE=NPE, atv=atv):
                    e.wait_ge(S["s_pe2"], NPE)
                    if g >= 1:
                        e.wait_ge(S["s_at"], atv)
                    e.activation(out=nsb[:], in_=pb[3][:], func=AF.Copy).then_inc(S["s_evv"], 1)
                p.on("scalar", a_num)
                NEV = p.bump("s_evv", 1)

                def v_att(e, g=g, tt=tt, NEV=NEV):
                    e.wait_ge(S["s_evv"], NEV)
                    if tt > 0 and g == 0:
                        e.wait_ge(S["s_pe"], pj_all[tt - 1])
                    if tt == 0 and g == 0:
                        e.wait_ge(S["s_xch"], KV_RDY)
                    e.tensor_tensor(out=att[g][:], in0=nsb[:], in1=pb[4][:],
                                    op=ALU.mult).then_inc(S["s_at"], 1)
                p.on("vector", v_att)
                p.bump("s_at", 1)

            pass2_marks[tt] = p.val("s_pe2")
            AT_TT = p.val("s_at")

            for Hc in range(CH):
                wk = 32 * tt + 16 + Hc
                wbq = (wqb + wqb_x)[wk % 4]
                wqfree = q_marks[wk - 4]

                sq = f"s_q{wk % 4}"
                se = f"s_e{Hc % 2}"

                def g_wp(e, Hc=Hc, wqfree=wqfree, sq=sq, wbq=wbq):
                    e.wait_ge(S["s_pe"], wqfree)
                    e.dma_start(out=wbq[:],
                                in_=wp_r.rearrange("(g p) o -> g p o", p=128)[Hc]
                                ).then_inc(S[sq], 16)
                p.on("gpsimd", g_wp)
                WP = p.bump(sq, 16)

                eo = p.val("s_eo")

                def s_ex(e, Hc=Hc, tt=tt, eo=eo, se=se):
                    for k in range(2):
                        if FFW_ALL[k]:
                            e.wait_ge(S[f"s_ffw{k}"], FFW_ALL[k])
                    if 16 * tt + Hc >= 2:
                        e.wait_ge(S["s_eo"], eo - 1)
                    e.dma_start(out=exb[Hc % 2][:],
                                in_=x_kv.rearrange("(c p) t -> p c t", p=128)[:, Hc, 512 * tt:512 * tt + 512]
                                ).then_inc(S[se], 16)
                    e.dma_start(out=fchunk[Hc % 2][:],
                                in_=ffn_d.rearrange("(c p) t -> c p t", p=128)[Hc, :, 512 * tt:512 * tt + 512]
                                ).then_inc(S[se], 16)
                p.on("sync", s_ex)
                EXR = p.bump(se, 32)

                evp = p.val("s_evk")

                def t_pj(e, Hc=Hc, WP=WP, AT_TT=AT_TT, evp=evp, tt=tt, sq=sq, wbq=wbq):
                    e.wait_ge(S[sq], WP)
                    e.wait_ge(S["s_at"], AT_TT)
                    if 16 * tt + Hc >= 2:
                        e.wait_ge(S["s_evk"], max(0, evp - 1))
                    pp = pb[Hc % 2]
                    for ch in range(CH):
                        mm = e.matmul(pp[:], wbq[:, 128 * ch:128 * ch + 128],
                                      att[ch][:], start=(ch == 0), stop=(ch == CH - 1))
                    mm.then_inc(S["s_pe"], 1)
                p.on("tensor", t_pj)
                PJ = p.bump("s_pe", 1)
                q_marks.append(PJ)

                eod = p.val("s_eo")

                def a_pj(e, Hc=Hc, PJ=PJ, eod=eod):
                    e.wait_ge(S["s_pe"], PJ)
                    if eod:
                        e.wait_ge(S["s_eo"], eod)
                    e.activation(out=tsb[:], in_=pb[Hc % 2][:],
                                 func=AF.Copy).then_inc(S["s_evk"], 1)
                p.on("scalar", a_pj)
                PJE = p.bump("s_evk", 1)

                sout = f"s_out{Hc % 2}"
                oo = p.val(sout)

                def v_e(e, Hc=Hc, tt=tt, PJE=PJE, EXR=EXR, oo=oo, se=se, sout=sout):
                    e.wait_ge(S["s_evk"], PJE)
                    e.wait_ge(S[se], EXR)
                    if oo:
                        e.wait_ge(S[sout], oo)
                    e.scalar_tensor_tensor(out=t2sb[:], in0=tsb[:], scalar=bp_s[:, Hc:Hc + 1],
                                           in1=exb[Hc % 2][:], op0=ALU.add, op1=ALU.add)
                    e.tensor_tensor(out=t3sb[:], in0=fchunk[Hc % 2][:], in1=rffb4[tt][:], op=ALU.mult)
                    e.tensor_tensor(out=osb[Hc % 2][:], in0=t2sb[:], in1=t3sb[:],
                                    op=ALU.add).then_inc(S["s_eo"], 1)
                p.on("vector", v_e)
                p.bump("s_eo", 1)

                EOD2 = p.val("s_eo")

                def s_out_f(e, Hc=Hc, tt=tt, EOD2=EOD2, sout=sout):
                    e.wait_ge(S["s_eo"], EOD2)
                    e.dma_start(out=out.rearrange("(c p) t -> c p t", p=128)[Hc, :, 512 * tt:512 * tt + 512],
                                in_=osb[Hc % 2][:]).then_inc(S[sout], 16)
                p.on("sync", s_out_f)
                p.bump(sout, 16)

            pj_all[tt] = p.val("s_pe")

        OUT_ALL = (p.val("s_out0"), p.val("s_out1"))
        DBG_ALL = p.val("s_dbg")

        def s_fin(e, OUT_ALL=OUT_ALL, DBG_ALL=DBG_ALL):
            e.wait_ge(S["s_out0"], OUT_ALL[0])
            e.wait_ge(S["s_out1"], OUT_ALL[1])
            if DBG_ALL:
                e.wait_ge(S["s_dbg"], DBG_ALL)
        p.on("sync", s_fin)

    def s_retire(e):
        for n in sem_names:
            if n in ("s_poll", "s_poll2", "s_poll3"):
                continue
            v = p.val(n)
            if v:
                e.wait_ge(S[n], v)
    p.on("sync", s_retire)

    p.finalize(sems=list(S.values()))
    nc.all_engine_barrier()
    nc.clear_and_free_semaphores(list(S.values()))
    nc.all_engine_barrier()
    estack.pop_all()
    return nc


def _host_prep(L0_attn_x, L0_attn_w_qkv, L0_attn_w_proj, L0_attn_b_proj,
               L0_ffn_gate_up_packed, L0_ffn_gate_up_scales,
               L0_ffn_down_packed, L0_ffn_down_scales):
    pi = _perm(C)
    pii = _perm(II)

    x = np.asarray(L0_attn_x)
    wqkv = np.asarray(L0_attn_w_qkv)
    wproj = np.asarray(L0_attn_w_proj)
    bp = np.asarray(L0_attn_b_proj)
    gup = np.asarray(L0_ffn_gate_up_packed)
    gus = np.asarray(L0_ffn_gate_up_scales)
    dnp = np.asarray(L0_ffn_down_packed)
    dns = np.asarray(L0_ffn_down_scales)

    wqkvT = np.ascontiguousarray(wqkv.T[pi, :], dtype=np.float32)
    wprojT = np.ascontiguousarray(wproj.T[:, pi], dtype=np.float32)
    bp_dev = np.ascontiguousarray(bp[pi].reshape(16, 128).T, dtype=np.float32)

    def _blocks(src, width):
        n = src.shape[1] // width
        b = src.reshape(16, 128, n * width)
        out = np.empty((n, 128, 16 * width), np.float32)
        for i in range(n):
            out[i] = b[:, :, width * i:width * (i + 1)].transpose(1, 0, 2).reshape(128, 16 * width)
        return out.reshape(n * 128, 16 * width)

    wkv_r = np.ascontiguousarray(_blocks(wqkvT[:, 2048:6144], 256))
    wq_r = np.ascontiguousarray(_blocks(wqkvT[:, 0:2048], 128))
    wp_r = np.ascontiguousarray(_blocks(wprojT, 128))

    rowperm = np.concatenate([pii, II + pii])
    gu_bytes_f = gup[rowperm].view(np.uint8).reshape(2 * II, C // 2).T
    gu_scales_f = gus[:, rowperm]
    dn_bytes_f = dnp[pi].view(np.uint8).reshape(C, II // 2).T
    dn_scales_f = dns[:, pi]
    gu_bytes_h = [np.ascontiguousarray(gu_bytes_f[:, 5632 * s:5632 * (s + 1)]) for s in range(2)]
    gu_scales_h = [np.ascontiguousarray(gu_scales_f[:, 5632 * s:5632 * (s + 1)]) for s in range(2)]
    dn_bytes_h = [np.ascontiguousarray(dn_bytes_f[:, 1024 * s:1024 * (s + 1)]) for s in range(2)]
    dn_scales_h = [np.ascontiguousarray(dn_scales_f[:, 1024 * s:1024 * (s + 1)]) for s in range(2)]
    gu_off = [0, 44 * 16 * 16384]
    dn_off = [0, 8 * ICn * 16384]

    consts = np.zeros((128, 8), np.float32)
    consts[:, 0] = 1e-5
    consts[:, 1] = -0.5
    consts[:, 2] = -2.0
    consts[:, 3] = -6.0
    consts[:, 4] = 2.0
    consts[:, 5] = 1e-6
    consts[:, 6] = 1e-15

    maskbd = np.zeros((128, 128), ml_dtypes.bfloat16)
    for hh in range(4):
        maskbd[32 * hh:32 * hh + 32, 32 * hh:32 * hh + 32] = 1.0
    hm4_a = np.zeros((128, 4), ml_dtypes.bfloat16)
    for hh in range(4):
        hm4_a[32 * hh:32 * hh + 32, hh] = 1.0
    bmap_a = np.zeros((64, NG * 128), ml_dtypes.bfloat16)
    for g in range(NG):
        for hh in range(4):
            bmap_a[4 * g + hh, 128 * g + 32 * hh:128 * g + 32 * hh + 32] = 1.0
    ones_a = np.ones((128, 2), ml_dtypes.bfloat16)
    ones1_a = np.ones((1, 128), np.float32)
    def _sctiles(sc_h, nob, nt):
        tiles = np.empty((nob * nt, 128, 512), np.float16)
        for ob in range(nob):
            for t in range(nt):
                tiles[nt * ob + t] = np.repeat(sc_h[2 * t:2 * t + 2, 512 * ob:512 * ob + 512],
                                               64, axis=0)
        return np.ascontiguousarray(tiles.reshape(nob * nt * 128, 512))

    gu_sct_h = [_sctiles(gu_scales_h[s], 11, 8) for s in range(2)]
    dn_sct_h = [_sctiles(dn_scales_h[s], 2, 22) for s in range(2)]

    shared = dict(wkv_r=wkv_r, wq_r=wq_r, wp_r=wp_r, bproj=bp_dev,
                  consts=consts, maskbd=maskbd, hm4=hm4_a, bmap=bmap_a,
                  onesc=ones_a, ones1f=ones1_a)

    nonce = (int.from_bytes(os.urandom(4), "little") & 0x7FFFFFFF) | 1

    in_maps = []
    tokmaps = []
    for c in range(8):
        b = c // 4
        kh = c % 2
        pidx = (c // 2) % 2
        out_lo = 4096 * kh + 2048 * pidx
        kvr = np.arange(4096 * kh, 4096 * kh + 4096)
        outr = np.arange(out_lo, out_lo + 2048)
        rest = np.setdiff1d(kvr, outr)
        order = np.concatenate([outr, rest])
        xs = np.ascontiguousarray(x[b][order][:, pi].T.astype(np.float32))
        s = c % 2
        m = dict(shared)
        m["x_kv"] = xs
        m["gu_by"] = gu_bytes_h[s]
        m["gu_sct"] = gu_sct_h[s]
        m["dn_by"] = dn_bytes_h[s]
        m["dn_sct"] = dn_sct_h[s]
        m["slots"] = np.array([[s, s ^ 1, gu_off[s], dn_off[s], nonce]], np.int32)
        in_maps.append(m)
        tokmaps.append((b, out_lo))
    return in_maps, tokmaps, pi


def kernel(L0_attn_x, L0_attn_w_qkv, L0_attn_w_proj, L0_attn_b_proj,
           L0_ffn_gate_up_packed, L0_ffn_gate_up_scales,
           L0_ffn_down_packed, L0_ffn_down_scales):
    in_maps, tokmaps, pi = _host_prep(
        L0_attn_x, L0_attn_w_qkv, L0_attn_w_proj, L0_attn_b_proj,
        L0_ffn_gate_up_packed, L0_ffn_gate_up_scales,
        L0_ffn_down_packed, L0_ffn_down_scales)
    nc = build()
    res = run_bass_kernel_spmd(nc, in_maps, core_ids=list(range(8)))
    B, N = 2, 8192
    full = np.empty((B, N, C), np.float32)
    for c in range(8):
        b, lo = tokmaps[c]
        o = res.results[c]["out"]
        full[b, lo:lo + TOUT][:, pi] = o.T
    return full
